# version 38
# baseline (speedup 1.0000x reference)
import math
import numpy as np
import ml_dtypes
import concourse.bass as bass
import concourse.mybir as mybir
from concourse.bass_utils import run_bass_kernel_spmd

F32 = mybir.dt.float32
BF16 = mybir.dt.bfloat16
AF = mybir.ActivationFunctionType
ALU = mybir.AluOpType
AX = mybir.AxisListType

NC = 8
TOK = 1024
S = 8192
D = 4096
NH = 8
HD = 128
DKV = 1024
ER = 128
EPS = 1e-5
ISQ = 1.0 / math.sqrt(128.0)

AGQ_BLK = 4096 * 128
AGW_WO, AGW_BALL, AGW_LAT = 0, 524288, 589824
AGW_BLK = 655360
HBLK = 524288


ENGS = ("sync", "gpsimd", "tensor", "vector", "scalar")


def build_program():
    nc = bass.Bass("TRN2", target_bir_lowering=False)

    x_sl = nc.dram_tensor("x_sl", [TOK, D], F32, kind="ExternalInput")
    wq_sl = nc.dram_tensor("wq_sl", [D, 512], F32, kind="ExternalInput")
    wk_sl = nc.dram_tensor("wk_sl", [D, 128], F32, kind="ExternalInput")
    wv_sl = nc.dram_tensor("wv_sl", [D, 128], F32, kind="ExternalInput")
    wo_sl = nc.dram_tensor("wo_sl", [DKV, 512], F32, kind="ExternalInput")
    wbt_sl = nc.dram_tensor("wbt_sl", [D, 512], F32, kind="ExternalInput")
    ball_sl = nc.dram_tensor("ball_sl", [ER, 512], F32, kind="ExternalInput")
    lat_sl = nc.dram_tensor("lat_sl", [512, ER], F32, kind="ExternalInput")
    wr_in = nc.dram_tensor("wr_in", [8, D], F32, kind="ExternalInput")
    b2_in = nc.dram_tensor("b2_in", [128, 32], F32, kind="ExternalInput")
    gmat_in = nc.dram_tensor("gmat_in", [8, 128], F32, kind="ExternalInput")
    id_bf = nc.dram_tensor("id_bf", [128, 128], BF16, kind="ExternalInput")
    id_f32 = nc.dram_tensor("id_f32", [128, 128], F32, kind="ExternalInput")
    out_d = nc.dram_tensor("out", [TOK, D], F32, kind="ExternalOutput")

    awk_in = nc.dram_tensor("awk_in", [HBLK], BF16)
    awk_out = nc.dram_tensor("awk_out", [NC * HBLK], BF16)
    awv_in = nc.dram_tensor("awv_in", [HBLK], BF16)
    awv_out = nc.dram_tensor("awv_out", [NC * HBLK], BF16)
    agq_in = nc.dram_tensor("agq_in", [AGQ_BLK], BF16)
    agq_out = nc.dram_tensor("agq_out", [NC * AGQ_BLK], BF16)
    agw_in = nc.dram_tensor("agw_in", [AGW_BLK], BF16)
    agw_out = nc.dram_tensor("agw_out", [NC * AGW_BLK], BF16)
    wbt_in = [nc.dram_tensor("wbt%d_in" % j, [HBLK], BF16) for j in range(4)]
    wbt_out = [nc.dram_tensor("wbt%d_out" % j, [NC * HBLK], BF16) for j in range(4)]
    ka_in = nc.dram_tensor("ka_in", [HBLK], BF16)
    ka_out = nc.dram_tensor("ka_out", [NC * HBLK], BF16)
    kb_in = nc.dram_tensor("kb_in", [HBLK], BF16)
    kb_out = nc.dram_tensor("kb_out", [NC * HBLK], BF16)
    va_in = nc.dram_tensor("va_in", [HBLK], BF16)
    va_out = nc.dram_tensor("va_out", [NC * HBLK], BF16)
    vb_in = nc.dram_tensor("vb_in", [HBLK], BF16)
    vb_out = nc.dram_tensor("vb_out", [NC * HBLK], BF16)

    from contextlib import ExitStack
    gsc = ExitStack()
    wqesc = ExitStack()

    def SB(scope, name, shape, dt, side="left"):
        return scope.enter_context(nc.sbuf_tensor(name, shape, dt, side=side))

    def PS(scope, name, shape, dt):
        return scope.enter_context(nc.psum_tensor(name, shape, dt))

    def SEM(name):
        return gsc.enter_context(nc.semaphore(name))

    sems = {}
    cnt = {}
    for nm in ("dx0", "dx1", "dx2", "dx3", "dw0", "dw1", "dw2", "dw3", "dkv0", "dkv1", "dqe", "dmisc",
               "dsm", "dag", "dkvin", "dki0", "dki1", "dki2", "dwo0", "dwo1", "dwo2", "dwo3", "dsg0", "dsg1", "st0", "st1", "st2", "st3", "ts", "vs",
               "ss", "cc", "gp", "dw4", "dw5", "dw6", "dw7",
               "dh0", "dh1", "dh2", "dh3", "dh4", "dh5", "dh6", "dh7", "dlat",
               "dkh0", "dkh1", "dkh2", "dkh3", "dkh4", "dkh5"):
        sems[nm] = SEM(nm)
        cnt[nm] = 0

    prog = {e: [] for e in ENGS}

    def E(eng, fn):
        prog[eng].append(fn)

    def W(eng, sem_nm, val):
        if val <= 0:
            return
        h = sems[sem_nm]
        E(eng, lambda e, h=h, v=val: e.wait_ge(h, v))

    def bump(sem_nm, n=1):
        cnt[sem_nm] += n
        return cnt[sem_nm]

    def dma(eng, sem_nm, out_ap, in_ap, accum=False):
        h = sems[sem_nm]
        if accum:
            E(eng, lambda e, o=out_ap, i=in_ap, h=h: e.dma_start(
                out=o, in_=i, accum_op=ALU.add).then_inc(h, 16))
        else:
            E(eng, lambda e, o=out_ap, i=in_ap, h=h: e.dma_start(
                out=o, in_=i).then_inc(h, 16))
        return bump(sem_nm, 16)

    def t_mm(out_ap, l_ap, r_ap, start, stop, inc=False):
        if inc:
            h = sems["ts"]
            E("tensor", lambda e, o=out_ap, l=l_ap, r=r_ap, a=start, b=stop, h=h:
              e.matmul(o, l, r, start=a, stop=b).then_inc(h, 1))
            return bump("ts")
        E("tensor", lambda e, o=out_ap, l=l_ap, r=r_ap, a=start, b=stop:
          e.matmul(o, l, r, start=a, stop=b))
        return cnt["ts"]

    def t_tr(out_ap, in_ap, id_ap, inc=False):
        if inc:
            h = sems["ts"]
            E("tensor", lambda e, o=out_ap, i=in_ap, d=id_ap, h=h:
              e.transpose(o, i, d).then_inc(h, 1))
            return bump("ts")
        E("tensor", lambda e, o=out_ap, i=in_ap, d=id_ap: e.transpose(o, i, d))
        return cnt["ts"]

    def v_op(meth, *aps, inc=False, **kw):
        if inc:
            h = sems["vs"]
            E("vector", lambda e, m=meth, a=aps, k=kw, h=h:
              getattr(e, m)(*a, **k).then_inc(h, 1))
            return bump("vs")
        E("vector", lambda e, m=meth, a=aps, k=kw: getattr(e, m)(*a, **k))
        return cnt["vs"]

    def vq(meth, *aps, **kw):
        W("vector", "vs", cnt["vs"])
        return v_op(meth, *aps, inc=True, **kw)

    def s_act(out_ap, in_ap, func, bias=0.0, scale=1.0, accum=None, inc=False):
        kw = dict(bias=bias, scale=scale)
        if accum is not None:
            kw["accum_out"] = accum
        if inc:
            h = sems["ss"]
            E("scalar", lambda e, o=out_ap, i=in_ap, f=func, k=kw, h=h:
              e.activation(o, i, f, **k).then_inc(h, 1))
            return bump("ss")
        E("scalar", lambda e, o=out_ap, i=in_ap, f=func, k=kw: e.activation(o, i, f, **k))
        return cnt["ss"]

    qsc = ExitStack()
    q_sb = SB(qsc, "q_sb", [128, NH * TOK], BF16, side="right")
    ident = SB(gsc, "ident", [128, 128], BF16)
    identf = SB(gsc, "identf", [128, 128], F32)
    ones_c = SB(gsc, "ones_c", [128, 1], BF16)
    onesr = SB(gsc, "onesr", [1, 128], BF16)
    gmat_sb = SB(gsc, "gmat_sb", [8, 128], BF16)
    b2_sb = SB(gsc, "b2_sb", [128, 32], F32)
    wrt_sb = SB(gsc, "wrt_sb", [128, 256], BF16)
    ltm_sb = SB(gsc, "ltm_sb", [128, 64], F32)
    wtm_sb = SB(gsc, "wtm_sb", [128, 64], F32)
    wfm_bf = SB(gsc, "wfm_bf", [8, TOK], BF16)
    rt0 = SB(gsc, "rt0", [128, 8], F32)
    rt1 = SB(gsc, "rt1", [128, 8], F32)
    rt2 = SB(gsc, "rt2", [128, 8], F32)
    rs0 = SB(gsc, "rs0", [128, 1], F32)
    rs1 = SB(gsc, "rs1", [128, 1], F32)
    stats = SB(gsc, "stats", [128, 16], F32)
    mq_sb = SB(gsc, "mq_sb", [1, 16], F32)
    bh_sb = SB(gsc, "bh_sb", [128, 8], F32)
    bhr_sb = SB(gsc, "bhr_sb", [1, 8], BF16)
    dtm_sb = SB(gsc, "dtm_sb", [128, 64], F32)
    dtm2_sb = SB(gsc, "dtm2_sb", [128, 64], F32)
    wgt = [SB(gsc, "wgt0", [128, D], BF16), SB(gsc, "wgt1", [128, D], BF16)]
    h_sb = SB(gsc, "h_sb", [128, TOK], F32)
    hw_sb = SB(gsc, "hw_sb", [128, TOK], BF16)
    kearly = ExitStack()
    ktmp = [SB(kearly, "ktmp%d" % i, [128, 1024], BF16) for i in range(3)]
    latsc = ExitStack()
    latw = SB(latsc, "latw", [128, D], BF16, side="right")
    p0sc = ExitStack()
    xtsc = ExitStack()
    xt = [SB(xtsc, "xt%d" % i, [128, D], F32) for i in range(3)]
    wr_sb = xt[2][0:8, 0:2048].bitcast(BF16)
    lg_sb = h_sb[0:8, :]
    wqe_acc = SB(wqesc, "wqe_acc", [128, D], F32)
    wqe_bf = SB(wqesc, "wqe_bf", [128, D], BF16)

    dma("gpsimd", "dmisc", gmat_sb[:, :], gmat_in.ap())
    rg = [list(range(NC))]

    def trigger_ag(in_t, out_t):
        E("gpsimd", lambda e, a=in_t, b=out_t: e.collective_compute(
            "AllGather", ALU.bypass, replica_groups=rg,
            ins=[a.ap().opt()], outs=[b.ap().opt()]).then_inc(sems["cc"], 1))
        return bump("cc")

    dma("gpsimd", "dag", bass.AP(awk_in, 0, [[128, 4096], [1, 128]]), wk_sl.ap())
    W("gpsimd", "dag", cnt["dag"])
    trigger_ag(awk_in, awk_out)
    dma("gpsimd", "dag", bass.AP(awv_in, 0, [[128, 4096], [1, 128]]), wv_sl.ap())
    W("gpsimd", "dag", cnt["dag"])
    trigger_ag(awv_in, awv_out)
    for j in range(4):
        v = dma("gpsimd", "dqe",
                wqe_acc[:, :].rearrange("p (a b) -> p a b", b=128),
                bass.AP(wq_sl, j * 128, [[512, 128], [65536, 32], [1, 128]]),
                accum=(j > 0))
        W("gpsimd", "dqe", v)
    E("gpsimd", lambda e: e.tensor_copy(wqe_bf[:, :], wqe_acc[:, :]).then_inc(
        sems["gp"], 1))
    bump("gp")
    W("gpsimd", "gp", cnt["gp"])
    dma("gpsimd", "dag", bass.AP(agq_in, 0, [[128, 128], [16384, 32], [1, 128]]),
        wqe_bf[:, :].rearrange("p (a b) -> p a b", b=128))
    W("gpsimd", "dag", cnt["dag"])
    trigger_ag(agq_in, agq_out)
    dma("gpsimd", "dag", bass.AP(agw_in, AGW_WO, [[512, 1024], [1, 512]]), wo_sl.ap())
    dma("gpsimd", "dag", bass.AP(agw_in, AGW_BALL, [[512, 128], [1, 512]]), ball_sl.ap())
    dma("gpsimd", "dag", bass.AP(agw_in, AGW_LAT, [[128, 512], [1, 128]]), lat_sl.ap())
    W("gpsimd", "dag", cnt["dag"])
    trigger_ag(agw_in, agw_out)
    for j in range(4):
        dma("gpsimd", "dag", bass.AP(wbt_in[j], 0, [[512, 1024], [1, 512]]),
            wbt_sl.ap()[j * 1024:(j + 1) * 1024, :])
    W("gpsimd", "dag", cnt["dag"])
    for j in (0, 1):
        trigger_ag(wbt_in[j], wbt_out[j])
    dma("gpsimd", "dmisc", wr_sb[:, :], wr_in.ap())
    W("gpsimd", "cc", 1)
    pdvK = [0, 0]
    for pi in range(2):
        pdvK[pi] = dma("gpsimd", "dwo%d" % pi,
                       wgt[pi][:, :].rearrange("p (a b) -> p a b", b=128),
                       bass.AP(awk_out, pi * HBLK,
                               [[128, 128], [16384, 32], [1, 128]]))
    W("gpsimd", "cc", 4)
    for r in range(NC):
        dlat = dma("gpsimd", "dlat",
                   latw[:, r * 512:(r + 1) * 512].rearrange("p (a b) -> p a b", b=128),
                   bass.AP(agw_out, r * AGW_BLK + AGW_LAT,
                           [[128, 128], [16384, 4], [1, 128]]))

    xnsc = ExitStack()
    xn_sb = SB(xnsc, "xn_sb", [128, 32 * TOK], BF16, side="right")
    hi_sb = SB(p0sc, "hi_sb", [128, D], BF16)
    qkvps = ExitStack()
    pqkv = [PS(qkvps, "pqkv%d" % i, [128, 512], F32) for i in range(4)]
    p0ps = ExitStack()
    ptr = [PS(p0ps, "ptr0", [128, 1024], BF16), PS(p0ps, "ptr1", [128, 1024], BF16)]
    plog = PS(p0ps, "plog", [128, 1024], F32)

    dma("sync", "dsm", ident[:, :], id_bf.ap())
    dma("sync", "dsm", identf[:, :], id_f32.ap())
    dma("sync", "dsm", b2_sb[:, :], b2_in.ap())
    v_op("memset", ones_c[:, :], 1.0)
    v_op("memset", onesr[:, :], 1.0, inc=True)
    vs_memset = cnt["vs"]

    W("tensor", "dmisc", 32)
    W("tensor", "dsm", 48)
    for fc in range(32):
        t_tr(ptr[0][:, fc * 8:(fc + 1) * 8], wr_sb[:, fc * 128:(fc + 1) * 128],
             ident[0:8, 0:8], inc=(fc == 31))
    ts_wrt = cnt["ts"]
    W("vector", "ts", ts_wrt)
    vs_wrt = v_op("tensor_copy", wrt_sb[:, :], ptr[0][:, 0:256], inc=True)

    ev3 = {}
    strip_ts = {}
    k_store_done = {-3: 0, -2: 0, -1: 0}
    k_sq_done = {-3: 0, -2: 0, -1: 0}
    vs_k_at = {}
    bank_free_v = [0] * 4

    def emit_kevac(dt):
        for st_, bk in enumerate([(2 * dt) % 4, (2 * dt + 1) % 4]):
            W("vector", "ts", strip_ts[(dt, st_ * 2 + 1)])
            bank_free_v[bk] = v_op("tensor_copy",
                                   ktmp[dt % 3][:, st_ * 512:(st_ + 1) * 512],
                                   pqkv[bk][:, :], inc=True)
        vs_k_at[dt] = cnt["vs"]
        W("sync", "vs", vs_k_at[dt])
        k_store_done[dt] = dma("sync", "dki%d" % (dt % 3),
                               bass.AP(ka_in, (dt % 4) * 131072,
                                       [[1024, 128], [1, 1024]]),
                               ktmp[dt % 3][:, :])

    def emit_kstrips(pr):
        W("tensor", "vs", ev3[2 * pr + 1])
        for ki in range(2):
            if pr == 0:
                W("tensor", "dwo%d" % ki, pdvK[ki])
            bank = ki * 2 + pr // 2
            col = (pr % 2) * 256
            for fc in range(32):
                t_mm(pqkv[bank][:, col:col + 256],
                     wgt[ki][:, fc * 128:(fc + 1) * 128],
                     xn_sb[:, fc * TOK + pr * 256: fc * TOK + pr * 256 + 256],
                     start=(fc == 0), stop=(fc == 31), inc=(fc == 31))
            strip_ts[(ki, pr)] = cnt["ts"]

    wgt4 = list(wgt)
    xt_free_s = [0, 0, 0]
    xt_free_v = [0, 0, 0]
    hi_free_t = 0
    tr_bank_free_v = [vs_wrt, vs_wrt]
    xn_done_v = 0
    for tt in range(8):
        sl = tt % 3
        W("sync", "ss", xt_free_s[sl])
        W("sync", "vs", xt_free_v[sl])
        if tt == 2:
            W("sync", "ts", ts_wrt)
        dv = dma("sync", "dx%d" % sl, xt[sl][:, 0:2048],
                 x_sl.ap()[tt * 128:(tt + 1) * 128, 0:2048])
        W("scalar", "ss", xt_free_s[sl])
        W("scalar", "vs", xt_free_v[sl])
        if tt == 2:
            W("scalar", "ts", ts_wrt)
        dvs = dma("scalar", "dh%d" % sl, xt[sl][:, 2048:4096],
                  x_sl.ap()[tt * 128:(tt + 1) * 128, 2048:4096])
        W("vector", "dx%d" % sl, dv)
        W("vector", "dh%d" % sl, dvs)
        if tt == 0:
            W("vector", "gp", 1)
        v_op("tensor_scalar", wqe_acc[:, :], xt[sl][:, :], 1.0, None, ALU.mult,
             op1=ALU.add, accum_out=stats[:, 0:1], inc=True)
        W("scalar", "dx%d" % sl, dv)
        W("scalar", "dh%d" % sl, dvs)
        W("scalar", "ts", hi_free_t)
        s_act(hi_sb[:, :], xt[sl][:, :], AF.Square, accum=stats[:, 1:2], inc=True)
        ss_sq = cnt["ss"]
        W("vector", "ss", ss_sq)
        vq("tensor_scalar", stats[:, 2:3], stats[:, 0:1], 1.0 / D, None, ALU.mult)
        vq("tensor_scalar", stats[:, 3:4], stats[:, 1:2], 1.0 / D, None, ALU.mult)
        vq("tensor_tensor", stats[:, 6:7], stats[:, 2:3], stats[:, 2:3], ALU.mult)
        vq("tensor_sub", stats[:, 3:4], stats[:, 3:4], stats[:, 6:7])
        vq("tensor_scalar", stats[:, 3:4], stats[:, 3:4], EPS, None, ALU.add)
        vq("reciprocal", stats[:, 6:7], stats[:, 3:4])
        vs_rv = cnt["vs"]
        W("scalar", "vs", vs_rv)
        s_act(stats[:, 4:5], stats[:, 6:7], AF.Sqrt, inc=True)
        ss_a = cnt["ss"]
        W("vector", "ss", ss_a)
        vq("tensor_tensor", stats[:, 5:6], stats[:, 2:3], stats[:, 4:5], ALU.mult)
        vq("tensor_scalar", stats[:, 5:6], stats[:, 5:6], -1.0, None, ALU.mult)
        vs_nb = cnt["vs"]
        W("scalar", "vs", vs_nb)
        W("scalar", "ts", hi_free_t)
        s_act(hi_sb[:, :], xt[sl][:, :], AF.Identity, bias=stats[:, 5:6],
              scale=stats[:, 4:5], inc=True)
        ss_xn = cnt["ss"]
        xt_free_s[sl] = ss_xn
        xt_free_v[sl] = cnt["vs"]
        W("tensor", "ss", ss_xn)
        for g in range(4):
            par = g % 2
            W("tensor", "vs", tr_bank_free_v[par])
            for k in range(8):
                fc = g * 8 + k
                t_tr(ptr[par][:, k * 128:(k + 1) * 128],
                     hi_sb[:, fc * 128:(fc + 1) * 128], ident[:, :],
                     inc=(k == 7))
            ts_g = cnt["ts"]
            if g == 3:
                hi_free_t = ts_g
            W("vector", "ts", ts_g)
            dst = xn_sb[:, :].rearrange("p (a b) -> p a b", b=TOK)[
                :, g * 8:(g + 1) * 8, tt * 128:(tt + 1) * 128]
            src = ptr[par][:, :].rearrange("p (a b) -> p a b", b=128)
            tr_bank_free_v[par] = v_op("tensor_copy", dst, src, inc=True)
        ev3[tt] = cnt["vs"]
        if tt >= 3 and tt % 2 == 1:
            emit_kstrips((tt - 3) // 2)
    emit_kstrips(2)
    emit_kstrips(3)
    emit_kevac(0)
    emit_kevac(1)
    xn_done_v = cnt["vs"]

    W("tensor", "vs", xn_done_v)
    W("tensor", "vs", vs_wrt)
    for fc in range(32):
        for st_ in range(2):
            t_mm(plog[0:8, st_ * 512:(st_ + 1) * 512],
                 wrt_sb[:, fc * 8:(fc + 1) * 8],
                 xn_sb[:, fc * TOK + st_ * 512: fc * TOK + st_ * 512 + 512],
                 start=(fc == 0), stop=(fc == 31), inc=(fc == 31 and st_ == 1))
    ts_log = cnt["ts"]
    W("vector", "ts", ts_log)
    vs_lg = v_op("tensor_copy", lg_sb[:, :], plog[0:8, 0:1024], inc=True)
    W("tensor", "vs", vs_lg)
    for tt in range(8):
        t_tr(plog[:, 512 + tt * 8:512 + (tt + 1) * 8],
             lg_sb[:, tt * 128:(tt + 1) * 128],
             identf[0:8, 0:8], inc=(tt == 7))
    ts_ltm = cnt["ts"]
    W("vector", "ts", ts_ltm)
    v_op("tensor_copy", ltm_sb[:, :], plog[:, 512:576], inc=True)
    p0_psum_free_v = cnt["vs"]
    for tt in range(8):
        lt = ltm_sb[:, tt * 8:(tt + 1) * 8]
        vq("reduce_max", rs0[:, :], lt, axis=AX.X)
        vq("tensor_scalar", rt1[:, :], lt, rs0[:, :], None, ALU.is_ge)
        vq("tensor_scalar", rs0[:, :], rs0[:, :], -1.0, None, ALU.mult)
        vs_m1 = cnt["vs"]
        W("scalar", "vs", vs_m1)
        s_act(rt0[:, :], lt, AF.Exp, bias=rs0[:, :], inc=True)
        ss_pr = cnt["ss"]
        W("vector", "ss", ss_pr)
        vq("tensor_tensor", rt2[:, :], rt0[:, :], rt1[:, :], ALU.mult)
        vq("tensor_sub", rt2[:, :], rt0[:, :], rt2[:, :])
        vq("reduce_max", rs1[:, :], rt2[:, :], axis=AX.X)
        vq("tensor_scalar", rt1[:, :], rt0[:, :], rs1[:, :], None, ALU.is_ge)
        vq("reduce_max", rs0[:, :], rt0[:, :], axis=AX.X)
        vq("tensor_add", rs1[:, :], rs1[:, :], rs0[:, :])
        vq("reciprocal", rs1[:, :], rs1[:, :])
        vq("tensor_tensor", rt2[:, :], rt0[:, :], rt1[:, :], ALU.mult)
        vq("tensor_scalar", wtm_sb[:, tt * 8:(tt + 1) * 8], rt2[:, :],
           rs1[:, :], None, ALU.mult)
    vs_w = cnt["vs"]

    p0ps.close()
    p0sc.close()
    wqesc.close()
    xtsc.close()
    p1wsc = ExitStack()
    wgt2 = [SB(p1wsc, "wgt2_%d" % i, [128, D], BF16, side="right") for i in range(6)]
    wgt4.extend(wgt2)
    p1sc = ExitStack()
    p1ps = ExitStack()
    pnorm = PS(p1ps, "pnorm", [128, 1024], F32)
    pbb = PS(p1ps, "pbb", [128, 512], F32)
    pvt = PS(p1ps, "pvt", [128, 1024], BF16)
    q2_sb = SB(p1sc, "q2_sb", [128, TOK], BF16)
    wgt4 = wgt + wgt2
    vfm = [SB(p1sc, "vfm0", [128, 1024], BF16), SB(p1sc, "vfm1", [128, 1024], BF16)]
    vtm = SB(p1sc, "vtm", [128, 1024], BF16)

    wblk_free_t = [0] * 8
    p0_v_done = p0_psum_free_v

    def qkv_block(kind, dt):
        nonlocal widx
        sl = widx % 8
        if kind == "q":
            ten, off, cc_need = agq_out, dt * AGQ_BLK, 3
        elif kind == "k":
            ten, off, cc_need = awk_out, dt * HBLK, 1
        else:
            ten, off, cc_need = awv_out, dt * HBLK, 2
        qe, s0, s1 = "scalar", "dw%d" % sl, "dh%d" % sl
        W(qe, "cc", cc_need)
        W(qe, "ts", wblk_free_t[sl])
        dvw_h = []
        for j, snm in enumerate((s0, s1)):
            dvw_h.append(dma(qe, snm,
                             wgt4[sl][:, j * 2048:(j + 1) * 2048].rearrange(
                                 "p (a b) -> p a b", b=128),
                             bass.AP(ten, off + j * 262144,
                                     [[128, 128], [16384, 16], [1, 128]])))
        widx += 1
        bks = [(2 * dt) % 4, (2 * dt + 1) % 4]
        W("tensor", "vs", max(bank_free_v[bks[0]], bank_free_v[bks[1]]))
        for fc in range(32):
            if fc % 16 == 0:
                W("tensor", (s0 if fc == 0 else s1), dvw_h[fc // 16])
            for st_ in range(2):
                t_mm(pqkv[bks[st_]][:, :], wgt4[sl][:, fc * 128:(fc + 1) * 128],
                     xn_sb[:, fc * TOK + st_ * 512: fc * TOK + st_ * 512 + 512],
                     start=(fc == 0), stop=(fc == 31),
                     inc=(fc == 31))
        wblk_free_t[sl] = cnt["ts"]
        return bks

    nrm_mm_done = 0
    nrm_red_done = p0_v_done

    def norms(src_ap, col):
        nonlocal nrm_mm_done, nrm_red_done
        W("scalar", "ts", nrm_mm_done)
        s_act(q2_sb[:, :], src_ap, AF.Square, inc=True)
        W("tensor", "ss", cnt["ss"])
        W("tensor", "vs", vs_memset)
        W("tensor", "vs", nrm_red_done)
        for st_ in range(2):
            t_mm(pnorm[0:1, st_ * 512:(st_ + 1) * 512], ones_c[:, :],
                 q2_sb[:, st_ * 512:(st_ + 1) * 512], start=True, stop=True,
                 inc=(st_ == 1))
        nrm_mm_done = cnt["ts"]
        W("vector", "ts", nrm_mm_done)
        nrm_red_done = v_op("reduce_max", mq_sb[0:1, col:col + 1], pnorm[0:1, 0:1024],
                            axis=AX.X, inc=True)

    widx = 2
    wblk_free_t[0] = strip_ts[(0, 3)]
    wblk_free_t[1] = strip_ts[(1, 3)]
    for dt in range(2, NH):
        banks = qkv_block("k", dt)
        for st_, bk in enumerate(banks):
            W("vector", "ts", cnt["ts"] - (1 - st_))
            if st_ == 0:
                W("vector", "dki%d" % (dt % 3), k_store_done[dt - 3])
                W("vector", "ss", k_sq_done[dt - 3])
            bank_free_v[bk] = v_op("tensor_copy",
                                   ktmp[dt % 3][:, st_ * 512:(st_ + 1) * 512],
                                   pqkv[bk][:, :], inc=True)
        vs_k_at[dt] = cnt["vs"]
        W("sync", "vs", vs_k_at[dt])
        k_store_done[dt] = dma("sync", "dki%d" % (dt % 3),
                               bass.AP(ka_in if dt < 4 else kb_in,
                                       (dt % 4) * 131072, [[1024, 128], [1, 1024]]),
                               ktmp[dt % 3][:, :])
        if dt == 3:
            for dkn in ("dki0", "dki1", "dki2"):
                W("gpsimd", dkn, cnt[dkn])
            cc_ka = trigger_ag(ka_in, ka_out)
            trigger_ag(wbt_in[2], wbt_out[2])
        if dt >= 2:
            W("scalar", "vs", vs_k_at[dt - 2])
            norms(ktmp[(dt - 2) % 3][:, :], 8 + dt - 2)
            k_sq_done[dt - 2] = cnt["ss"]
    for dtl in (NH - 2, NH - 1):
        W("scalar", "vs", vs_k_at[dtl])
        norms(ktmp[dtl % 3][:, :], 8 + dtl)
        k_sq_done[dtl] = cnt["ss"]
    for dkn in ("dki0", "dki1", "dki2"):
        W("gpsimd", dkn, cnt[dkn])
    trigger_ag(kb_in, kb_out)
    trigger_ag(wbt_in[3], wbt_out[3])
    v_store_done = cnt["dkvin"]
    vtail_ts = {}
    vfm_vs = {}
    vtm_copy_vs = 0

    def v_tail(dt):
        nonlocal v_store_done, vtm_copy_vs
        W("tensor", "vs", vfm_vs[dt])
        W("tensor", "vs", vtm_copy_vs)
        for tt in range(8):
            t_tr(pvt[:, tt * 128:(tt + 1) * 128],
                 vfm[dt % 2][:, tt * 128:(tt + 1) * 128],
                 ident[:, :], inc=(tt == 7))
        vtail_ts[dt] = cnt["ts"]
        W("vector", "ts", vtail_ts[dt])
        W("vector", "dkvin", v_store_done)
        vs_vt = v_op("tensor_copy", vtm[:, :], pvt[:, 0:1024], inc=True)
        vtm_copy_vs = vs_vt
        W("sync", "vs", vs_vt)
        dma("sync", "dkvin",
            bass.AP(va_in, dt * 128, [[1024, 128], [131072, 4], [1, 128]]),
            vtm[:, 0:512].rearrange("p (a b) -> p a b", b=128))
        v_store_done = dma("sync", "dkvin",
                           bass.AP(vb_in, dt * 128,
                                   [[1024, 128], [131072, 4], [1, 128]]),
                           vtm[:, 512:1024].rearrange("p (a b) -> p a b", b=128))

    for dt in range(NH):
        banks = qkv_block("v", dt)
        for st_, bk in enumerate(banks):
            W("vector", "ts", cnt["ts"] - (1 - st_))
            if st_ == 0 and dt >= 2:
                W("vector", "ts", vtail_ts[dt - 2])
            bank_free_v[bk] = v_op("tensor_copy",
                                   vfm[dt % 2][:, st_ * 512:(st_ + 1) * 512],
                                   pqkv[bk][:, :], inc=True)
        vfm_vs[dt] = cnt["vs"]
        if dt >= 1:
            v_tail(dt - 1)
    v_tail(NH - 1)
    q_ev_at = {}
    for dt in range(NH):
        banks = qkv_block("q", dt)
        for st_, bk in enumerate(banks):
            W("vector", "ts", cnt["ts"] - (1 - st_))
            bank_free_v[bk] = v_op(
                "tensor_copy", q_sb[:, dt * TOK + st_ * 512: dt * TOK + st_ * 512 + 512],
                pqkv[bk][:, :], inc=True)
        q_ev_at[dt] = cnt["vs"]
        if dt >= 2:
            W("scalar", "vs", q_ev_at[dt - 2])
            norms(q_sb[:, (dt - 2) * TOK: (dt - 1) * TOK], dt - 2)
    for dtl in (NH - 2, NH - 1):
        W("scalar", "vs", q_ev_at[dtl])
        norms(q_sb[:, dtl * TOK: (dtl + 1) * TOK], dtl)
    vs_q_done = cnt["vs"]
    vq("tensor_tensor", mq_sb[0:1, 0:8], mq_sb[0:1, 0:8], mq_sb[0:1, 8:16], ALU.mult)
    vs_bh1 = cnt["vs"]
    W("scalar", "vs", vs_bh1)
    s_act(mq_sb[0:1, 0:8], mq_sb[0:1, 0:8], AF.Sqrt, inc=True)
    ss_bh = cnt["ss"]
    W("vector", "ss", ss_bh)
    v_op("tensor_scalar", bhr_sb[0:1, 0:8], mq_sb[0:1, 0:8], ISQ, None, ALU.mult,
         inc=True)
    vs_bh2 = cnt["vs"]
    W("tensor", "vs", vs_bh2)
    t_mm(pbb[0:128, 0:8], onesr[:, :], bhr_sb[0:1, 0:8], start=True, stop=True,
         inc=True)
    ts_bb = cnt["ts"]
    W("vector", "ts", ts_bb)
    vs_bh = v_op("tensor_scalar", bh_sb[:, :], pbb[0:128, 0:8], -1.0, None, ALU.mult,
                 inc=True)
    W("gpsimd", "dkvin", cnt["dkvin"])
    trigger_ag(va_in, va_out)
    trigger_ag(vb_in, vb_out)

    p1ps.close()
    qkvps.close()
    p2ps = ExitStack()
    pbase = [PS(p2ps, "pbase%d" % i, [128, 512], F32) for i in range(4)]
    ph = PS(p2ps, "ph", [128, 1024], F32)
    pwx = PS(p2ps, "pwx", [128, 1024], F32)
    p1sc.close()
    p1wsc.close()
    kearly.close()
    kvearly = ExitStack()
    kbuf = [SB(kvearly, "kbuf0", [128, D], BF16)]
    vbuf = [SB(kvearly, "vbuf0", [128, D], BF16)]
    gssc = ExitStack()
    g_sb = SB(gssc, "g_sb", [128, 32 * TOK], BF16)
    p2sc = ExitStack()
    ball_sb = SB(p2sc, "ball_sb", [128, D], BF16, side="right")
    W("gpsimd", "ts", wblk_free_t[2])
    dball = dma("gpsimd", "dmisc", ball_sb[:, :].rearrange("p (a b) -> p a b", b=512),
                bass.AP(agw_out, AGW_BALL, [[512, 128], [AGW_BLK, 8], [1, 512]]))
    W("tensor", "vs", cnt["vs"])
    W("tensor", "vs", vs_w)
    for tt in range(8):
        t_tr(pwx[0:8, tt * 128:(tt + 1) * 128], wtm_sb[:, tt * 8:(tt + 1) * 8],
             identf[:, :], inc=(tt == 7))
    ts_wf = cnt["ts"]
    W("vector", "ts", ts_wf)
    vs_wfm = v_op("tensor_copy", wfm_bf[:, :], pwx[0:8, 0:1024], inc=True)
    W("tensor", "dlat", dlat)
    W("tensor", "dmisc", dball)
    for fc in range(32):
        for st_ in range(2):
            t_mm(ph[:, st_ * 512:(st_ + 1) * 512], latw[:, fc * 128:(fc + 1) * 128],
                 xn_sb[:, fc * TOK + st_ * 512: fc * TOK + st_ * 512 + 512],
                 start=(fc == 0), stop=(fc == 31), inc=(fc == 31 and st_ == 1))
    ts_h = cnt["ts"]
    W("tensor", "vs", vs_wfm)
    W("tensor", "dmisc", 32)
    for st_ in range(2):
        t_mm(pwx[:, st_ * 512:(st_ + 1) * 512], gmat_sb[:, :],
             wfm_bf[0:8, st_ * 512:(st_ + 1) * 512], start=True, stop=True,
             inc=(st_ == 1))
    ts_wx = cnt["ts"]
    W("vector", "ts", max(ts_h, ts_wx))
    v_op("tensor_copy", h_sb[:, :], ph[:, 0:1024], inc=True)
    vs_hw = vq("tensor_tensor", hw_sb[:, :], h_sb[:, :], pwx[:, 0:1024], ALU.mult)
    base_bank_free_s = [0, 0, 0, 0]
    wblk_free_t = wblk_free_t[:2]
    wblk_free_t[0] = max(wblk_free_t[0], cnt["ts"])
    wblk_free_t[1] = max(wblk_free_t[1], cnt["ts"])
    for ot in range(32):
        sl = widx % 2
        W("sync", "cc", 10)
        W("sync", "ts", wblk_free_t[sl])
        for j in range(4):
            dvw = dma("sync", "dw%d" % sl,
                      wgt[sl][:, j * 1024:(j + 1) * 1024].rearrange(
                          "p (a b) -> p a b", b=128),
                      bass.AP(wbt_out[j], (ot // 4) * HBLK + (ot % 4) * 128,
                              [[512, 128], [65536, 8], [1, 128]]))
        widx += 1
        W("tensor", "dw%d" % sl, dvw)
        if ot == 0:
            W("tensor", "vs", vs_hw)
            W("tensor", "dmisc", dball)
        bks = [2 * (ot % 2), 2 * (ot % 2) + 1]
        W("tensor", "ss", max(base_bank_free_s[bks[0]], base_bank_free_s[bks[1]]))
        for fc in range(32):
            for st_ in range(2):
                t_mm(pbase[bks[st_]][:, :], wgt[sl][:, fc * 128:(fc + 1) * 128],
                     xn_sb[:, fc * TOK + st_ * 512: fc * TOK + st_ * 512 + 512],
                     start=(fc == 0), stop=False)
        for st_ in range(2):
            t_mm(pbase[bks[st_]][:, :], ball_sb[:, ot * 128:(ot + 1) * 128],
                 hw_sb[:, st_ * 512:(st_ + 1) * 512], start=False, stop=True,
                 inc=(st_ == 1))
        wblk_free_t[sl] = cnt["ts"]
        if ot == 0:
            W("scalar", "dsm", 48)
            W("scalar", "dkvin", cnt["dkvin"])
            W("scalar", "dki0", cnt["dki0"])
            W("scalar", "dki1", cnt["dki1"])
            W("scalar", "dki2", cnt["dki2"])
        W("scalar", "ts", cnt["ts"])
        for st_ in range(2):
            bk = 2 * (ot % 2) + st_
            base_bank_free_s[bk] = s_act(
                g_sb[:, ot * TOK + st_ * 512: ot * TOK + st_ * 512 + 512],
                pbase[bk][:, :], AF.Silu, bias=b2_sb[:, ot:ot + 1], inc=True)
    g_done_s = cnt["ss"]

    p2ps.close()
    p2sc.close()
    xnsc.close()
    p3ps = ExitStack()
    pO = [PS(p3ps, "pO%d" % i, [128, 512], F32) for i in range(2)]
    pST = [PS(p3ps, "pST%d" % i, [128, 1024], F32) for i in range(2)]
    pdr = PS(p3ps, "pdr", [128, 512], F32)
    pot1 = PS(p3ps, "pot1", [128, 1024], BF16)
    osc = ExitStack()
    o_sb = SB(osc, "o_sb", [128, NH * TOK], BF16)
    p3sc = ExitStack()
    kbuf.append(SB(p3sc, "kbuf1", [128, D], BF16))
    vbuf.append(SB(p3sc, "vbuf1", [128, D], BF16))
    pt_sb = [SB(p3sc, "pt0", [128, 1024], BF16), SB(p3sc, "pt1", [128, 1024], BF16)]
    dacc = SB(p3sc, "dacc", [128, 1024], F32)
    otm = SB(p3sc, "otm", [128, 1024], BF16)

    def emit_kv_load(idx, slot_free_ts):
        h, half = divmod(idx, 2)
        sl = idx % 2
        W("sync", "cc", 7 if h < 4 else 9)
        W("sync", "ts", slot_free_ts)
        kt_src = ka_out if h < 4 else kb_out
        dma("sync", "dkv%d" % sl,
            kbuf[sl][:, :].rearrange("p (a b) -> p a b", b=1024),
            bass.AP(kt_src, 4 * half * HBLK + (h % 4) * 131072,
                    [[1024, 128], [HBLK, 4], [1, 1024]]))
        W("sync", "cc", 12)
        last = 0
        for r4 in range(4):
            dma("sync", "dkv%d" % sl,
                vbuf[sl][:, r4 * 1024: r4 * 1024 + 512].rearrange(
                    "p (a b) -> p a b", b=128),
                bass.AP(va_out, (4 * half + r4) * HBLK + h * 128,
                        [[1024, 128], [131072, 4], [1, 128]]))
            last = dma("sync", "dkv%d" % sl,
                       vbuf[sl][:, r4 * 1024 + 512: (r4 + 1) * 1024].rearrange(
                           "p (a b) -> p a b", b=128),
                       bass.AP(vb_out, (4 * half + r4) * HBLK + h * 128,
                               [[1024, 128], [131072, 4], [1, 128]]))
        return last

    ts_p2_done = cnt["ts"]
    kv_dma = {0: emit_kv_load(0, 0), 1: emit_kv_load(1, ts_p2_done)}
    scores_ts = {}
    exp_ss = {}
    pv_ts = {-2: 0, -1: 0}
    dadd_vs = {-2: 0, -1: 0}
    o_evac_v = 0
    dred_vs = {-1: p0_v_done}
    ts_transp = {-1: 0}
    pot1_free_v = [0]
    otm_free_t = [0]
    pending_dance = []

    def emit_dance(h):
        W("tensor", "vs", o_evac_v)
        W("tensor", "vs", pot1_free_v[0])
        for tt in range(8):
            t_tr(pot1[:, tt * 128:(tt + 1) * 128],
                 o_sb[:, h * TOK + tt * 128: h * TOK + tt * 128 + 128],
                 ident[:, :], inc=(tt == 7))
        ts_o1 = cnt["ts"]
        W("vector", "ts", ts_o1)
        W("vector", "vs", dred_vs[h])
        W("vector", "ts", otm_free_t[0])
        for tt in range(8):
            v_op("tensor_scalar", otm[:, tt * 128:(tt + 1) * 128],
                 pot1[:, tt * 128:(tt + 1) * 128],
                 dtm2_sb[:, h * 8 + tt: h * 8 + tt + 1], None, ALU.mult,
                 inc=(tt == 7))
        vs_otm = cnt["vs"]
        W("tensor", "vs", vs_otm)
        for tt in range(8):
            t_tr(pot1[:, tt * 128:(tt + 1) * 128],
                 otm[:, tt * 128:(tt + 1) * 128], ident[:, :], inc=(tt == 7))
        otm_free_t[0] = cnt["ts"]
        W("vector", "ts", otm_free_t[0])
        pot1_free_v[0] = v_op("tensor_copy", o_sb[:, h * TOK:(h + 1) * TOK],
                              pot1[:, 0:1024], inc=True)

    def emit_scores(g):
        h, kt = divmod(g, 64)
        idx = 2 * h + (kt // 32)
        sl = idx % 2
        kt32 = kt % 32
        stb = g % 2
        W("tensor", "dkv%d" % sl, kv_dma[idx])
        if g == 0:
            W("tensor", "vs", vs_q_done)
            W("tensor", "vs", vs_bh)
            W("tensor", "vs", vs_hw)
            W("tensor", "ss", g_done_s)
        W("tensor", "ss", exp_ss.get(g - 2, 0))
        if kt == 0:
            W("tensor", "vs", o_evac_v)
        for st_ in range(2):
            t_mm(pST[stb][:, st_ * 512:(st_ + 1) * 512],
                 kbuf[sl][:, kt32 * 128:(kt32 + 1) * 128],
                 q_sb[:, h * TOK + st_ * 512: h * TOK + st_ * 512 + 512],
                 start=True, stop=True, inc=(st_ == 1))
        scores_ts[g] = cnt["ts"]

    def emit_exp(g):
        h = g // 64
        stb = g % 2
        W("scalar", "ts", scores_ts[g])
        W("scalar", "ts", pv_ts.get(g - 2, 0))
        W("scalar", "vs", dadd_vs.get(g - 2, 0))
        s_act(pt_sb[stb][:, :], pST[stb][:, 0:1024], AF.Exp,
              bias=bh_sb[:, h:h + 1], scale=ISQ, inc=True)
        exp_ss[g] = cnt["ss"]

    def emit_dadd(g):
        kt = g % 64
        stb = g % 2
        W("vector", "ss", exp_ss[g])
        if kt == 0:
            W("vector", "ts", ts_transp.get(g // 64 - 1, 0))
            v_op("tensor_copy", dacc[:, :], pt_sb[stb][:, :], inc=True)
        else:
            W("vector", "vs", cnt["vs"])
            v_op("tensor_add", dacc[:, :], dacc[:, :], pt_sb[stb][:, :], inc=True)
        dadd_vs[g] = cnt["vs"]

    def emit_pv(g):
        h, kt = divmod(g, 64)
        idx = 2 * h + (kt // 32)
        sl = idx % 2
        kt32 = kt % 32
        stb = g % 2
        W("tensor", "ss", exp_ss[g])
        for st_ in range(2):
            t_mm(pO[st_][:, :], vbuf[sl][:, kt32 * 128:(kt32 + 1) * 128],
                 pt_sb[stb][:, st_ * 512:(st_ + 1) * 512],
                 start=(kt == 0), stop=(kt == 63), inc=(st_ == 1))
        pv_ts[g] = cnt["ts"]

    emit_scores(0)
    for g in range(NH * 64):
        h, kt = divmod(g, 64)
        if g + 1 < NH * 64 and (g + 1) % 64 != 0:
            emit_scores(g + 1)
        emit_exp(g)
        emit_dadd(g)
        emit_pv(g)
        if kt == 4 and pending_dance:
            emit_dance(pending_dance.pop(0))
        if kt == 31:
            idx = 2 * h + 0
            if idx + 2 < 16:
                kv_dma[idx + 2] = emit_kv_load(idx + 2, cnt["ts"])
        if kt == 63:
            idx = 2 * h + 1
            if idx + 2 < 16:
                kv_dma[idx + 2] = emit_kv_load(idx + 2, cnt["ts"])
            W("vector", "ts", pv_ts[g])
            for st_ in range(2):
                v_op("tensor_copy",
                     o_sb[:, h * TOK + st_ * 512: h * TOK + st_ * 512 + 512],
                     pO[st_][:, :], inc=(st_ == 1))
            o_evac_v = cnt["vs"]
            for half4 in range(2):
                W("tensor", "vs", dadd_vs[g] if half4 == 0 else cnt["vs"])
                W("tensor", "vs", dred_vs.get(h - 1, 0) if half4 == 0 else cnt["vs"])
                for tt4 in range(4):
                    tt = half4 * 4 + tt4
                    t_tr(pdr[:, tt4 * 128:(tt4 + 1) * 128],
                         dacc[:, tt * 128:(tt + 1) * 128], identf[:, :],
                         inc=(tt4 == 3))
                W("vector", "ts", cnt["ts"])
                for tt4 in range(4):
                    tt = half4 * 4 + tt4
                    v_op("reduce_sum", dtm_sb[:, h * 8 + tt: h * 8 + tt + 1],
                         pdr[:, tt4 * 128:(tt4 + 1) * 128], axis=AX.X,
                         inc=(tt4 == 3))
            ts_transp[h] = cnt["ts"]
            vq("reciprocal", dtm2_sb[:, h * 8:(h + 1) * 8],
               dtm_sb[:, h * 8:(h + 1) * 8])
            dred_vs[h] = cnt["vs"]
            if g + 1 < NH * 64:
                emit_scores(g + 1)
            pending_dance.append(h)

    while pending_dance:
        emit_dance(pending_dance.pop(0))
    p3ps.close()
    p3tps = ExitStack()
    o_norm_v = cnt["vs"]

    p3tps.close()
    p3sc.close()
    p4ps = ExitStack()
    ps4 = [PS(p4ps, "ps4_%d" % i, [128, 512], F32) for i in range(4)]
    psg = [PS(p4ps, "psg0", [128, 1024], BF16), PS(p4ps, "psg1", [128, 1024], BF16)]
    latsc.close()
    qsc.close()
    p4sc = ExitStack()
    xcol = [SB(p4sc, "xcol%d" % i, [128, 1024], F32) for i in range(4)]
    outp = [SB(p4sc, "outp%d" % i, [128, 1024], F32) for i in range(4)]


    s4_bank_free_v = [0, 0, 0, 0]
    psg_free_v = [0, 0]
    sgv_free_t = [0, 0]
    vs_sg = {}
    out_store_done = {-4: 0, -3: 0, -2: 0, -1: 0}
    xcol_d = {}
    mm_ts = {}
    for par0 in range(4):
        W("scalar", "vs", o_norm_v)
        xcol_d[par0] = dma("scalar", "dx%d" % par0,
                           xcol[par0][:, :].rearrange("p (a b) -> p a b", b=128),
                           bass.AP(x_sl, par0 * 128,
                                   [[4096, 128], [524288, 8], [1, 128]]))

    ts_p4_wgt_free = max(wblk_free_t[0], wblk_free_t[1])
    dwo_done = {}

    def wo_slot(q4):
        return wgt[q4 % 2][:, (q4 // 2) * 1024:(q4 // 2) * 1024 + 1024]

    def wo_load(dmt, gate_ts):
        q4 = dmt % 4
        W("gpsimd", "ts", gate_ts)
        dwo_done[dmt] = dma("gpsimd", "dwo%d" % q4,
                            wo_slot(q4).rearrange("p (a b) -> p a b", b=128),
                            bass.AP(agw_out, (dmt // 4) * AGW_BLK + AGW_WO + (dmt % 4) * 128,
                                    [[512, 128], [65536, 8], [1, 128]]))

    for dmt0 in range(4):
        wo_load(dmt0, ts_p4_wgt_free)

    def p4_mms(dmt):
        q4 = dmt % 4
        W("tensor", "dwo%d" % q4, dwo_done[dmt])
        if dmt == 0:
            W("tensor", "vs", o_norm_v)
        bks4 = [2 * (dmt % 2), 2 * (dmt % 2) + 1]
        W("tensor", "vs", max(s4_bank_free_v[bks4[0]], s4_bank_free_v[bks4[1]]))
        wsl = wo_slot(q4)
        for dkc in range(8):
            for st_ in range(2):
                t_mm(ps4[bks4[st_]][:, :], wsl[:, dkc * 128:(dkc + 1) * 128],
                     o_sb[:, dkc * TOK + st_ * 512: dkc * TOK + st_ * 512 + 512],
                     start=(dkc == 0), stop=(dkc == 7),
                     inc=(dkc == 7 and st_ == 1))
        mm_ts[dmt] = cnt["ts"]
        if dmt + 4 < 32:
            wo_load(dmt + 4, mm_ts[dmt])

    def p4_evac(dmt):
        par = dmt % 2
        if dmt == 0:
            W("vector", "ss", g_done_s)
        W("vector", "ts", sgv_free_t[par])
        W("vector", "ts", mm_ts[dmt])
        for st_ in range(2):
            bk = 2 * (dmt % 2) + st_
            s4_bank_free_v[bk] = v_op(
                "tensor_tensor",
                wgt[par][:, 2048 + st_ * 512: 2048 + st_ * 512 + 512],
                ps4[bk][:, :],
                g_sb[:, dmt * TOK + st_ * 512: dmt * TOK + st_ * 512 + 512],
                ALU.mult, inc=True)
        vs_sg[dmt] = cnt["vs"]

    def p4_fin(dmt):
        par = dmt % 2
        q4 = dmt % 4
        W("tensor", "vs", vs_sg[dmt])
        W("tensor", "vs", psg_free_v[par])
        for tt in range(8):
            t_tr(psg[par][:, tt * 128:(tt + 1) * 128],
                 wgt[par][:, 2048 + tt * 128: 2048 + tt * 128 + 128], ident[:, :],
                 inc=(tt == 7))
        sgv_free_t[par] = cnt["ts"]
        W("vector", "ts", sgv_free_t[par])
        W("vector", "dx%d" % q4, xcol_d[dmt])
        W("vector", "st%d" % q4, out_store_done[dmt - 4])
        psg_free_v[par] = v_op("scalar_tensor_tensor", outp[q4][:, :],
                               psg[par][:, 0:1024], 1.0, xcol[q4][:, :],
                               ALU.mult, ALU.add, inc=True)
        W("sync", "vs", psg_free_v[par])
        out_store_done[dmt] = dma("sync", "st%d" % q4,
                                  bass.AP(out_d, dmt * 128,
                                          [[4096, 128], [524288, 8], [1, 128]]),
                                  outp[q4][:, :].rearrange("p (a b) -> p a b", b=128))
        if dmt + 4 < 32:
            W("scalar", "vs", psg_free_v[par])
            xcol_d[dmt + 4] = dma("scalar", "dx%d" % q4,
                                  xcol[q4][:, :].rearrange("p (a b) -> p a b", b=128),
                                  bass.AP(x_sl, (dmt + 4) * 128,
                                          [[4096, 128], [524288, 8], [1, 128]]))

    p4_mms(0)
    p4_mms(1)
    for dmt in range(32):
        p4_evac(dmt)
        if dmt % 2 == 1:
            p4_fin(dmt - 1)
            p4_fin(dmt)
        if dmt + 2 < 32:
            p4_mms(dmt + 2)
    for stn in ("st0", "st1", "st2", "st3"):
        W("sync", stn, cnt[stn])

    p4ps.close()
    p4sc.close()
    osc.close()
    gssc.close()
    kvearly.close()
    wqesc_dummy = None
    gsc.close()

    with nc.Block() as block:
        @block.sync
        def _(e):
            for f in prog["sync"]:
                f(e)

        @block.gpsimd
        def _(e):
            for f in prog["gpsimd"]:
                f(e)

        @block.tensor
        def _(e):
            for f in prog["tensor"]:
                f(e)

        @block.vector
        def _(e):
            for f in prog["vector"]:
                f(e)

        @block.scalar
        def _(e):
            for f in prog["scalar"]:
                f(e)

    return nc


_NC_CACHE = []


def kernel(L0_attn_x, L0_attn_wq, L0_attn_wk, L0_attn_wv, L0_attn_wo,
           L0_ffn_W_base, L0_ffn_b_base, L0_ffn_W_router, L0_ffn_lora_A,
           L0_ffn_lora_B):
    x = np.ascontiguousarray(np.asarray(L0_attn_x, dtype=np.float32)[0])
    wq = np.asarray(L0_attn_wq, dtype=np.float32)
    wk = np.asarray(L0_attn_wk, dtype=np.float32)
    wv = np.asarray(L0_attn_wv, dtype=np.float32)
    wo = np.asarray(L0_attn_wo, dtype=np.float32)
    wb = np.asarray(L0_ffn_W_base, dtype=np.float32)
    bb = np.asarray(L0_ffn_b_base, dtype=np.float32)
    wr = np.asarray(L0_ffn_W_router, dtype=np.float32)
    la = np.asarray(L0_ffn_lora_A, dtype=np.float32)
    lb = np.asarray(L0_ffn_lora_B, dtype=np.float32)

    ball = np.ascontiguousarray(lb.transpose(0, 2, 1).reshape(ER, D))
    lat = np.ascontiguousarray(la.reshape(ER, D).T)
    b2 = np.ascontiguousarray(bb.reshape(32, 128).T)
    gmat = np.ascontiguousarray(
        (np.kron(np.eye(8), np.ones((1, 16))) * 2.0).astype(np.float32))
    id_bf = np.eye(128, dtype=ml_dtypes.bfloat16)
    id_f32 = np.eye(128, dtype=np.float32)

    if not _NC_CACHE:
        _NC_CACHE.append(build_program())
    nc = _NC_CACHE[0]

    in_maps = []
    for c in range(NC):
        in_maps.append({
            "x_sl": np.ascontiguousarray(x[c * TOK:(c + 1) * TOK]),
            "wq_sl": np.ascontiguousarray(wq[:, c * 512:(c + 1) * 512]),
            "wk_sl": np.ascontiguousarray(wk[:, c * 128:(c + 1) * 128]),
            "wv_sl": np.ascontiguousarray(wv[:, c * 128:(c + 1) * 128]),
            "wo_sl": np.ascontiguousarray(wo[:, c * 512:(c + 1) * 512]),
            "wbt_sl": np.ascontiguousarray(wb[c * 512:(c + 1) * 512, :].T),
            "ball_sl": np.ascontiguousarray(ball[:, c * 512:(c + 1) * 512]),
            "lat_sl": np.ascontiguousarray(lat[c * 512:(c + 1) * 512, :]),
            "wr_in": wr,
            "b2_in": b2,
            "gmat_in": gmat,
            "id_bf": id_bf,
            "id_f32": id_f32,
        })
    res = run_bass_kernel_spmd(nc, in_maps, core_ids=list(range(NC)))
    out = np.concatenate([np.asarray(res.results[c]["out"]) for c in range(NC)],
                         axis=0)
    return out.reshape(1, S, D).astype(np.float32)


# version 39 (final)
# speedup vs baseline: 1.1376x; 1.1376x over previous
import math
import numpy as np
import ml_dtypes
import concourse.bass as bass
import concourse.mybir as mybir
from concourse.bass_utils import run_bass_kernel_spmd

F32 = mybir.dt.float32
BF16 = mybir.dt.bfloat16
AF = mybir.ActivationFunctionType
ALU = mybir.AluOpType
AX = mybir.AxisListType

NC = 8
TOK = 1024
S = 8192
D = 4096
NH = 8
HD = 128
DKV = 1024
ER = 128
EPS = 1e-5
ISQ = 1.0 / math.sqrt(128.0)

AGQ_BLK = 4096 * 128
AGW_WO, AGW_BALL, AGW_LAT = 0, 524288, 589824
AGW_BLK = 655360
HBLK = 524288


ENGS = ("sync", "gpsimd", "tensor", "vector", "scalar")


def build_program():
    nc = bass.Bass("TRN2", target_bir_lowering=False)

    x_sl = nc.dram_tensor("x_sl", [TOK, D], F32, kind="ExternalInput")
    wq_sl = nc.dram_tensor("wq_sl", [D, 512], F32, kind="ExternalInput")
    wk_sl = nc.dram_tensor("wk_sl", [D, 128], F32, kind="ExternalInput")
    wv_sl = nc.dram_tensor("wv_sl", [D, 128], F32, kind="ExternalInput")
    wo_sl = nc.dram_tensor("wo_sl", [DKV, 512], F32, kind="ExternalInput")
    wbt_sl = nc.dram_tensor("wbt_sl", [D, 512], F32, kind="ExternalInput")
    ball_sl = nc.dram_tensor("ball_sl", [ER, 512], F32, kind="ExternalInput")
    lat_sl = nc.dram_tensor("lat_sl", [512, ER], F32, kind="ExternalInput")
    wr_in = nc.dram_tensor("wr_in", [8, D], F32, kind="ExternalInput")
    b2_in = nc.dram_tensor("b2_in", [128, 32], F32, kind="ExternalInput")
    gmat_in = nc.dram_tensor("gmat_in", [8, 128], F32, kind="ExternalInput")
    id_bf = nc.dram_tensor("id_bf", [128, 128], BF16, kind="ExternalInput")
    id_f32 = nc.dram_tensor("id_f32", [128, 128], F32, kind="ExternalInput")
    out_d = nc.dram_tensor("out", [TOK, D], F32, kind="ExternalOutput")

    awk_in = nc.dram_tensor("awk_in", [HBLK], BF16)
    awk_out = nc.dram_tensor("awk_out", [NC * HBLK], BF16)
    awv_in = nc.dram_tensor("awv_in", [HBLK], BF16)
    awv_out = nc.dram_tensor("awv_out", [NC * HBLK], BF16)
    agq_in = nc.dram_tensor("agq_in", [AGQ_BLK], BF16)
    agq_out = nc.dram_tensor("agq_out", [NC * AGQ_BLK], BF16)
    agw_in = nc.dram_tensor("agw_in", [AGW_BLK], BF16)
    agw_out = nc.dram_tensor("agw_out", [NC * AGW_BLK], BF16)
    wbt_in = [nc.dram_tensor("wbt%d_in" % j, [HBLK], BF16) for j in range(4)]
    wbt_out = [nc.dram_tensor("wbt%d_out" % j, [NC * HBLK], BF16) for j in range(4)]
    ka_in = nc.dram_tensor("ka_in", [HBLK], BF16)
    ka_out = nc.dram_tensor("ka_out", [NC * HBLK], BF16)
    kb_in = nc.dram_tensor("kb_in", [HBLK], BF16)
    kb_out = nc.dram_tensor("kb_out", [NC * HBLK], BF16)
    va_in = nc.dram_tensor("va_in", [HBLK], BF16)
    va_out = nc.dram_tensor("va_out", [NC * HBLK], BF16)
    vb_in = nc.dram_tensor("vb_in", [HBLK], BF16)
    vb_out = nc.dram_tensor("vb_out", [NC * HBLK], BF16)

    from contextlib import ExitStack
    gsc = ExitStack()
    wqesc = ExitStack()

    def SB(scope, name, shape, dt, side="left"):
        return scope.enter_context(nc.sbuf_tensor(name, shape, dt, side=side))

    def PS(scope, name, shape, dt):
        return scope.enter_context(nc.psum_tensor(name, shape, dt))

    def SEM(name):
        return gsc.enter_context(nc.semaphore(name))

    sems = {}
    cnt = {}
    for nm in ("dx0", "dx1", "dx2", "dx3", "dw0", "dw1", "dw2", "dw3", "dkv0", "dkv1", "dqe", "dmisc",
               "dsm", "dag", "dkvin", "dki0", "dki1", "dki2", "dwo0", "dwo1", "dwo2", "dwo3", "dsg0", "dsg1", "st0", "st1", "st2", "st3", "ts", "vs",
               "ss", "cc", "gp", "dw4", "dw5", "dw6", "dw7",
               "dh0", "dh1", "dh2", "dh3", "dh4", "dh5", "dh6", "dh7", "dlat",
               "dkh0", "dkh1", "dkh2", "dkh3", "dkh4", "dkh5"):
        sems[nm] = SEM(nm)
        cnt[nm] = 0

    prog = {e: [] for e in ENGS}

    def E(eng, fn):
        prog[eng].append(fn)

    def W(eng, sem_nm, val):
        if val <= 0:
            return
        h = sems[sem_nm]
        E(eng, lambda e, h=h, v=val: e.wait_ge(h, v))

    def bump(sem_nm, n=1):
        cnt[sem_nm] += n
        return cnt[sem_nm]

    def dma(eng, sem_nm, out_ap, in_ap, accum=False):
        h = sems[sem_nm]
        if accum:
            E(eng, lambda e, o=out_ap, i=in_ap, h=h: e.dma_start(
                out=o, in_=i, accum_op=ALU.add).then_inc(h, 16))
        else:
            E(eng, lambda e, o=out_ap, i=in_ap, h=h: e.dma_start(
                out=o, in_=i).then_inc(h, 16))
        return bump(sem_nm, 16)

    def t_mm(out_ap, l_ap, r_ap, start, stop, inc=False):
        if inc:
            h = sems["ts"]
            E("tensor", lambda e, o=out_ap, l=l_ap, r=r_ap, a=start, b=stop, h=h:
              e.matmul(o, l, r, start=a, stop=b).then_inc(h, 1))
            return bump("ts")
        E("tensor", lambda e, o=out_ap, l=l_ap, r=r_ap, a=start, b=stop:
          e.matmul(o, l, r, start=a, stop=b))
        return cnt["ts"]

    def t_tr(out_ap, in_ap, id_ap, inc=False):
        if inc:
            h = sems["ts"]
            E("tensor", lambda e, o=out_ap, i=in_ap, d=id_ap, h=h:
              e.transpose(o, i, d).then_inc(h, 1))
            return bump("ts")
        E("tensor", lambda e, o=out_ap, i=in_ap, d=id_ap: e.transpose(o, i, d))
        return cnt["ts"]

    def v_op(meth, *aps, inc=False, **kw):
        if inc:
            h = sems["vs"]
            E("vector", lambda e, m=meth, a=aps, k=kw, h=h:
              getattr(e, m)(*a, **k).then_inc(h, 1))
            return bump("vs")
        E("vector", lambda e, m=meth, a=aps, k=kw: getattr(e, m)(*a, **k))
        return cnt["vs"]

    def vq(meth, *aps, **kw):
        W("vector", "vs", cnt["vs"])
        return v_op(meth, *aps, inc=True, **kw)

    def s_act(out_ap, in_ap, func, bias=0.0, scale=1.0, accum=None, inc=False):
        kw = dict(bias=bias, scale=scale)
        if accum is not None:
            kw["accum_out"] = accum
        if inc:
            h = sems["ss"]
            E("scalar", lambda e, o=out_ap, i=in_ap, f=func, k=kw, h=h:
              e.activation(o, i, f, **k).then_inc(h, 1))
            return bump("ss")
        E("scalar", lambda e, o=out_ap, i=in_ap, f=func, k=kw: e.activation(o, i, f, **k))
        return cnt["ss"]

    qsc = ExitStack()
    q_sb = SB(qsc, "q_sb", [128, NH * TOK], BF16, side="right")
    ident = SB(gsc, "ident", [128, 128], BF16)
    identf = SB(gsc, "identf", [128, 128], F32)
    ones_c = SB(gsc, "ones_c", [128, 1], BF16)
    onesr = SB(gsc, "onesr", [1, 128], BF16)
    gmat_sb = SB(gsc, "gmat_sb", [8, 128], BF16)
    b2_sb = SB(gsc, "b2_sb", [128, 32], F32)
    wrt_sb = SB(gsc, "wrt_sb", [128, 256], BF16)
    ltm_sb = SB(gsc, "ltm_sb", [128, 64], F32)
    wtm_sb = SB(gsc, "wtm_sb", [128, 64], F32)
    wfm_bf = SB(gsc, "wfm_bf", [8, TOK], BF16)
    rt0 = SB(gsc, "rt0", [128, 8], F32)
    rt1 = SB(gsc, "rt1", [128, 8], F32)
    rt2 = SB(gsc, "rt2", [128, 8], F32)
    rs0 = SB(gsc, "rs0", [128, 1], F32)
    rs1 = SB(gsc, "rs1", [128, 1], F32)
    stats = SB(gsc, "stats", [128, 16], F32)
    mq_sb = SB(gsc, "mq_sb", [1, 16], F32)
    bh_sb = SB(gsc, "bh_sb", [128, 8], F32)
    bhr_sb = SB(gsc, "bhr_sb", [1, 8], BF16)
    dtm_sb = SB(gsc, "dtm_sb", [128, 64], F32)
    dtm2_sb = SB(gsc, "dtm2_sb", [128, 64], F32)
    wgt = [SB(gsc, "wgt0", [128, D], BF16), SB(gsc, "wgt1", [128, D], BF16)]
    h_sb = SB(gsc, "h_sb", [128, TOK], F32)
    hw_sb = SB(gsc, "hw_sb", [128, TOK], BF16)
    kearly = ExitStack()
    ktmp = [SB(kearly, "ktmp%d" % i, [128, 1024], BF16) for i in range(3)]
    latsc = ExitStack()
    latw = SB(latsc, "latw", [128, D], BF16, side="right")
    p0sc = ExitStack()
    xtsc = ExitStack()
    xt = [SB(xtsc, "xt%d" % i, [128, D], F32) for i in range(3)]
    wr_sb = xt[2][0:8, 0:2048].bitcast(BF16)
    lg_sb = h_sb[0:8, :]
    wqe_acc = SB(wqesc, "wqe_acc", [128, D], F32)
    wqe_bf = SB(wqesc, "wqe_bf", [128, D], BF16)

    dma("gpsimd", "dmisc", gmat_sb[:, :], gmat_in.ap())
    rg = [list(range(NC))]

    def trigger_ag(in_t, out_t):
        E("gpsimd", lambda e, a=in_t, b=out_t: e.collective_compute(
            "AllGather", ALU.bypass, replica_groups=rg,
            ins=[a.ap().opt()], outs=[b.ap().opt()]).then_inc(sems["cc"], 1))
        return bump("cc")

    dma("gpsimd", "dag", bass.AP(awk_in, 0, [[4096, 128], [128, 32], [1, 128]]),
        bass.AP(wk_sl, 0, [[128, 128], [16384, 32], [1, 128]]))
    W("gpsimd", "dag", cnt["dag"])
    trigger_ag(awk_in, awk_out)
    dma("gpsimd", "dag", bass.AP(awv_in, 0, [[4096, 128], [128, 32], [1, 128]]),
        bass.AP(wv_sl, 0, [[128, 128], [16384, 32], [1, 128]]))
    W("gpsimd", "dag", cnt["dag"])
    trigger_ag(awv_in, awv_out)
    for j in range(4):
        v = dma("gpsimd", "dqe",
                wqe_acc[:, :].rearrange("p (a b) -> p a b", b=128),
                bass.AP(wq_sl, j * 128, [[512, 128], [65536, 32], [1, 128]]),
                accum=(j > 0))
        W("gpsimd", "dqe", v)
    E("gpsimd", lambda e: e.tensor_copy(wqe_bf[:, :], wqe_acc[:, :]).then_inc(
        sems["gp"], 1))
    bump("gp")
    W("gpsimd", "gp", cnt["gp"])
    dma("gpsimd", "dag", bass.AP(agq_in, 0, [[4096, 128], [128, 32], [1, 128]]),
        wqe_bf[:, :].rearrange("p (a b) -> p a b", b=128))
    W("gpsimd", "dag", cnt["dag"])
    trigger_ag(agq_in, agq_out)
    dma("gpsimd", "dag", bass.AP(agw_in, AGW_WO, [[512, 1024], [1, 512]]), wo_sl.ap())
    dma("gpsimd", "dag", bass.AP(agw_in, AGW_BALL, [[512, 128], [1, 512]]), ball_sl.ap())
    dma("gpsimd", "dag", bass.AP(agw_in, AGW_LAT, [[128, 512], [1, 128]]), lat_sl.ap())
    W("gpsimd", "dag", cnt["dag"])
    trigger_ag(agw_in, agw_out)
    for j in range(4):
        dma("gpsimd", "dag", bass.AP(wbt_in[j], 0, [[512, 1024], [1, 512]]),
            wbt_sl.ap()[j * 1024:(j + 1) * 1024, :])
    W("gpsimd", "dag", cnt["dag"])
    for j in (0, 1):
        trigger_ag(wbt_in[j], wbt_out[j])
    dma("gpsimd", "dmisc", wr_sb[:, :], wr_in.ap())
    W("gpsimd", "cc", 4)
    for r in range(NC):
        dlat = dma("gpsimd", "dlat",
                   latw[:, r * 512:(r + 1) * 512].rearrange("p (a b) -> p a b", b=128),
                   bass.AP(agw_out, r * AGW_BLK + AGW_LAT,
                           [[128, 128], [16384, 4], [1, 128]]))

    xnsc = ExitStack()
    xn_sb = SB(xnsc, "xn_sb", [128, 32 * TOK], BF16, side="right")
    hi_sb = SB(p0sc, "hi_sb", [128, D], BF16)
    qkvps = ExitStack()
    pqkv = [PS(qkvps, "pqkv%d" % i, [128, 512], F32) for i in range(4)]
    p0ps = ExitStack()
    ptr = [PS(p0ps, "ptr0", [128, 1024], BF16), PS(p0ps, "ptr1", [128, 1024], BF16)]
    plog = PS(p0ps, "plog", [128, 1024], F32)

    dma("sync", "dsm", ident[:, :], id_bf.ap())
    dma("sync", "dsm", identf[:, :], id_f32.ap())
    dma("sync", "dsm", b2_sb[:, :], b2_in.ap())
    v_op("memset", ones_c[:, :], 1.0)
    v_op("memset", onesr[:, :], 1.0, inc=True)
    vs_memset = cnt["vs"]

    W("tensor", "dmisc", 32)
    W("tensor", "dsm", 48)
    for fc in range(32):
        t_tr(ptr[0][:, fc * 8:(fc + 1) * 8], wr_sb[:, fc * 128:(fc + 1) * 128],
             ident[0:8, 0:8], inc=(fc == 31))
    ts_wrt = cnt["ts"]
    W("vector", "ts", ts_wrt)
    vs_wrt = v_op("tensor_copy", wrt_sb[:, :], ptr[0][:, 0:256], inc=True)

    pdvK = [0, 0]
    ev3 = {}
    strip_ts = {}
    k_store_done = {-3: 0, -2: 0, -1: 0}
    k_sq_done = {-3: 0, -2: 0, -1: 0}
    vs_k_at = {}
    bank_free_v = [0] * 4

    def emit_kevac(dt):
        for st_, bk in enumerate([(2 * dt) % 4, (2 * dt + 1) % 4]):
            W("vector", "ts", strip_ts[(dt, st_ * 2 + 1)])
            bank_free_v[bk] = v_op("tensor_copy",
                                   ktmp[dt % 3][:, st_ * 512:(st_ + 1) * 512],
                                   pqkv[bk][:, :], inc=True)
        vs_k_at[dt] = cnt["vs"]
        W("sync", "vs", vs_k_at[dt])
        k_store_done[dt] = dma("sync", "dki%d" % (dt % 3),
                               bass.AP(ka_in, (dt % 4) * 131072,
                                       [[1024, 128], [1, 1024]]),
                               ktmp[dt % 3][:, :])

    def emit_kstrips(pr):
        W("tensor", "vs", ev3[2 * pr + 1])
        for ki in range(2):
            if pr == 0:
                W("tensor", "dw%d" % ki, pdvK[ki])
            bank = ki * 2 + pr // 2
            col = (pr % 2) * 256
            for fc in range(32):
                t_mm(pqkv[bank][:, col:col + 256],
                     wgt[ki][:, fc * 128:(fc + 1) * 128],
                     xn_sb[:, fc * TOK + pr * 256: fc * TOK + pr * 256 + 256],
                     start=(fc == 0), stop=(fc == 31), inc=(fc == 31))
            strip_ts[(ki, pr)] = cnt["ts"]

    wgt4 = list(wgt)
    xt_free_s = [0, 0, 0]
    xt_free_v = [0, 0, 0]
    hi_free_t = 0
    tr_bank_free_v = [vs_wrt, vs_wrt]
    xn_done_v = 0
    for tt in range(8):
        sl = tt % 3
        W("sync", "ss", xt_free_s[sl])
        W("sync", "vs", xt_free_v[sl])
        if tt == 2:
            W("sync", "ts", ts_wrt)
        dv = dma("sync", "dx%d" % sl, xt[sl][:, :],
                 x_sl.ap()[tt * 128:(tt + 1) * 128, :])
        if tt == 1:
            for pi in range(2):
                W("scalar", "cc", 1)
                pdvK[pi] = dma("scalar", "dw%d" % pi, wgt[pi][:, :],
                               bass.AP(awk_out, pi * HBLK,
                                       [[4096, 128], [1, 4096]]))
        W("vector", "dx%d" % sl, dv)
        if tt == 0:
            W("vector", "gp", 1)
        v_op("tensor_scalar", wqe_acc[:, :], xt[sl][:, :], 1.0, None, ALU.mult,
             op1=ALU.add, accum_out=stats[:, 0:1], inc=True)
        W("scalar", "dx%d" % sl, dv)
        W("scalar", "ts", hi_free_t)
        s_act(hi_sb[:, :], xt[sl][:, :], AF.Square, accum=stats[:, 1:2], inc=True)
        ss_sq = cnt["ss"]
        W("vector", "ss", ss_sq)
        vq("tensor_scalar", stats[:, 2:3], stats[:, 0:1], 1.0 / D, None, ALU.mult)
        vq("tensor_scalar", stats[:, 3:4], stats[:, 1:2], 1.0 / D, None, ALU.mult)
        vq("tensor_tensor", stats[:, 6:7], stats[:, 2:3], stats[:, 2:3], ALU.mult)
        vq("tensor_sub", stats[:, 3:4], stats[:, 3:4], stats[:, 6:7])
        vq("tensor_scalar", stats[:, 3:4], stats[:, 3:4], EPS, None, ALU.add)
        vq("reciprocal", stats[:, 6:7], stats[:, 3:4])
        vs_rv = cnt["vs"]
        W("scalar", "vs", vs_rv)
        s_act(stats[:, 4:5], stats[:, 6:7], AF.Sqrt, inc=True)
        ss_a = cnt["ss"]
        W("vector", "ss", ss_a)
        vq("tensor_tensor", stats[:, 5:6], stats[:, 2:3], stats[:, 4:5], ALU.mult)
        vq("tensor_scalar", stats[:, 5:6], stats[:, 5:6], -1.0, None, ALU.mult)
        vs_nb = cnt["vs"]
        W("scalar", "vs", vs_nb)
        W("scalar", "ts", hi_free_t)
        s_act(hi_sb[:, :], xt[sl][:, :], AF.Identity, bias=stats[:, 5:6],
              scale=stats[:, 4:5], inc=True)
        ss_xn = cnt["ss"]
        xt_free_s[sl] = ss_xn
        xt_free_v[sl] = cnt["vs"]
        W("tensor", "ss", ss_xn)
        for g in range(4):
            par = g % 2
            W("tensor", "vs", tr_bank_free_v[par])
            for k in range(8):
                fc = g * 8 + k
                t_tr(ptr[par][:, k * 128:(k + 1) * 128],
                     hi_sb[:, fc * 128:(fc + 1) * 128], ident[:, :],
                     inc=(k == 7))
            ts_g = cnt["ts"]
            if g == 3:
                hi_free_t = ts_g
            W("vector", "ts", ts_g)
            dst = xn_sb[:, :].rearrange("p (a b) -> p a b", b=TOK)[
                :, g * 8:(g + 1) * 8, tt * 128:(tt + 1) * 128]
            src = ptr[par][:, :].rearrange("p (a b) -> p a b", b=128)
            tr_bank_free_v[par] = v_op("tensor_copy", dst, src, inc=True)
        ev3[tt] = cnt["vs"]
        if tt >= 3 and tt % 2 == 1:
            emit_kstrips((tt - 3) // 2)
    emit_kstrips(2)
    emit_kstrips(3)
    emit_kevac(0)
    emit_kevac(1)
    xn_done_v = cnt["vs"]

    W("tensor", "vs", xn_done_v)
    W("tensor", "vs", vs_wrt)
    for fc in range(32):
        for st_ in range(2):
            t_mm(plog[0:8, st_ * 512:(st_ + 1) * 512],
                 wrt_sb[:, fc * 8:(fc + 1) * 8],
                 xn_sb[:, fc * TOK + st_ * 512: fc * TOK + st_ * 512 + 512],
                 start=(fc == 0), stop=(fc == 31), inc=(fc == 31 and st_ == 1))
    ts_log = cnt["ts"]
    W("vector", "ts", ts_log)
    vs_lg = v_op("tensor_copy", lg_sb[:, :], plog[0:8, 0:1024], inc=True)
    W("tensor", "vs", vs_lg)
    for tt in range(8):
        t_tr(plog[:, 512 + tt * 8:512 + (tt + 1) * 8],
             lg_sb[:, tt * 128:(tt + 1) * 128],
             identf[0:8, 0:8], inc=(tt == 7))
    ts_ltm = cnt["ts"]
    W("vector", "ts", ts_ltm)
    v_op("tensor_copy", ltm_sb[:, :], plog[:, 512:576], inc=True)
    p0_psum_free_v = cnt["vs"]
    for tt in range(8):
        lt = ltm_sb[:, tt * 8:(tt + 1) * 8]
        vq("reduce_max", rs0[:, :], lt, axis=AX.X)
        vq("tensor_scalar", rt1[:, :], lt, rs0[:, :], None, ALU.is_ge)
        vq("tensor_scalar", rs0[:, :], rs0[:, :], -1.0, None, ALU.mult)
        vs_m1 = cnt["vs"]
        W("scalar", "vs", vs_m1)
        s_act(rt0[:, :], lt, AF.Exp, bias=rs0[:, :], inc=True)
        ss_pr = cnt["ss"]
        W("vector", "ss", ss_pr)
        vq("tensor_tensor", rt2[:, :], rt0[:, :], rt1[:, :], ALU.mult)
        vq("tensor_sub", rt2[:, :], rt0[:, :], rt2[:, :])
        vq("reduce_max", rs1[:, :], rt2[:, :], axis=AX.X)
        vq("tensor_scalar", rt1[:, :], rt0[:, :], rs1[:, :], None, ALU.is_ge)
        vq("reduce_max", rs0[:, :], rt0[:, :], axis=AX.X)
        vq("tensor_add", rs1[:, :], rs1[:, :], rs0[:, :])
        vq("reciprocal", rs1[:, :], rs1[:, :])
        vq("tensor_tensor", rt2[:, :], rt0[:, :], rt1[:, :], ALU.mult)
        vq("tensor_scalar", wtm_sb[:, tt * 8:(tt + 1) * 8], rt2[:, :],
           rs1[:, :], None, ALU.mult)
    vs_w = cnt["vs"]

    p0ps.close()
    p0sc.close()
    wqesc.close()
    xtsc.close()
    p1wsc = ExitStack()
    wgt2 = [SB(p1wsc, "wgt2_%d" % i, [128, D], BF16, side="right") for i in range(6)]
    wgt4.extend(wgt2)
    p1sc = ExitStack()
    p1ps = ExitStack()
    pnorm = PS(p1ps, "pnorm", [128, 1024], F32)
    pbb = PS(p1ps, "pbb", [128, 512], F32)
    pvt = PS(p1ps, "pvt", [128, 1024], BF16)
    q2_sb = SB(p1sc, "q2_sb", [128, TOK], BF16)
    wgt4 = wgt + wgt2
    vfm = [SB(p1sc, "vfm0", [128, 1024], BF16), SB(p1sc, "vfm1", [128, 1024], BF16)]
    vtm = SB(p1sc, "vtm", [128, 1024], BF16)

    wblk_free_t = [0] * 8
    p0_v_done = p0_psum_free_v

    def qkv_block(kind, dt):
        nonlocal widx
        sl = widx % 8
        if kind == "q":
            ten, off, cc_need = agq_out, dt * AGQ_BLK, 3
        elif kind == "k":
            ten, off, cc_need = awk_out, dt * HBLK, 1
        else:
            ten, off, cc_need = awv_out, dt * HBLK, 2
        qe, s0, s1 = "scalar", "dw%d" % sl, "dh%d" % sl
        W(qe, "cc", cc_need)
        W(qe, "ts", wblk_free_t[sl])
        dvw_h = []
        for j, snm in enumerate((s0, s1)):
            dvw_h.append(dma(qe, snm,
                             wgt4[sl][:, j * 2048:(j + 1) * 2048],
                             bass.AP(ten, off + j * 2048,
                                     [[4096, 128], [1, 2048]])))
        widx += 1
        bks = [(2 * dt) % 4, (2 * dt + 1) % 4]
        W("tensor", "vs", max(bank_free_v[bks[0]], bank_free_v[bks[1]]))
        for fc in range(32):
            if fc % 16 == 0:
                W("tensor", (s0 if fc == 0 else s1), dvw_h[fc // 16])
            for st_ in range(2):
                t_mm(pqkv[bks[st_]][:, :], wgt4[sl][:, fc * 128:(fc + 1) * 128],
                     xn_sb[:, fc * TOK + st_ * 512: fc * TOK + st_ * 512 + 512],
                     start=(fc == 0), stop=(fc == 31),
                     inc=(fc == 31))
        wblk_free_t[sl] = cnt["ts"]
        return bks

    nrm_mm_done = 0
    nrm_red_done = p0_v_done

    def norms(src_ap, col):
        nonlocal nrm_mm_done, nrm_red_done
        W("scalar", "ts", nrm_mm_done)
        s_act(q2_sb[:, :], src_ap, AF.Square, inc=True)
        W("tensor", "ss", cnt["ss"])
        W("tensor", "vs", vs_memset)
        W("tensor", "vs", nrm_red_done)
        for st_ in range(2):
            t_mm(pnorm[0:1, st_ * 512:(st_ + 1) * 512], ones_c[:, :],
                 q2_sb[:, st_ * 512:(st_ + 1) * 512], start=True, stop=True,
                 inc=(st_ == 1))
        nrm_mm_done = cnt["ts"]
        W("vector", "ts", nrm_mm_done)
        nrm_red_done = v_op("reduce_max", mq_sb[0:1, col:col + 1], pnorm[0:1, 0:1024],
                            axis=AX.X, inc=True)

    widx = 2
    wblk_free_t[0] = strip_ts[(0, 3)]
    wblk_free_t[1] = strip_ts[(1, 3)]
    for dt in range(2, NH):
        banks = qkv_block("k", dt)
        for st_, bk in enumerate(banks):
            W("vector", "ts", cnt["ts"] - (1 - st_))
            if st_ == 0:
                W("vector", "dki%d" % (dt % 3), k_store_done[dt - 3])
                W("vector", "ss", k_sq_done[dt - 3])
            bank_free_v[bk] = v_op("tensor_copy",
                                   ktmp[dt % 3][:, st_ * 512:(st_ + 1) * 512],
                                   pqkv[bk][:, :], inc=True)
        vs_k_at[dt] = cnt["vs"]
        W("sync", "vs", vs_k_at[dt])
        k_store_done[dt] = dma("sync", "dki%d" % (dt % 3),
                               bass.AP(ka_in if dt < 4 else kb_in,
                                       (dt % 4) * 131072, [[1024, 128], [1, 1024]]),
                               ktmp[dt % 3][:, :])
        if dt == 3:
            for dkn in ("dki0", "dki1", "dki2"):
                W("gpsimd", dkn, cnt[dkn])
            cc_ka = trigger_ag(ka_in, ka_out)
            trigger_ag(wbt_in[2], wbt_out[2])
        if dt >= 2:
            W("scalar", "vs", vs_k_at[dt - 2])
            norms(ktmp[(dt - 2) % 3][:, :], 8 + dt - 2)
            k_sq_done[dt - 2] = cnt["ss"]
    for dtl in (NH - 2, NH - 1):
        W("scalar", "vs", vs_k_at[dtl])
        norms(ktmp[dtl % 3][:, :], 8 + dtl)
        k_sq_done[dtl] = cnt["ss"]
    for dkn in ("dki0", "dki1", "dki2"):
        W("gpsimd", dkn, cnt[dkn])
    trigger_ag(kb_in, kb_out)
    trigger_ag(wbt_in[3], wbt_out[3])
    v_store_done = cnt["dkvin"]
    vtail_ts = {}
    vfm_vs = {}
    vtm_copy_vs = 0

    def v_tail(dt):
        nonlocal v_store_done, vtm_copy_vs
        W("tensor", "vs", vfm_vs[dt])
        W("tensor", "vs", vtm_copy_vs)
        for tt in range(8):
            t_tr(pvt[:, tt * 128:(tt + 1) * 128],
                 vfm[dt % 2][:, tt * 128:(tt + 1) * 128],
                 ident[:, :], inc=(tt == 7))
        vtail_ts[dt] = cnt["ts"]
        W("vector", "ts", vtail_ts[dt])
        W("vector", "dkvin", v_store_done)
        vs_vt = v_op("tensor_copy", vtm[:, :], pvt[:, 0:1024], inc=True)
        vtm_copy_vs = vs_vt
        W("sync", "vs", vs_vt)
        dma("sync", "dkvin",
            bass.AP(va_in, dt * 128, [[1024, 128], [131072, 4], [1, 128]]),
            vtm[:, 0:512].rearrange("p (a b) -> p a b", b=128))
        v_store_done = dma("sync", "dkvin",
                           bass.AP(vb_in, dt * 128,
                                   [[1024, 128], [131072, 4], [1, 128]]),
                           vtm[:, 512:1024].rearrange("p (a b) -> p a b", b=128))

    for dt in range(NH):
        banks = qkv_block("v", dt)
        for st_, bk in enumerate(banks):
            W("vector", "ts", cnt["ts"] - (1 - st_))
            if st_ == 0 and dt >= 2:
                W("vector", "ts", vtail_ts[dt - 2])
            bank_free_v[bk] = v_op("tensor_copy",
                                   vfm[dt % 2][:, st_ * 512:(st_ + 1) * 512],
                                   pqkv[bk][:, :], inc=True)
        vfm_vs[dt] = cnt["vs"]
        if dt >= 1:
            v_tail(dt - 1)
    v_tail(NH - 1)
    q_ev_at = {}
    for dt in range(NH):
        banks = qkv_block("q", dt)
        for st_, bk in enumerate(banks):
            W("vector", "ts", cnt["ts"] - (1 - st_))
            bank_free_v[bk] = v_op(
                "tensor_copy", q_sb[:, dt * TOK + st_ * 512: dt * TOK + st_ * 512 + 512],
                pqkv[bk][:, :], inc=True)
        q_ev_at[dt] = cnt["vs"]
        if dt >= 2:
            W("scalar", "vs", q_ev_at[dt - 2])
            norms(q_sb[:, (dt - 2) * TOK: (dt - 1) * TOK], dt - 2)
    for dtl in (NH - 2, NH - 1):
        W("scalar", "vs", q_ev_at[dtl])
        norms(q_sb[:, dtl * TOK: (dtl + 1) * TOK], dtl)
    vs_q_done = cnt["vs"]
    vq("tensor_tensor", mq_sb[0:1, 0:8], mq_sb[0:1, 0:8], mq_sb[0:1, 8:16], ALU.mult)
    vs_bh1 = cnt["vs"]
    W("scalar", "vs", vs_bh1)
    s_act(mq_sb[0:1, 0:8], mq_sb[0:1, 0:8], AF.Sqrt, inc=True)
    ss_bh = cnt["ss"]
    W("vector", "ss", ss_bh)
    v_op("tensor_scalar", bhr_sb[0:1, 0:8], mq_sb[0:1, 0:8], ISQ, None, ALU.mult,
         inc=True)
    vs_bh2 = cnt["vs"]
    W("tensor", "vs", vs_bh2)
    t_mm(pbb[0:128, 0:8], onesr[:, :], bhr_sb[0:1, 0:8], start=True, stop=True,
         inc=True)
    ts_bb = cnt["ts"]
    W("vector", "ts", ts_bb)
    vs_bh = v_op("tensor_scalar", bh_sb[:, :], pbb[0:128, 0:8], -1.0, None, ALU.mult,
                 inc=True)
    W("gpsimd", "dkvin", cnt["dkvin"])
    trigger_ag(va_in, va_out)
    trigger_ag(vb_in, vb_out)

    p1ps.close()
    qkvps.close()
    p2ps = ExitStack()
    pbase = [PS(p2ps, "pbase%d" % i, [128, 512], F32) for i in range(4)]
    ph = PS(p2ps, "ph", [128, 1024], F32)
    pwx = PS(p2ps, "pwx", [128, 1024], F32)
    p1sc.close()
    p1wsc.close()
    kearly.close()
    kvearly = ExitStack()
    kbuf = [SB(kvearly, "kbuf0", [128, D], BF16)]
    vbuf = [SB(kvearly, "vbuf0", [128, D], BF16)]
    gssc = ExitStack()
    g_sb = SB(gssc, "g_sb", [128, 32 * TOK], BF16)
    p2sc = ExitStack()
    ball_sb = SB(p2sc, "ball_sb", [128, D], BF16, side="right")
    W("gpsimd", "ts", wblk_free_t[2])
    dball = dma("gpsimd", "dmisc", ball_sb[:, :].rearrange("p (a b) -> p a b", b=512),
                bass.AP(agw_out, AGW_BALL, [[512, 128], [AGW_BLK, 8], [1, 512]]))
    W("tensor", "vs", cnt["vs"])
    W("tensor", "vs", vs_w)
    for tt in range(8):
        t_tr(pwx[0:8, tt * 128:(tt + 1) * 128], wtm_sb[:, tt * 8:(tt + 1) * 8],
             identf[:, :], inc=(tt == 7))
    ts_wf = cnt["ts"]
    W("vector", "ts", ts_wf)
    vs_wfm = v_op("tensor_copy", wfm_bf[:, :], pwx[0:8, 0:1024], inc=True)
    W("tensor", "dlat", dlat)
    W("tensor", "dmisc", dball)
    for fc in range(32):
        for st_ in range(2):
            t_mm(ph[:, st_ * 512:(st_ + 1) * 512], latw[:, fc * 128:(fc + 1) * 128],
                 xn_sb[:, fc * TOK + st_ * 512: fc * TOK + st_ * 512 + 512],
                 start=(fc == 0), stop=(fc == 31), inc=(fc == 31 and st_ == 1))
    ts_h = cnt["ts"]
    W("tensor", "vs", vs_wfm)
    W("tensor", "dmisc", 32)
    for st_ in range(2):
        t_mm(pwx[:, st_ * 512:(st_ + 1) * 512], gmat_sb[:, :],
             wfm_bf[0:8, st_ * 512:(st_ + 1) * 512], start=True, stop=True,
             inc=(st_ == 1))
    ts_wx = cnt["ts"]
    W("vector", "ts", max(ts_h, ts_wx))
    v_op("tensor_copy", h_sb[:, :], ph[:, 0:1024], inc=True)
    vs_hw = vq("tensor_tensor", hw_sb[:, :], h_sb[:, :], pwx[:, 0:1024], ALU.mult)
    base_bank_free_s = [0, 0, 0, 0]
    wblk_free_t = wblk_free_t[:2]
    wblk_free_t[0] = max(wblk_free_t[0], cnt["ts"])
    wblk_free_t[1] = max(wblk_free_t[1], cnt["ts"])
    for ot in range(32):
        sl = widx % 2
        W("sync", "cc", 10)
        W("sync", "ts", wblk_free_t[sl])
        for j in range(4):
            dvw = dma("sync", "dw%d" % sl,
                      wgt[sl][:, j * 1024:(j + 1) * 1024].rearrange(
                          "p (a b) -> p a b", b=128),
                      bass.AP(wbt_out[j], (ot // 4) * HBLK + (ot % 4) * 128,
                              [[512, 128], [65536, 8], [1, 128]]))
        widx += 1
        W("tensor", "dw%d" % sl, dvw)
        if ot == 0:
            W("tensor", "vs", vs_hw)
            W("tensor", "dmisc", dball)
        bks = [2 * (ot % 2), 2 * (ot % 2) + 1]
        W("tensor", "ss", max(base_bank_free_s[bks[0]], base_bank_free_s[bks[1]]))
        for fc in range(32):
            for st_ in range(2):
                t_mm(pbase[bks[st_]][:, :], wgt[sl][:, fc * 128:(fc + 1) * 128],
                     xn_sb[:, fc * TOK + st_ * 512: fc * TOK + st_ * 512 + 512],
                     start=(fc == 0), stop=False)
        for st_ in range(2):
            t_mm(pbase[bks[st_]][:, :], ball_sb[:, ot * 128:(ot + 1) * 128],
                 hw_sb[:, st_ * 512:(st_ + 1) * 512], start=False, stop=True,
                 inc=(st_ == 1))
        wblk_free_t[sl] = cnt["ts"]
        if ot == 0:
            W("scalar", "dsm", 48)
            W("scalar", "dkvin", cnt["dkvin"])
            W("scalar", "dki0", cnt["dki0"])
            W("scalar", "dki1", cnt["dki1"])
            W("scalar", "dki2", cnt["dki2"])
        W("scalar", "ts", cnt["ts"])
        for st_ in range(2):
            bk = 2 * (ot % 2) + st_
            base_bank_free_s[bk] = s_act(
                g_sb[:, ot * TOK + st_ * 512: ot * TOK + st_ * 512 + 512],
                pbase[bk][:, :], AF.Silu, bias=b2_sb[:, ot:ot + 1], inc=True)
    g_done_s = cnt["ss"]

    p2ps.close()
    p2sc.close()
    xnsc.close()
    p3ps = ExitStack()
    pO = [PS(p3ps, "pO%d" % i, [128, 512], F32) for i in range(2)]
    pST = [PS(p3ps, "pST%d" % i, [128, 1024], F32) for i in range(2)]
    pdr = PS(p3ps, "pdr", [128, 512], F32)
    pot1 = PS(p3ps, "pot1", [128, 1024], BF16)
    osc = ExitStack()
    o_sb = SB(osc, "o_sb", [128, NH * TOK], BF16)
    p3sc = ExitStack()
    kbuf.append(SB(p3sc, "kbuf1", [128, D], BF16))
    vbuf.append(SB(p3sc, "vbuf1", [128, D], BF16))
    pt_sb = [SB(p3sc, "pt0", [128, 1024], BF16), SB(p3sc, "pt1", [128, 1024], BF16)]
    dacc = SB(p3sc, "dacc", [128, 1024], F32)
    otm = SB(p3sc, "otm", [128, 1024], BF16)

    def emit_kv_load(idx, slot_free_ts):
        h, half = divmod(idx, 2)
        sl = idx % 2
        W("sync", "cc", 7 if h < 4 else 9)
        W("sync", "ts", slot_free_ts)
        kt_src = ka_out if h < 4 else kb_out
        dma("sync", "dkv%d" % sl,
            kbuf[sl][:, :].rearrange("p (a b) -> p a b", b=1024),
            bass.AP(kt_src, 4 * half * HBLK + (h % 4) * 131072,
                    [[1024, 128], [HBLK, 4], [1, 1024]]))
        W("sync", "cc", 12)
        last = 0
        for r4 in range(4):
            dma("sync", "dkv%d" % sl,
                vbuf[sl][:, r4 * 1024: r4 * 1024 + 512].rearrange(
                    "p (a b) -> p a b", b=128),
                bass.AP(va_out, (4 * half + r4) * HBLK + h * 128,
                        [[1024, 128], [131072, 4], [1, 128]]))
            last = dma("sync", "dkv%d" % sl,
                       vbuf[sl][:, r4 * 1024 + 512: (r4 + 1) * 1024].rearrange(
                           "p (a b) -> p a b", b=128),
                       bass.AP(vb_out, (4 * half + r4) * HBLK + h * 128,
                               [[1024, 128], [131072, 4], [1, 128]]))
        return last

    ts_p2_done = cnt["ts"]
    kv_dma = {0: emit_kv_load(0, 0), 1: emit_kv_load(1, ts_p2_done)}
    scores_ts = {}
    exp_ss = {}
    pv_ts = {-2: 0, -1: 0}
    dadd_vs = {-2: 0, -1: 0}
    o_evac_v = 0
    dred_vs = {-1: p0_v_done}
    ts_transp = {-1: 0}
    pot1_free_v = [0]
    otm_free_t = [0]
    pending_dance = []

    def emit_dance(h):
        W("tensor", "vs", o_evac_v)
        W("tensor", "vs", pot1_free_v[0])
        for tt in range(8):
            t_tr(pot1[:, tt * 128:(tt + 1) * 128],
                 o_sb[:, h * TOK + tt * 128: h * TOK + tt * 128 + 128],
                 ident[:, :], inc=(tt == 7))
        ts_o1 = cnt["ts"]
        W("vector", "ts", ts_o1)
        W("vector", "vs", dred_vs[h])
        W("vector", "ts", otm_free_t[0])
        for tt in range(8):
            v_op("tensor_scalar", otm[:, tt * 128:(tt + 1) * 128],
                 pot1[:, tt * 128:(tt + 1) * 128],
                 dtm2_sb[:, h * 8 + tt: h * 8 + tt + 1], None, ALU.mult,
                 inc=(tt == 7))
        vs_otm = cnt["vs"]
        W("tensor", "vs", vs_otm)
        for tt in range(8):
            t_tr(pot1[:, tt * 128:(tt + 1) * 128],
                 otm[:, tt * 128:(tt + 1) * 128], ident[:, :], inc=(tt == 7))
        otm_free_t[0] = cnt["ts"]
        W("vector", "ts", otm_free_t[0])
        pot1_free_v[0] = v_op("tensor_copy", o_sb[:, h * TOK:(h + 1) * TOK],
                              pot1[:, 0:1024], inc=True)

    def emit_scores(g):
        h, kt = divmod(g, 64)
        idx = 2 * h + (kt // 32)
        sl = idx % 2
        kt32 = kt % 32
        stb = g % 2
        W("tensor", "dkv%d" % sl, kv_dma[idx])
        if g == 0:
            W("tensor", "vs", vs_q_done)
            W("tensor", "vs", vs_bh)
            W("tensor", "vs", vs_hw)
            W("tensor", "ss", g_done_s)
        W("tensor", "ss", exp_ss.get(g - 2, 0))
        if kt == 0:
            W("tensor", "vs", o_evac_v)
        for st_ in range(2):
            t_mm(pST[stb][:, st_ * 512:(st_ + 1) * 512],
                 kbuf[sl][:, kt32 * 128:(kt32 + 1) * 128],
                 q_sb[:, h * TOK + st_ * 512: h * TOK + st_ * 512 + 512],
                 start=True, stop=True, inc=(st_ == 1))
        scores_ts[g] = cnt["ts"]

    def emit_exp(g):
        h = g // 64
        stb = g % 2
        W("scalar", "ts", scores_ts[g])
        W("scalar", "ts", pv_ts.get(g - 2, 0))
        W("scalar", "vs", dadd_vs.get(g - 2, 0))
        s_act(pt_sb[stb][:, :], pST[stb][:, 0:1024], AF.Exp,
              bias=bh_sb[:, h:h + 1], scale=ISQ, inc=True)
        exp_ss[g] = cnt["ss"]

    def emit_dadd(g):
        kt = g % 64
        stb = g % 2
        W("vector", "ss", exp_ss[g])
        if kt == 0:
            W("vector", "ts", ts_transp.get(g // 64 - 1, 0))
            v_op("tensor_copy", dacc[:, :], pt_sb[stb][:, :], inc=True)
        else:
            W("vector", "vs", cnt["vs"])
            v_op("tensor_add", dacc[:, :], dacc[:, :], pt_sb[stb][:, :], inc=True)
        dadd_vs[g] = cnt["vs"]

    def emit_pv(g):
        h, kt = divmod(g, 64)
        idx = 2 * h + (kt // 32)
        sl = idx % 2
        kt32 = kt % 32
        stb = g % 2
        W("tensor", "ss", exp_ss[g])
        for st_ in range(2):
            t_mm(pO[st_][:, :], vbuf[sl][:, kt32 * 128:(kt32 + 1) * 128],
                 pt_sb[stb][:, st_ * 512:(st_ + 1) * 512],
                 start=(kt == 0), stop=(kt == 63), inc=(st_ == 1))
        pv_ts[g] = cnt["ts"]

    emit_scores(0)
    for g in range(NH * 64):
        h, kt = divmod(g, 64)
        if g + 1 < NH * 64 and (g + 1) % 64 != 0:
            emit_scores(g + 1)
        emit_exp(g)
        emit_dadd(g)
        emit_pv(g)
        if kt == 4 and pending_dance:
            emit_dance(pending_dance.pop(0))
        if kt == 31:
            idx = 2 * h + 0
            if idx + 2 < 16:
                kv_dma[idx + 2] = emit_kv_load(idx + 2, cnt["ts"])
        if kt == 63:
            idx = 2 * h + 1
            if idx + 2 < 16:
                kv_dma[idx + 2] = emit_kv_load(idx + 2, cnt["ts"])
            W("vector", "ts", pv_ts[g])
            for st_ in range(2):
                v_op("tensor_copy",
                     o_sb[:, h * TOK + st_ * 512: h * TOK + st_ * 512 + 512],
                     pO[st_][:, :], inc=(st_ == 1))
            o_evac_v = cnt["vs"]
            for half4 in range(2):
                W("tensor", "vs", dadd_vs[g] if half4 == 0 else cnt["vs"])
                W("tensor", "vs", dred_vs.get(h - 1, 0) if half4 == 0 else cnt["vs"])
                for tt4 in range(4):
                    tt = half4 * 4 + tt4
                    t_tr(pdr[:, tt4 * 128:(tt4 + 1) * 128],
                         dacc[:, tt * 128:(tt + 1) * 128], identf[:, :],
                         inc=(tt4 == 3))
                W("vector", "ts", cnt["ts"])
                for tt4 in range(4):
                    tt = half4 * 4 + tt4
                    v_op("reduce_sum", dtm_sb[:, h * 8 + tt: h * 8 + tt + 1],
                         pdr[:, tt4 * 128:(tt4 + 1) * 128], axis=AX.X,
                         inc=(tt4 == 3))
            ts_transp[h] = cnt["ts"]
            vq("reciprocal", dtm2_sb[:, h * 8:(h + 1) * 8],
               dtm_sb[:, h * 8:(h + 1) * 8])
            dred_vs[h] = cnt["vs"]
            if g + 1 < NH * 64:
                emit_scores(g + 1)
            pending_dance.append(h)

    while pending_dance:
        emit_dance(pending_dance.pop(0))
    p3ps.close()
    p3tps = ExitStack()
    o_norm_v = cnt["vs"]

    p3tps.close()
    p3sc.close()
    p4ps = ExitStack()
    ps4 = [PS(p4ps, "ps4_%d" % i, [128, 512], F32) for i in range(4)]
    psg = [PS(p4ps, "psg0", [128, 1024], BF16), PS(p4ps, "psg1", [128, 1024], BF16)]
    latsc.close()
    qsc.close()
    p4sc = ExitStack()
    xcol = [SB(p4sc, "xcol%d" % i, [128, 1024], F32) for i in range(4)]
    outp = [SB(p4sc, "outp%d" % i, [128, 1024], F32) for i in range(4)]


    s4_bank_free_v = [0, 0, 0, 0]
    psg_free_v = [0, 0]
    sgv_free_t = [0, 0]
    vs_sg = {}
    out_store_done = {-4: 0, -3: 0, -2: 0, -1: 0}
    xcol_d = {}
    mm_ts = {}
    for par0 in range(4):
        W("scalar", "vs", o_norm_v)
        xcol_d[par0] = dma("scalar", "dx%d" % par0,
                           xcol[par0][:, :].rearrange("p (a b) -> p a b", b=128),
                           bass.AP(x_sl, par0 * 128,
                                   [[4096, 128], [524288, 8], [1, 128]]))

    ts_p4_wgt_free = max(wblk_free_t[0], wblk_free_t[1])
    dwo_done = {}

    def wo_slot(q4):
        return wgt[q4 % 2][:, (q4 // 2) * 1024:(q4 // 2) * 1024 + 1024]

    def wo_load(dmt, gate_ts):
        q4 = dmt % 4
        W("gpsimd", "ts", gate_ts)
        dwo_done[dmt] = dma("gpsimd", "dwo%d" % q4,
                            wo_slot(q4).rearrange("p (a b) -> p a b", b=128),
                            bass.AP(agw_out, (dmt // 4) * AGW_BLK + AGW_WO + (dmt % 4) * 128,
                                    [[512, 128], [65536, 8], [1, 128]]))

    for dmt0 in range(4):
        wo_load(dmt0, ts_p4_wgt_free)

    def p4_mms(dmt):
        q4 = dmt % 4
        W("tensor", "dwo%d" % q4, dwo_done[dmt])
        if dmt == 0:
            W("tensor", "vs", o_norm_v)
        bks4 = [2 * (dmt % 2), 2 * (dmt % 2) + 1]
        W("tensor", "vs", max(s4_bank_free_v[bks4[0]], s4_bank_free_v[bks4[1]]))
        wsl = wo_slot(q4)
        for dkc in range(8):
            for st_ in range(2):
                t_mm(ps4[bks4[st_]][:, :], wsl[:, dkc * 128:(dkc + 1) * 128],
                     o_sb[:, dkc * TOK + st_ * 512: dkc * TOK + st_ * 512 + 512],
                     start=(dkc == 0), stop=(dkc == 7),
                     inc=(dkc == 7 and st_ == 1))
        mm_ts[dmt] = cnt["ts"]
        if dmt + 4 < 32:
            wo_load(dmt + 4, mm_ts[dmt])

    def p4_evac(dmt):
        par = dmt % 2
        if dmt == 0:
            W("vector", "ss", g_done_s)
        W("vector", "ts", sgv_free_t[par])
        W("vector", "ts", mm_ts[dmt])
        for st_ in range(2):
            bk = 2 * (dmt % 2) + st_
            s4_bank_free_v[bk] = v_op(
                "tensor_tensor",
                wgt[par][:, 2048 + st_ * 512: 2048 + st_ * 512 + 512],
                ps4[bk][:, :],
                g_sb[:, dmt * TOK + st_ * 512: dmt * TOK + st_ * 512 + 512],
                ALU.mult, inc=True)
        vs_sg[dmt] = cnt["vs"]

    def p4_fin(dmt):
        par = dmt % 2
        q4 = dmt % 4
        W("tensor", "vs", vs_sg[dmt])
        W("tensor", "vs", psg_free_v[par])
        for tt in range(8):
            t_tr(psg[par][:, tt * 128:(tt + 1) * 128],
                 wgt[par][:, 2048 + tt * 128: 2048 + tt * 128 + 128], ident[:, :],
                 inc=(tt == 7))
        sgv_free_t[par] = cnt["ts"]
        W("vector", "ts", sgv_free_t[par])
        W("vector", "dx%d" % q4, xcol_d[dmt])
        W("vector", "st%d" % q4, out_store_done[dmt - 4])
        psg_free_v[par] = v_op("scalar_tensor_tensor", outp[q4][:, :],
                               psg[par][:, 0:1024], 1.0, xcol[q4][:, :],
                               ALU.mult, ALU.add, inc=True)
        W("sync", "vs", psg_free_v[par])
        out_store_done[dmt] = dma("sync", "st%d" % q4,
                                  bass.AP(out_d, dmt * 128,
                                          [[4096, 128], [524288, 8], [1, 128]]),
                                  outp[q4][:, :].rearrange("p (a b) -> p a b", b=128))
        if dmt + 4 < 32:
            W("scalar", "vs", psg_free_v[par])
            xcol_d[dmt + 4] = dma("scalar", "dx%d" % q4,
                                  xcol[q4][:, :].rearrange("p (a b) -> p a b", b=128),
                                  bass.AP(x_sl, (dmt + 4) * 128,
                                          [[4096, 128], [524288, 8], [1, 128]]))

    p4_mms(0)
    p4_mms(1)
    for dmt in range(32):
        p4_evac(dmt)
        if dmt % 2 == 1:
            p4_fin(dmt - 1)
            p4_fin(dmt)
        if dmt + 2 < 32:
            p4_mms(dmt + 2)
    for stn in ("st0", "st1", "st2", "st3"):
        W("sync", stn, cnt[stn])

    p4ps.close()
    p4sc.close()
    osc.close()
    gssc.close()
    kvearly.close()
    wqesc_dummy = None
    gsc.close()

    with nc.Block() as block:
        @block.sync
        def _(e):
            for f in prog["sync"]:
                f(e)

        @block.gpsimd
        def _(e):
            for f in prog["gpsimd"]:
                f(e)

        @block.tensor
        def _(e):
            for f in prog["tensor"]:
                f(e)

        @block.vector
        def _(e):
            for f in prog["vector"]:
                f(e)

        @block.scalar
        def _(e):
            for f in prog["scalar"]:
                f(e)

    return nc


_NC_CACHE = []


def kernel(L0_attn_x, L0_attn_wq, L0_attn_wk, L0_attn_wv, L0_attn_wo,
           L0_ffn_W_base, L0_ffn_b_base, L0_ffn_W_router, L0_ffn_lora_A,
           L0_ffn_lora_B):
    x = np.ascontiguousarray(np.asarray(L0_attn_x, dtype=np.float32)[0])
    wq = np.asarray(L0_attn_wq, dtype=np.float32)
    wk = np.asarray(L0_attn_wk, dtype=np.float32)
    wv = np.asarray(L0_attn_wv, dtype=np.float32)
    wo = np.asarray(L0_attn_wo, dtype=np.float32)
    wb = np.asarray(L0_ffn_W_base, dtype=np.float32)
    bb = np.asarray(L0_ffn_b_base, dtype=np.float32)
    wr = np.asarray(L0_ffn_W_router, dtype=np.float32)
    la = np.asarray(L0_ffn_lora_A, dtype=np.float32)
    lb = np.asarray(L0_ffn_lora_B, dtype=np.float32)

    ball = np.ascontiguousarray(lb.transpose(0, 2, 1).reshape(ER, D))
    lat = np.ascontiguousarray(la.reshape(ER, D).T)
    b2 = np.ascontiguousarray(bb.reshape(32, 128).T)
    gmat = np.ascontiguousarray(
        (np.kron(np.eye(8), np.ones((1, 16))) * 2.0).astype(np.float32))
    id_bf = np.eye(128, dtype=ml_dtypes.bfloat16)
    id_f32 = np.eye(128, dtype=np.float32)

    if not _NC_CACHE:
        _NC_CACHE.append(build_program())
    nc = _NC_CACHE[0]

    in_maps = []
    for c in range(NC):
        in_maps.append({
            "x_sl": np.ascontiguousarray(x[c * TOK:(c + 1) * TOK]),
            "wq_sl": np.ascontiguousarray(wq[:, c * 512:(c + 1) * 512]),
            "wk_sl": np.ascontiguousarray(wk[:, c * 128:(c + 1) * 128]),
            "wv_sl": np.ascontiguousarray(wv[:, c * 128:(c + 1) * 128]),
            "wo_sl": np.ascontiguousarray(wo[:, c * 512:(c + 1) * 512]),
            "wbt_sl": np.ascontiguousarray(wb[c * 512:(c + 1) * 512, :].T),
            "ball_sl": np.ascontiguousarray(ball[:, c * 512:(c + 1) * 512]),
            "lat_sl": np.ascontiguousarray(lat[c * 512:(c + 1) * 512, :]),
            "wr_in": wr,
            "b2_in": b2,
            "gmat_in": gmat,
            "id_bf": id_bf,
            "id_f32": id_f32,
        })
    res = run_bass_kernel_spmd(nc, in_maps, core_ids=list(range(NC)))
    out = np.concatenate([np.asarray(res.results[c]["out"]) for c in range(NC)],
                         axis=0)
    return out.reshape(1, S, D).astype(np.float32)


# version 40
# speedup vs baseline: 1.1407x; 1.0027x over previous
import math
import numpy as np
import ml_dtypes
import concourse.bass as bass
import concourse.mybir as mybir
from concourse.bass_utils import run_bass_kernel_spmd

F32 = mybir.dt.float32
BF16 = mybir.dt.bfloat16
AF = mybir.ActivationFunctionType
ALU = mybir.AluOpType
AX = mybir.AxisListType

NC = 8
TOK = 1024
S = 8192
D = 4096
NH = 8
HD = 128
DKV = 1024
ER = 128
EPS = 1e-5
ISQ = 1.0 / math.sqrt(128.0)

AGQ_BLK = 4096 * 128
AGW_WO, AGW_BALL, AGW_LAT = 0, 524288, 589824
AGW_BLK = 655360
HBLK = 524288


ENGS = ("sync", "gpsimd", "tensor", "vector", "scalar")


def build_program():
    nc = bass.Bass("TRN2", target_bir_lowering=False)

    x_sl = nc.dram_tensor("x_sl", [TOK, D], F32, kind="ExternalInput")
    wq_sl = nc.dram_tensor("wq_sl", [D, 512], F32, kind="ExternalInput")
    wk_sl = nc.dram_tensor("wk_sl", [D, 128], F32, kind="ExternalInput")
    wv_sl = nc.dram_tensor("wv_sl", [D, 128], F32, kind="ExternalInput")
    wo_sl = nc.dram_tensor("wo_sl", [DKV, 512], F32, kind="ExternalInput")
    wbt_sl = nc.dram_tensor("wbt_sl", [D, 512], F32, kind="ExternalInput")
    ball_sl = nc.dram_tensor("ball_sl", [ER, 512], F32, kind="ExternalInput")
    lat_sl = nc.dram_tensor("lat_sl", [512, ER], F32, kind="ExternalInput")
    wr_in = nc.dram_tensor("wr_in", [8, D], F32, kind="ExternalInput")
    b2_in = nc.dram_tensor("b2_in", [128, 32], F32, kind="ExternalInput")
    gmat_in = nc.dram_tensor("gmat_in", [8, 128], F32, kind="ExternalInput")
    id_bf = nc.dram_tensor("id_bf", [128, 128], BF16, kind="ExternalInput")
    id_f32 = nc.dram_tensor("id_f32", [128, 128], F32, kind="ExternalInput")
    out_d = nc.dram_tensor("out", [TOK, D], F32, kind="ExternalOutput")

    awk_in = nc.dram_tensor("awk_in", [HBLK], BF16)
    awk_out = nc.dram_tensor("awk_out", [NC * HBLK], BF16)
    awv_in = nc.dram_tensor("awv_in", [HBLK], BF16)
    awv_out = nc.dram_tensor("awv_out", [NC * HBLK], BF16)
    agq_in = nc.dram_tensor("agq_in", [AGQ_BLK], BF16)
    agq_out = nc.dram_tensor("agq_out", [NC * AGQ_BLK], BF16)
    agw_in = nc.dram_tensor("agw_in", [AGW_BLK], BF16)
    agw_out = nc.dram_tensor("agw_out", [NC * AGW_BLK], BF16)
    wbt_in = [nc.dram_tensor("wbt%d_in" % j, [HBLK], BF16) for j in range(4)]
    wbt_out = [nc.dram_tensor("wbt%d_out" % j, [NC * HBLK], BF16) for j in range(4)]
    ka_in = nc.dram_tensor("ka_in", [HBLK], BF16)
    ka_out = nc.dram_tensor("ka_out", [NC * HBLK], BF16)
    kb_in = nc.dram_tensor("kb_in", [HBLK], BF16)
    kb_out = nc.dram_tensor("kb_out", [NC * HBLK], BF16)
    va_in = nc.dram_tensor("va_in", [HBLK], BF16)
    va_out = nc.dram_tensor("va_out", [NC * HBLK], BF16)
    vb_in = nc.dram_tensor("vb_in", [HBLK], BF16)
    vb_out = nc.dram_tensor("vb_out", [NC * HBLK], BF16)

    from contextlib import ExitStack
    gsc = ExitStack()
    wqesc = ExitStack()

    def SB(scope, name, shape, dt, side="left"):
        return scope.enter_context(nc.sbuf_tensor(name, shape, dt, side=side))

    def PS(scope, name, shape, dt):
        return scope.enter_context(nc.psum_tensor(name, shape, dt))

    def SEM(name):
        return gsc.enter_context(nc.semaphore(name))

    sems = {}
    cnt = {}
    for nm in ("dx0", "dx1", "dx2", "dx3", "dw0", "dw1", "dw2", "dw3", "dkv0", "dkv1", "dqe", "dmisc",
               "dsm", "dag", "dkvin", "dki0", "dki1", "dki2", "dwo0", "dwo1", "dwo2", "dwo3", "dsg0", "dsg1", "st0", "st1", "st2", "st3", "ts", "vs",
               "ss", "cc", "gp", "dw4", "dw5", "dw6", "dw7",
               "dh0", "dh1", "dh2", "dh3", "dh4", "dh5", "dh6", "dh7", "dlat",
               "dkh0", "dkh1", "dkh2", "dkh3", "dkh4", "dkh5"):
        sems[nm] = SEM(nm)
        cnt[nm] = 0

    prog = {e: [] for e in ENGS}

    def E(eng, fn):
        prog[eng].append(fn)

    def W(eng, sem_nm, val):
        if val <= 0:
            return
        h = sems[sem_nm]
        E(eng, lambda e, h=h, v=val: e.wait_ge(h, v))

    def bump(sem_nm, n=1):
        cnt[sem_nm] += n
        return cnt[sem_nm]

    def dma(eng, sem_nm, out_ap, in_ap, accum=False):
        h = sems[sem_nm]
        if accum:
            E(eng, lambda e, o=out_ap, i=in_ap, h=h: e.dma_start(
                out=o, in_=i, accum_op=ALU.add).then_inc(h, 16))
        else:
            E(eng, lambda e, o=out_ap, i=in_ap, h=h: e.dma_start(
                out=o, in_=i).then_inc(h, 16))
        return bump(sem_nm, 16)

    def t_mm(out_ap, l_ap, r_ap, start, stop, inc=False):
        if inc:
            h = sems["ts"]
            E("tensor", lambda e, o=out_ap, l=l_ap, r=r_ap, a=start, b=stop, h=h:
              e.matmul(o, l, r, start=a, stop=b).then_inc(h, 1))
            return bump("ts")
        E("tensor", lambda e, o=out_ap, l=l_ap, r=r_ap, a=start, b=stop:
          e.matmul(o, l, r, start=a, stop=b))
        return cnt["ts"]

    def t_tr(out_ap, in_ap, id_ap, inc=False):
        if inc:
            h = sems["ts"]
            E("tensor", lambda e, o=out_ap, i=in_ap, d=id_ap, h=h:
              e.transpose(o, i, d).then_inc(h, 1))
            return bump("ts")
        E("tensor", lambda e, o=out_ap, i=in_ap, d=id_ap: e.transpose(o, i, d))
        return cnt["ts"]

    def v_op(meth, *aps, inc=False, **kw):
        if inc:
            h = sems["vs"]
            E("vector", lambda e, m=meth, a=aps, k=kw, h=h:
              getattr(e, m)(*a, **k).then_inc(h, 1))
            return bump("vs")
        E("vector", lambda e, m=meth, a=aps, k=kw: getattr(e, m)(*a, **k))
        return cnt["vs"]

    def vq(meth, *aps, **kw):
        W("vector", "vs", cnt["vs"])
        return v_op(meth, *aps, inc=True, **kw)

    def s_act(out_ap, in_ap, func, bias=0.0, scale=1.0, accum=None, inc=False):
        kw = dict(bias=bias, scale=scale)
        if accum is not None:
            kw["accum_out"] = accum
        if inc:
            h = sems["ss"]
            E("scalar", lambda e, o=out_ap, i=in_ap, f=func, k=kw, h=h:
              e.activation(o, i, f, **k).then_inc(h, 1))
            return bump("ss")
        E("scalar", lambda e, o=out_ap, i=in_ap, f=func, k=kw: e.activation(o, i, f, **k))
        return cnt["ss"]

    qsc = ExitStack()
    q_sb = SB(qsc, "q_sb", [128, NH * TOK], BF16, side="right")
    ident = SB(gsc, "ident", [128, 128], BF16)
    identf = SB(gsc, "identf", [128, 128], F32)
    ones_c = SB(gsc, "ones_c", [128, 1], BF16)
    onesr = SB(gsc, "onesr", [1, 128], BF16)
    gmat_sb = SB(gsc, "gmat_sb", [8, 128], BF16)
    b2_sb = SB(gsc, "b2_sb", [128, 32], F32)
    wrt_sb = SB(gsc, "wrt_sb", [128, 256], BF16)
    ltm_sb = SB(gsc, "ltm_sb", [128, 64], F32)
    wtm_sb = SB(gsc, "wtm_sb", [128, 64], F32)
    wfm_bf = SB(gsc, "wfm_bf", [8, TOK], BF16)
    rt0 = SB(gsc, "rt0", [128, 8], F32)
    rt1 = SB(gsc, "rt1", [128, 8], F32)
    rt2 = SB(gsc, "rt2", [128, 8], F32)
    rs0 = SB(gsc, "rs0", [128, 1], F32)
    rs1 = SB(gsc, "rs1", [128, 1], F32)
    stats = SB(gsc, "stats", [128, 16], F32)
    mq_sb = SB(gsc, "mq_sb", [1, 16], F32)
    bh_sb = SB(gsc, "bh_sb", [128, 8], F32)
    bhr_sb = SB(gsc, "bhr_sb", [1, 8], BF16)
    dtm_sb = SB(gsc, "dtm_sb", [128, 64], F32)
    dtm2_sb = SB(gsc, "dtm2_sb", [128, 64], F32)
    wgt = [SB(gsc, "wgt0", [128, D], BF16), SB(gsc, "wgt1", [128, D], BF16)]
    h_sb = SB(gsc, "h_sb", [128, TOK], F32)
    hw_sb = SB(gsc, "hw_sb", [128, TOK], BF16)
    kearly = ExitStack()
    ktmp = [SB(kearly, "ktmp%d" % i, [128, 1024], BF16) for i in range(3)]
    latsc = ExitStack()
    latw = SB(latsc, "latw", [128, D], BF16, side="right")
    p0sc = ExitStack()
    xtsc = ExitStack()
    xt = [SB(xtsc, "xt%d" % i, [128, D], F32) for i in range(3)]
    wr_sb = xt[2][0:8, 0:2048].bitcast(BF16)
    lg_sb = h_sb[0:8, :]
    wqe_acc = SB(wqesc, "wqe_acc", [128, D], F32)
    wqe_bf = SB(wqesc, "wqe_bf", [128, D], BF16)

    dma("gpsimd", "dmisc", gmat_sb[:, :], gmat_in.ap())
    rg = [list(range(NC))]

    def trigger_ag(in_t, out_t):
        E("gpsimd", lambda e, a=in_t, b=out_t: e.collective_compute(
            "AllGather", ALU.bypass, replica_groups=rg,
            ins=[a.ap().opt()], outs=[b.ap().opt()]).then_inc(sems["cc"], 1))
        return bump("cc")

    dma("gpsimd", "dag", bass.AP(awk_in, 0, [[4096, 128], [128, 32], [1, 128]]),
        bass.AP(wk_sl, 0, [[128, 128], [16384, 32], [1, 128]]))
    W("gpsimd", "dag", cnt["dag"])
    trigger_ag(awk_in, awk_out)
    dma("gpsimd", "dag", bass.AP(awv_in, 0, [[4096, 128], [128, 32], [1, 128]]),
        bass.AP(wv_sl, 0, [[128, 128], [16384, 32], [1, 128]]))
    W("gpsimd", "dag", cnt["dag"])
    trigger_ag(awv_in, awv_out)
    for j in range(4):
        v = dma("gpsimd", "dqe",
                wqe_acc[:, :].rearrange("p (a b) -> p a b", b=128),
                bass.AP(wq_sl, j * 128, [[512, 128], [65536, 32], [1, 128]]),
                accum=(j > 0))
        W("gpsimd", "dqe", v)
    E("gpsimd", lambda e: e.tensor_copy(wqe_bf[:, :], wqe_acc[:, :]).then_inc(
        sems["gp"], 1))
    bump("gp")
    W("gpsimd", "gp", cnt["gp"])
    dma("gpsimd", "dag", bass.AP(agq_in, 0, [[4096, 128], [128, 32], [1, 128]]),
        wqe_bf[:, :].rearrange("p (a b) -> p a b", b=128))
    W("gpsimd", "dag", cnt["dag"])
    trigger_ag(agq_in, agq_out)
    for o4 in range(4):
        dma("gpsimd", "dag",
            bass.AP(agw_in, AGW_WO + o4 * 131072, [[1024, 128], [128, 8], [1, 128]]),
            bass.AP(wo_sl, o4 * 128, [[512, 128], [65536, 8], [1, 128]]))
    dma("gpsimd", "dag", bass.AP(agw_in, AGW_BALL, [[512, 128], [1, 512]]), ball_sl.ap())
    dma("gpsimd", "dag", bass.AP(agw_in, AGW_LAT, [[128, 512], [1, 128]]), lat_sl.ap())
    W("gpsimd", "dag", cnt["dag"])
    trigger_ag(agw_in, agw_out)
    for j in range(4):
        for o4 in range(4):
            dma("gpsimd", "dag",
                bass.AP(wbt_in[j], o4 * 131072, [[1024, 128], [128, 8], [1, 128]]),
                bass.AP(wbt_sl, j * 524288 + o4 * 128,
                        [[512, 128], [65536, 8], [1, 128]]))
    W("gpsimd", "dag", cnt["dag"])
    for j in (0, 1):
        trigger_ag(wbt_in[j], wbt_out[j])
    dma("gpsimd", "dmisc", wr_sb[:, :], wr_in.ap())
    W("gpsimd", "cc", 4)
    for r in range(NC):
        dlat = dma("gpsimd", "dlat",
                   latw[:, r * 512:(r + 1) * 512].rearrange("p (a b) -> p a b", b=128),
                   bass.AP(agw_out, r * AGW_BLK + AGW_LAT,
                           [[128, 128], [16384, 4], [1, 128]]))

    xnsc = ExitStack()
    xn_sb = SB(xnsc, "xn_sb", [128, 32 * TOK], BF16, side="right")
    hi_sb = SB(p0sc, "hi_sb", [128, D], BF16)
    qkvps = ExitStack()
    pqkv = [PS(qkvps, "pqkv%d" % i, [128, 512], F32) for i in range(4)]
    p0ps = ExitStack()
    ptr = [PS(p0ps, "ptr0", [128, 1024], BF16), PS(p0ps, "ptr1", [128, 1024], BF16)]
    plog = PS(p0ps, "plog", [128, 1024], F32)

    dma("sync", "dsm", ident[:, :], id_bf.ap())
    dma("sync", "dsm", identf[:, :], id_f32.ap())
    dma("sync", "dsm", b2_sb[:, :], b2_in.ap())
    v_op("memset", ones_c[:, :], 1.0)
    v_op("memset", onesr[:, :], 1.0, inc=True)
    vs_memset = cnt["vs"]

    W("tensor", "dmisc", 32)
    W("tensor", "dsm", 48)
    for fc in range(32):
        t_tr(ptr[0][:, fc * 8:(fc + 1) * 8], wr_sb[:, fc * 128:(fc + 1) * 128],
             ident[0:8, 0:8], inc=(fc == 31))
    ts_wrt = cnt["ts"]
    W("vector", "ts", ts_wrt)
    vs_wrt = v_op("tensor_copy", wrt_sb[:, :], ptr[0][:, 0:256], inc=True)

    pdvK = [0, 0]
    ev3 = {}
    strip_ts = {}
    k_store_done = {-3: 0, -2: 0, -1: 0}
    k_sq_done = {-3: 0, -2: 0, -1: 0}
    vs_k_at = {}
    bank_free_v = [0] * 4

    def emit_kevac(dt):
        for st_, bk in enumerate([(2 * dt) % 4, (2 * dt + 1) % 4]):
            W("vector", "ts", strip_ts[(dt, st_ * 2 + 1)])
            bank_free_v[bk] = v_op("tensor_copy",
                                   ktmp[dt % 3][:, st_ * 512:(st_ + 1) * 512],
                                   pqkv[bk][:, :], inc=True)
        vs_k_at[dt] = cnt["vs"]
        W("sync", "vs", vs_k_at[dt])
        k_store_done[dt] = dma("sync", "dki%d" % (dt % 3),
                               bass.AP(ka_in, (dt % 4) * 131072,
                                       [[1024, 128], [1, 1024]]),
                               ktmp[dt % 3][:, :])

    def emit_kstrips(pr):
        W("tensor", "vs", ev3[2 * pr + 1])
        for ki in range(2):
            if pr == 0:
                W("tensor", "dw%d" % ki, pdvK[ki])
            bank = ki * 2 + pr // 2
            col = (pr % 2) * 256
            for fc in range(32):
                t_mm(pqkv[bank][:, col:col + 256],
                     wgt[ki][:, fc * 128:(fc + 1) * 128],
                     xn_sb[:, fc * TOK + pr * 256: fc * TOK + pr * 256 + 256],
                     start=(fc == 0), stop=(fc == 31), inc=(fc == 31))
            strip_ts[(ki, pr)] = cnt["ts"]

    wgt4 = list(wgt)
    xt_free_s = [0, 0, 0]
    xt_free_v = [0, 0, 0]
    hi_free_t = 0
    tr_bank_free_v = [vs_wrt, vs_wrt]
    xn_done_v = 0
    for tt in range(8):
        sl = tt % 3
        W("sync", "ss", xt_free_s[sl])
        W("sync", "vs", xt_free_v[sl])
        if tt == 2:
            W("sync", "ts", ts_wrt)
        dv = dma("sync", "dx%d" % sl, xt[sl][:, :],
                 x_sl.ap()[tt * 128:(tt + 1) * 128, :])
        if tt == 1:
            for pi in range(2):
                W("scalar", "cc", 1)
                pdvK[pi] = dma("scalar", "dw%d" % pi, wgt[pi][:, :],
                               bass.AP(awk_out, pi * HBLK,
                                       [[4096, 128], [1, 4096]]))
        W("vector", "dx%d" % sl, dv)
        if tt == 0:
            W("vector", "gp", 1)
        v_op("tensor_scalar", wqe_acc[:, :], xt[sl][:, :], 1.0, None, ALU.mult,
             op1=ALU.add, accum_out=stats[:, 0:1], inc=True)
        W("scalar", "dx%d" % sl, dv)
        W("scalar", "ts", hi_free_t)
        s_act(hi_sb[:, :], xt[sl][:, :], AF.Square, accum=stats[:, 1:2], inc=True)
        ss_sq = cnt["ss"]
        W("vector", "ss", ss_sq)
        vq("tensor_scalar", stats[:, 2:3], stats[:, 0:1], 1.0 / D, None, ALU.mult)
        vq("tensor_scalar", stats[:, 3:4], stats[:, 1:2], 1.0 / D, None, ALU.mult)
        vq("tensor_tensor", stats[:, 6:7], stats[:, 2:3], stats[:, 2:3], ALU.mult)
        vq("tensor_sub", stats[:, 3:4], stats[:, 3:4], stats[:, 6:7])
        vq("tensor_scalar", stats[:, 3:4], stats[:, 3:4], EPS, None, ALU.add)
        vq("reciprocal", stats[:, 6:7], stats[:, 3:4])
        vs_rv = cnt["vs"]
        W("scalar", "vs", vs_rv)
        s_act(stats[:, 4:5], stats[:, 6:7], AF.Sqrt, inc=True)
        ss_a = cnt["ss"]
        W("vector", "ss", ss_a)
        vq("tensor_tensor", stats[:, 5:6], stats[:, 2:3], stats[:, 4:5], ALU.mult)
        vq("tensor_scalar", stats[:, 5:6], stats[:, 5:6], -1.0, None, ALU.mult)
        vs_nb = cnt["vs"]
        W("scalar", "vs", vs_nb)
        W("scalar", "ts", hi_free_t)
        s_act(hi_sb[:, :], xt[sl][:, :], AF.Identity, bias=stats[:, 5:6],
              scale=stats[:, 4:5], inc=True)
        ss_xn = cnt["ss"]
        xt_free_s[sl] = ss_xn
        xt_free_v[sl] = cnt["vs"]
        W("tensor", "ss", ss_xn)
        for g in range(4):
            par = g % 2
            W("tensor", "vs", tr_bank_free_v[par])
            for k in range(8):
                fc = g * 8 + k
                t_tr(ptr[par][:, k * 128:(k + 1) * 128],
                     hi_sb[:, fc * 128:(fc + 1) * 128], ident[:, :],
                     inc=(k == 7))
            ts_g = cnt["ts"]
            if g == 3:
                hi_free_t = ts_g
            W("vector", "ts", ts_g)
            dst = xn_sb[:, :].rearrange("p (a b) -> p a b", b=TOK)[
                :, g * 8:(g + 1) * 8, tt * 128:(tt + 1) * 128]
            src = ptr[par][:, :].rearrange("p (a b) -> p a b", b=128)
            tr_bank_free_v[par] = v_op("tensor_copy", dst, src, inc=True)
        ev3[tt] = cnt["vs"]
        if tt >= 3 and tt % 2 == 1:
            emit_kstrips((tt - 3) // 2)
    emit_kstrips(2)
    emit_kstrips(3)
    emit_kevac(0)
    emit_kevac(1)
    xn_done_v = cnt["vs"]

    W("tensor", "vs", xn_done_v)
    W("tensor", "vs", vs_wrt)
    for fc in range(32):
        for st_ in range(2):
            t_mm(plog[0:8, st_ * 512:(st_ + 1) * 512],
                 wrt_sb[:, fc * 8:(fc + 1) * 8],
                 xn_sb[:, fc * TOK + st_ * 512: fc * TOK + st_ * 512 + 512],
                 start=(fc == 0), stop=(fc == 31), inc=(fc == 31 and st_ == 1))
    ts_log = cnt["ts"]
    W("vector", "ts", ts_log)
    vs_lg = v_op("tensor_copy", lg_sb[:, :], plog[0:8, 0:1024], inc=True)
    W("tensor", "vs", vs_lg)
    for tt in range(8):
        t_tr(plog[:, 512 + tt * 8:512 + (tt + 1) * 8],
             lg_sb[:, tt * 128:(tt + 1) * 128],
             identf[0:8, 0:8], inc=(tt == 7))
    ts_ltm = cnt["ts"]
    W("vector", "ts", ts_ltm)
    v_op("tensor_copy", ltm_sb[:, :], plog[:, 512:576], inc=True)
    p0_psum_free_v = cnt["vs"]
    for tt in range(8):
        lt = ltm_sb[:, tt * 8:(tt + 1) * 8]
        vq("reduce_max", rs0[:, :], lt, axis=AX.X)
        vq("tensor_scalar", rt1[:, :], lt, rs0[:, :], None, ALU.is_ge)
        vq("tensor_scalar", rs0[:, :], rs0[:, :], -1.0, None, ALU.mult)
        vs_m1 = cnt["vs"]
        W("scalar", "vs", vs_m1)
        s_act(rt0[:, :], lt, AF.Exp, bias=rs0[:, :], inc=True)
        ss_pr = cnt["ss"]
        W("vector", "ss", ss_pr)
        vq("tensor_tensor", rt2[:, :], rt0[:, :], rt1[:, :], ALU.mult)
        vq("tensor_sub", rt2[:, :], rt0[:, :], rt2[:, :])
        vq("reduce_max", rs1[:, :], rt2[:, :], axis=AX.X)
        vq("tensor_scalar", rt1[:, :], rt0[:, :], rs1[:, :], None, ALU.is_ge)
        vq("reduce_max", rs0[:, :], rt0[:, :], axis=AX.X)
        vq("tensor_add", rs1[:, :], rs1[:, :], rs0[:, :])
        vq("reciprocal", rs1[:, :], rs1[:, :])
        vq("tensor_tensor", rt2[:, :], rt0[:, :], rt1[:, :], ALU.mult)
        vq("tensor_scalar", wtm_sb[:, tt * 8:(tt + 1) * 8], rt2[:, :],
           rs1[:, :], None, ALU.mult)
    vs_w = cnt["vs"]

    p0ps.close()
    p0sc.close()
    wqesc.close()
    xtsc.close()
    p1wsc = ExitStack()
    wgt2 = [SB(p1wsc, "wgt2_%d" % i, [128, D], BF16, side="right") for i in range(6)]
    wgt4.extend(wgt2)
    p1sc = ExitStack()
    p1ps = ExitStack()
    pnorm = PS(p1ps, "pnorm", [128, 1024], F32)
    pbb = PS(p1ps, "pbb", [128, 512], F32)
    pvt = PS(p1ps, "pvt", [128, 1024], BF16)
    q2_sb = SB(p1sc, "q2_sb", [128, TOK], BF16)
    wgt4 = wgt + wgt2
    vfm = [SB(p1sc, "vfm0", [128, 1024], BF16), SB(p1sc, "vfm1", [128, 1024], BF16)]
    vtm = SB(p1sc, "vtm", [128, 1024], BF16)

    wblk_free_t = [0] * 8
    p0_v_done = p0_psum_free_v

    def qkv_block(kind, dt):
        nonlocal widx
        sl = widx % 8
        if kind == "q":
            ten, off, cc_need = agq_out, dt * AGQ_BLK, 3
        elif kind == "k":
            ten, off, cc_need = awk_out, dt * HBLK, 1
        else:
            ten, off, cc_need = awv_out, dt * HBLK, 2
        qe, s0, s1 = "scalar", "dw%d" % sl, "dh%d" % sl
        W(qe, "cc", cc_need)
        W(qe, "ts", wblk_free_t[sl])
        dvw_h = []
        for j, snm in enumerate((s0, s1)):
            dvw_h.append(dma(qe, snm,
                             wgt4[sl][:, j * 2048:(j + 1) * 2048],
                             bass.AP(ten, off + j * 2048,
                                     [[4096, 128], [1, 2048]])))
        widx += 1
        bks = [(2 * dt) % 4, (2 * dt + 1) % 4]
        W("tensor", "vs", max(bank_free_v[bks[0]], bank_free_v[bks[1]]))
        for fc in range(32):
            if fc % 16 == 0:
                W("tensor", (s0 if fc == 0 else s1), dvw_h[fc // 16])
            for st_ in range(2):
                t_mm(pqkv[bks[st_]][:, :], wgt4[sl][:, fc * 128:(fc + 1) * 128],
                     xn_sb[:, fc * TOK + st_ * 512: fc * TOK + st_ * 512 + 512],
                     start=(fc == 0), stop=(fc == 31),
                     inc=(fc == 31))
        wblk_free_t[sl] = cnt["ts"]
        return bks

    nrm_mm_done = 0
    nrm_red_done = p0_v_done

    def norms(src_ap, col):
        nonlocal nrm_mm_done, nrm_red_done
        W("scalar", "ts", nrm_mm_done)
        s_act(q2_sb[:, :], src_ap, AF.Square, inc=True)
        W("tensor", "ss", cnt["ss"])
        W("tensor", "vs", vs_memset)
        W("tensor", "vs", nrm_red_done)
        for st_ in range(2):
            t_mm(pnorm[0:1, st_ * 512:(st_ + 1) * 512], ones_c[:, :],
                 q2_sb[:, st_ * 512:(st_ + 1) * 512], start=True, stop=True,
                 inc=(st_ == 1))
        nrm_mm_done = cnt["ts"]
        W("vector", "ts", nrm_mm_done)
        nrm_red_done = v_op("reduce_max", mq_sb[0:1, col:col + 1], pnorm[0:1, 0:1024],
                            axis=AX.X, inc=True)

    widx = 2
    wblk_free_t[0] = strip_ts[(0, 3)]
    wblk_free_t[1] = strip_ts[(1, 3)]
    for dt in range(2, NH):
        banks = qkv_block("k", dt)
        for st_, bk in enumerate(banks):
            W("vector", "ts", cnt["ts"] - (1 - st_))
            if st_ == 0:
                W("vector", "dki%d" % (dt % 3), k_store_done[dt - 3])
                W("vector", "ss", k_sq_done[dt - 3])
            bank_free_v[bk] = v_op("tensor_copy",
                                   ktmp[dt % 3][:, st_ * 512:(st_ + 1) * 512],
                                   pqkv[bk][:, :], inc=True)
        vs_k_at[dt] = cnt["vs"]
        W("sync", "vs", vs_k_at[dt])
        k_store_done[dt] = dma("sync", "dki%d" % (dt % 3),
                               bass.AP(ka_in if dt < 4 else kb_in,
                                       (dt % 4) * 131072, [[1024, 128], [1, 1024]]),
                               ktmp[dt % 3][:, :])
        if dt == 3:
            for dkn in ("dki0", "dki1", "dki2"):
                W("gpsimd", dkn, cnt[dkn])
            cc_ka = trigger_ag(ka_in, ka_out)
            trigger_ag(wbt_in[2], wbt_out[2])
        if dt >= 2:
            W("scalar", "vs", vs_k_at[dt - 2])
            norms(ktmp[(dt - 2) % 3][:, :], 8 + dt - 2)
            k_sq_done[dt - 2] = cnt["ss"]
    for dtl in (NH - 2, NH - 1):
        W("scalar", "vs", vs_k_at[dtl])
        norms(ktmp[dtl % 3][:, :], 8 + dtl)
        k_sq_done[dtl] = cnt["ss"]
    for dkn in ("dki0", "dki1", "dki2"):
        W("gpsimd", dkn, cnt[dkn])
    trigger_ag(kb_in, kb_out)
    trigger_ag(wbt_in[3], wbt_out[3])
    v_store_done = cnt["dkvin"]
    vtail_ts = {}
    vfm_vs = {}
    vtm_copy_vs = 0

    def v_tail(dt):
        nonlocal v_store_done, vtm_copy_vs
        W("tensor", "vs", vfm_vs[dt])
        W("tensor", "vs", vtm_copy_vs)
        for tt in range(8):
            t_tr(pvt[:, tt * 128:(tt + 1) * 128],
                 vfm[dt % 2][:, tt * 128:(tt + 1) * 128],
                 ident[:, :], inc=(tt == 7))
        vtail_ts[dt] = cnt["ts"]
        W("vector", "ts", vtail_ts[dt])
        W("vector", "dkvin", v_store_done)
        vs_vt = v_op("tensor_copy", vtm[:, :], pvt[:, 0:1024], inc=True)
        vtm_copy_vs = vs_vt
        W("sync", "vs", vs_vt)
        dma("sync", "dkvin",
            bass.AP(va_in, dt * 128, [[1024, 128], [131072, 4], [1, 128]]),
            vtm[:, 0:512].rearrange("p (a b) -> p a b", b=128))
        v_store_done = dma("sync", "dkvin",
                           bass.AP(vb_in, dt * 128,
                                   [[1024, 128], [131072, 4], [1, 128]]),
                           vtm[:, 512:1024].rearrange("p (a b) -> p a b", b=128))

    for dt in range(NH):
        banks = qkv_block("v", dt)
        for st_, bk in enumerate(banks):
            W("vector", "ts", cnt["ts"] - (1 - st_))
            if st_ == 0 and dt >= 2:
                W("vector", "ts", vtail_ts[dt - 2])
            bank_free_v[bk] = v_op("tensor_copy",
                                   vfm[dt % 2][:, st_ * 512:(st_ + 1) * 512],
                                   pqkv[bk][:, :], inc=True)
        vfm_vs[dt] = cnt["vs"]
        if dt >= 1:
            v_tail(dt - 1)
    v_tail(NH - 1)
    q_ev_at = {}
    for dt in range(NH):
        banks = qkv_block("q", dt)
        for st_, bk in enumerate(banks):
            W("vector", "ts", cnt["ts"] - (1 - st_))
            bank_free_v[bk] = v_op(
                "tensor_copy", q_sb[:, dt * TOK + st_ * 512: dt * TOK + st_ * 512 + 512],
                pqkv[bk][:, :], inc=True)
        q_ev_at[dt] = cnt["vs"]
        if dt >= 2:
            W("scalar", "vs", q_ev_at[dt - 2])
            norms(q_sb[:, (dt - 2) * TOK: (dt - 1) * TOK], dt - 2)
    for dtl in (NH - 2, NH - 1):
        W("scalar", "vs", q_ev_at[dtl])
        norms(q_sb[:, dtl * TOK: (dtl + 1) * TOK], dtl)
    vs_q_done = cnt["vs"]
    vq("tensor_tensor", mq_sb[0:1, 0:8], mq_sb[0:1, 0:8], mq_sb[0:1, 8:16], ALU.mult)
    vs_bh1 = cnt["vs"]
    W("scalar", "vs", vs_bh1)
    s_act(mq_sb[0:1, 0:8], mq_sb[0:1, 0:8], AF.Sqrt, inc=True)
    ss_bh = cnt["ss"]
    W("vector", "ss", ss_bh)
    v_op("tensor_scalar", bhr_sb[0:1, 0:8], mq_sb[0:1, 0:8], ISQ, None, ALU.mult,
         inc=True)
    vs_bh2 = cnt["vs"]
    W("tensor", "vs", vs_bh2)
    t_mm(pbb[0:128, 0:8], onesr[:, :], bhr_sb[0:1, 0:8], start=True, stop=True,
         inc=True)
    ts_bb = cnt["ts"]
    W("vector", "ts", ts_bb)
    vs_bh = v_op("tensor_scalar", bh_sb[:, :], pbb[0:128, 0:8], -1.0, None, ALU.mult,
                 inc=True)
    W("gpsimd", "dkvin", cnt["dkvin"])
    trigger_ag(va_in, va_out)
    trigger_ag(vb_in, vb_out)

    p1ps.close()
    qkvps.close()
    p2ps = ExitStack()
    pbase = [PS(p2ps, "pbase%d" % i, [128, 512], F32) for i in range(4)]
    ph = PS(p2ps, "ph", [128, 1024], F32)
    pwx = PS(p2ps, "pwx", [128, 1024], F32)
    p1sc.close()
    p1wsc.close()
    kearly.close()
    kvearly = ExitStack()
    kbuf = [SB(kvearly, "kbuf0", [128, D], BF16)]
    vbuf = [SB(kvearly, "vbuf0", [128, D], BF16)]
    gssc = ExitStack()
    g_sb = SB(gssc, "g_sb", [128, 32 * TOK], BF16)
    p2sc = ExitStack()
    ball_sb = SB(p2sc, "ball_sb", [128, D], BF16, side="right")
    W("gpsimd", "ts", wblk_free_t[2])
    dball = dma("gpsimd", "dmisc", ball_sb[:, :].rearrange("p (a b) -> p a b", b=512),
                bass.AP(agw_out, AGW_BALL, [[512, 128], [AGW_BLK, 8], [1, 512]]))
    W("tensor", "vs", cnt["vs"])
    W("tensor", "vs", vs_w)
    for tt in range(8):
        t_tr(pwx[0:8, tt * 128:(tt + 1) * 128], wtm_sb[:, tt * 8:(tt + 1) * 8],
             identf[:, :], inc=(tt == 7))
    ts_wf = cnt["ts"]
    W("vector", "ts", ts_wf)
    vs_wfm = v_op("tensor_copy", wfm_bf[:, :], pwx[0:8, 0:1024], inc=True)
    W("tensor", "dlat", dlat)
    W("tensor", "dmisc", dball)
    for fc in range(32):
        for st_ in range(2):
            t_mm(ph[:, st_ * 512:(st_ + 1) * 512], latw[:, fc * 128:(fc + 1) * 128],
                 xn_sb[:, fc * TOK + st_ * 512: fc * TOK + st_ * 512 + 512],
                 start=(fc == 0), stop=(fc == 31), inc=(fc == 31 and st_ == 1))
    ts_h = cnt["ts"]
    W("tensor", "vs", vs_wfm)
    W("tensor", "dmisc", 32)
    for st_ in range(2):
        t_mm(pwx[:, st_ * 512:(st_ + 1) * 512], gmat_sb[:, :],
             wfm_bf[0:8, st_ * 512:(st_ + 1) * 512], start=True, stop=True,
             inc=(st_ == 1))
    ts_wx = cnt["ts"]
    W("vector", "ts", max(ts_h, ts_wx))
    v_op("tensor_copy", h_sb[:, :], ph[:, 0:1024], inc=True)
    vs_hw = vq("tensor_tensor", hw_sb[:, :], h_sb[:, :], pwx[:, 0:1024], ALU.mult)
    base_bank_free_s = [0, 0, 0, 0]
    wblk_free_t = wblk_free_t[:2]
    wblk_free_t[0] = max(wblk_free_t[0], cnt["ts"])
    wblk_free_t[1] = max(wblk_free_t[1], cnt["ts"])
    for ot in range(32):
        sl = widx % 2
        W("sync", "cc", 10)
        W("sync", "ts", wblk_free_t[sl])
        for j in range(4):
            dvw = dma("sync", "dw%d" % sl,
                      wgt[sl][:, j * 1024:(j + 1) * 1024],
                      bass.AP(wbt_out[j], (ot // 4) * HBLK + (ot % 4) * 131072,
                              [[1024, 128], [1, 1024]]))
        widx += 1
        W("tensor", "dw%d" % sl, dvw)
        if ot == 0:
            W("tensor", "vs", vs_hw)
            W("tensor", "dmisc", dball)
        bks = [2 * (ot % 2), 2 * (ot % 2) + 1]
        W("tensor", "ss", max(base_bank_free_s[bks[0]], base_bank_free_s[bks[1]]))
        for fc in range(32):
            for st_ in range(2):
                t_mm(pbase[bks[st_]][:, :], wgt[sl][:, fc * 128:(fc + 1) * 128],
                     xn_sb[:, fc * TOK + st_ * 512: fc * TOK + st_ * 512 + 512],
                     start=(fc == 0), stop=False)
        for st_ in range(2):
            t_mm(pbase[bks[st_]][:, :], ball_sb[:, ot * 128:(ot + 1) * 128],
                 hw_sb[:, st_ * 512:(st_ + 1) * 512], start=False, stop=True,
                 inc=(st_ == 1))
        wblk_free_t[sl] = cnt["ts"]
        if ot == 0:
            W("scalar", "dsm", 48)
            W("scalar", "dkvin", cnt["dkvin"])
            W("scalar", "dki0", cnt["dki0"])
            W("scalar", "dki1", cnt["dki1"])
            W("scalar", "dki2", cnt["dki2"])
        W("scalar", "ts", cnt["ts"])
        for st_ in range(2):
            bk = 2 * (ot % 2) + st_
            base_bank_free_s[bk] = s_act(
                g_sb[:, ot * TOK + st_ * 512: ot * TOK + st_ * 512 + 512],
                pbase[bk][:, :], AF.Silu, bias=b2_sb[:, ot:ot + 1], inc=True)
    g_done_s = cnt["ss"]

    p2ps.close()
    p2sc.close()
    xnsc.close()
    p3ps = ExitStack()
    pO = [PS(p3ps, "pO%d" % i, [128, 512], F32) for i in range(2)]
    pST = [PS(p3ps, "pST%d" % i, [128, 1024], F32) for i in range(2)]
    pdr = PS(p3ps, "pdr", [128, 512], F32)
    pot1 = PS(p3ps, "pot1", [128, 1024], BF16)
    osc = ExitStack()
    o_sb = SB(osc, "o_sb", [128, NH * TOK], BF16)
    p3sc = ExitStack()
    kbuf.append(SB(p3sc, "kbuf1", [128, D], BF16))
    vbuf.append(SB(p3sc, "vbuf1", [128, D], BF16))
    pt_sb = [SB(p3sc, "pt0", [128, 1024], BF16), SB(p3sc, "pt1", [128, 1024], BF16)]
    dacc = SB(p3sc, "dacc", [128, 1024], F32)
    otm = SB(p3sc, "otm", [128, 1024], BF16)

    def emit_kv_load(idx, slot_free_ts):
        h, half = divmod(idx, 2)
        sl = idx % 2
        W("sync", "cc", 7 if h < 4 else 9)
        W("sync", "ts", slot_free_ts)
        kt_src = ka_out if h < 4 else kb_out
        dma("sync", "dkv%d" % sl,
            kbuf[sl][:, :].rearrange("p (a b) -> p a b", b=1024),
            bass.AP(kt_src, 4 * half * HBLK + (h % 4) * 131072,
                    [[1024, 128], [HBLK, 4], [1, 1024]]))
        W("sync", "cc", 12)
        last = 0
        for r4 in range(4):
            dma("sync", "dkv%d" % sl,
                vbuf[sl][:, r4 * 1024: r4 * 1024 + 512].rearrange(
                    "p (a b) -> p a b", b=128),
                bass.AP(va_out, (4 * half + r4) * HBLK + h * 128,
                        [[1024, 128], [131072, 4], [1, 128]]))
            last = dma("sync", "dkv%d" % sl,
                       vbuf[sl][:, r4 * 1024 + 512: (r4 + 1) * 1024].rearrange(
                           "p (a b) -> p a b", b=128),
                       bass.AP(vb_out, (4 * half + r4) * HBLK + h * 128,
                               [[1024, 128], [131072, 4], [1, 128]]))
        return last

    ts_p2_done = cnt["ts"]
    kv_dma = {0: emit_kv_load(0, 0), 1: emit_kv_load(1, ts_p2_done)}
    scores_ts = {}
    exp_ss = {}
    pv_ts = {-2: 0, -1: 0}
    dadd_vs = {-2: 0, -1: 0}
    o_evac_v = 0
    dred_vs = {-1: p0_v_done}
    ts_transp = {-1: 0}
    pot1_free_v = [0]
    otm_free_t = [0]
    pending_dance = []

    def emit_dance(h):
        W("tensor", "vs", o_evac_v)
        W("tensor", "vs", pot1_free_v[0])
        for tt in range(8):
            t_tr(pot1[:, tt * 128:(tt + 1) * 128],
                 o_sb[:, h * TOK + tt * 128: h * TOK + tt * 128 + 128],
                 ident[:, :], inc=(tt == 7))
        ts_o1 = cnt["ts"]
        W("vector", "ts", ts_o1)
        W("vector", "vs", dred_vs[h])
        W("vector", "ts", otm_free_t[0])
        for tt in range(8):
            v_op("tensor_scalar", otm[:, tt * 128:(tt + 1) * 128],
                 pot1[:, tt * 128:(tt + 1) * 128],
                 dtm2_sb[:, h * 8 + tt: h * 8 + tt + 1], None, ALU.mult,
                 inc=(tt == 7))
        vs_otm = cnt["vs"]
        W("tensor", "vs", vs_otm)
        for tt in range(8):
            t_tr(pot1[:, tt * 128:(tt + 1) * 128],
                 otm[:, tt * 128:(tt + 1) * 128], ident[:, :], inc=(tt == 7))
        otm_free_t[0] = cnt["ts"]
        W("vector", "ts", otm_free_t[0])
        pot1_free_v[0] = v_op("tensor_copy", o_sb[:, h * TOK:(h + 1) * TOK],
                              pot1[:, 0:1024], inc=True)

    def emit_scores(g):
        h, kt = divmod(g, 64)
        idx = 2 * h + (kt // 32)
        sl = idx % 2
        kt32 = kt % 32
        stb = g % 2
        W("tensor", "dkv%d" % sl, kv_dma[idx])
        if g == 0:
            W("tensor", "vs", vs_q_done)
            W("tensor", "vs", vs_bh)
            W("tensor", "vs", vs_hw)
            W("tensor", "ss", g_done_s)
        W("tensor", "ss", exp_ss.get(g - 2, 0))
        if kt == 0:
            W("tensor", "vs", o_evac_v)
        for st_ in range(2):
            t_mm(pST[stb][:, st_ * 512:(st_ + 1) * 512],
                 kbuf[sl][:, kt32 * 128:(kt32 + 1) * 128],
                 q_sb[:, h * TOK + st_ * 512: h * TOK + st_ * 512 + 512],
                 start=True, stop=True, inc=(st_ == 1))
        scores_ts[g] = cnt["ts"]

    def emit_exp(g):
        h = g // 64
        stb = g % 2
        W("scalar", "ts", scores_ts[g])
        W("scalar", "ts", pv_ts.get(g - 2, 0))
        W("scalar", "vs", dadd_vs.get(g - 2, 0))
        s_act(pt_sb[stb][:, :], pST[stb][:, 0:1024], AF.Exp,
              bias=bh_sb[:, h:h + 1], scale=ISQ, inc=True)
        exp_ss[g] = cnt["ss"]

    def emit_dadd(g):
        kt = g % 64
        stb = g % 2
        W("vector", "ss", exp_ss[g])
        if kt == 0:
            W("vector", "ts", ts_transp.get(g // 64 - 1, 0))
            v_op("tensor_copy", dacc[:, :], pt_sb[stb][:, :], inc=True)
        else:
            W("vector", "vs", cnt["vs"])
            v_op("tensor_add", dacc[:, :], dacc[:, :], pt_sb[stb][:, :], inc=True)
        dadd_vs[g] = cnt["vs"]

    def emit_pv(g):
        h, kt = divmod(g, 64)
        idx = 2 * h + (kt // 32)
        sl = idx % 2
        kt32 = kt % 32
        stb = g % 2
        W("tensor", "ss", exp_ss[g])
        for st_ in range(2):
            t_mm(pO[st_][:, :], vbuf[sl][:, kt32 * 128:(kt32 + 1) * 128],
                 pt_sb[stb][:, st_ * 512:(st_ + 1) * 512],
                 start=(kt == 0), stop=(kt == 63), inc=(st_ == 1))
        pv_ts[g] = cnt["ts"]

    emit_scores(0)
    for g in range(NH * 64):
        h, kt = divmod(g, 64)
        if g + 1 < NH * 64 and (g + 1) % 64 != 0:
            emit_scores(g + 1)
        emit_exp(g)
        emit_dadd(g)
        emit_pv(g)
        if kt == 4 and pending_dance:
            emit_dance(pending_dance.pop(0))
        if kt == 31:
            idx = 2 * h + 0
            if idx + 2 < 16:
                kv_dma[idx + 2] = emit_kv_load(idx + 2, cnt["ts"])
        if kt == 63:
            idx = 2 * h + 1
            if idx + 2 < 16:
                kv_dma[idx + 2] = emit_kv_load(idx + 2, cnt["ts"])
            W("vector", "ts", pv_ts[g])
            for st_ in range(2):
                v_op("tensor_copy",
                     o_sb[:, h * TOK + st_ * 512: h * TOK + st_ * 512 + 512],
                     pO[st_][:, :], inc=(st_ == 1))
            o_evac_v = cnt["vs"]
            for half4 in range(2):
                W("tensor", "vs", dadd_vs[g] if half4 == 0 else cnt["vs"])
                W("tensor", "vs", dred_vs.get(h - 1, 0) if half4 == 0 else cnt["vs"])
                for tt4 in range(4):
                    tt = half4 * 4 + tt4
                    t_tr(pdr[:, tt4 * 128:(tt4 + 1) * 128],
                         dacc[:, tt * 128:(tt + 1) * 128], identf[:, :],
                         inc=(tt4 == 3))
                W("vector", "ts", cnt["ts"])
                for tt4 in range(4):
                    tt = half4 * 4 + tt4
                    v_op("reduce_sum", dtm_sb[:, h * 8 + tt: h * 8 + tt + 1],
                         pdr[:, tt4 * 128:(tt4 + 1) * 128], axis=AX.X,
                         inc=(tt4 == 3))
            ts_transp[h] = cnt["ts"]
            vq("reciprocal", dtm2_sb[:, h * 8:(h + 1) * 8],
               dtm_sb[:, h * 8:(h + 1) * 8])
            dred_vs[h] = cnt["vs"]
            if g + 1 < NH * 64:
                emit_scores(g + 1)
            pending_dance.append(h)

    while pending_dance:
        emit_dance(pending_dance.pop(0))
    p3ps.close()
    p3tps = ExitStack()
    o_norm_v = cnt["vs"]

    p3tps.close()
    p3sc.close()
    p4ps = ExitStack()
    ps4 = [PS(p4ps, "ps4_%d" % i, [128, 512], F32) for i in range(4)]
    psg = [PS(p4ps, "psg0", [128, 1024], BF16), PS(p4ps, "psg1", [128, 1024], BF16)]
    latsc.close()
    qsc.close()
    p4sc = ExitStack()
    xcol = [SB(p4sc, "xcol%d" % i, [128, 1024], F32) for i in range(4)]
    outp = [SB(p4sc, "outp%d" % i, [128, 1024], F32) for i in range(4)]


    s4_bank_free_v = [0, 0, 0, 0]
    psg_free_v = [0, 0]
    sgv_free_t = [0, 0]
    vs_sg = {}
    out_store_done = {-4: 0, -3: 0, -2: 0, -1: 0}
    xcol_d = {}
    mm_ts = {}
    for par0 in range(4):
        W("scalar", "vs", o_norm_v)
        xcol_d[par0] = dma("scalar", "dx%d" % par0,
                           xcol[par0][:, :].rearrange("p (a b) -> p a b", b=128),
                           bass.AP(x_sl, par0 * 128,
                                   [[4096, 128], [524288, 8], [1, 128]]))

    ts_p4_wgt_free = max(wblk_free_t[0], wblk_free_t[1])
    dwo_done = {}

    def wo_slot(q4):
        return wgt[q4 % 2][:, (q4 // 2) * 1024:(q4 // 2) * 1024 + 1024]

    def wo_load(dmt, gate_ts):
        q4 = dmt % 4
        W("gpsimd", "ts", gate_ts)
        dwo_done[dmt] = dma("gpsimd", "dwo%d" % q4,
                            wo_slot(q4),
                            bass.AP(agw_out,
                                    (dmt // 4) * AGW_BLK + AGW_WO + (dmt % 4) * 131072,
                                    [[1024, 128], [1, 1024]]))

    for dmt0 in range(4):
        wo_load(dmt0, ts_p4_wgt_free)

    def p4_mms(dmt):
        q4 = dmt % 4
        W("tensor", "dwo%d" % q4, dwo_done[dmt])
        if dmt == 0:
            W("tensor", "vs", o_norm_v)
        bks4 = [2 * (dmt % 2), 2 * (dmt % 2) + 1]
        W("tensor", "vs", max(s4_bank_free_v[bks4[0]], s4_bank_free_v[bks4[1]]))
        wsl = wo_slot(q4)
        for dkc in range(8):
            for st_ in range(2):
                t_mm(ps4[bks4[st_]][:, :], wsl[:, dkc * 128:(dkc + 1) * 128],
                     o_sb[:, dkc * TOK + st_ * 512: dkc * TOK + st_ * 512 + 512],
                     start=(dkc == 0), stop=(dkc == 7),
                     inc=(dkc == 7 and st_ == 1))
        mm_ts[dmt] = cnt["ts"]
        if dmt + 4 < 32:
            wo_load(dmt + 4, mm_ts[dmt])

    def p4_evac(dmt):
        par = dmt % 2
        if dmt == 0:
            W("vector", "ss", g_done_s)
        W("vector", "ts", sgv_free_t[par])
        W("vector", "ts", mm_ts[dmt])
        for st_ in range(2):
            bk = 2 * (dmt % 2) + st_
            s4_bank_free_v[bk] = v_op(
                "tensor_tensor",
                wgt[par][:, 2048 + st_ * 512: 2048 + st_ * 512 + 512],
                ps4[bk][:, :],
                g_sb[:, dmt * TOK + st_ * 512: dmt * TOK + st_ * 512 + 512],
                ALU.mult, inc=True)
        vs_sg[dmt] = cnt["vs"]

    def p4_fin(dmt):
        par = dmt % 2
        q4 = dmt % 4
        W("tensor", "vs", vs_sg[dmt])
        W("tensor", "vs", psg_free_v[par])
        for tt in range(8):
            t_tr(psg[par][:, tt * 128:(tt + 1) * 128],
                 wgt[par][:, 2048 + tt * 128: 2048 + tt * 128 + 128], ident[:, :],
                 inc=(tt == 7))
        sgv_free_t[par] = cnt["ts"]
        W("vector", "ts", sgv_free_t[par])
        W("vector", "dx%d" % q4, xcol_d[dmt])
        W("vector", "st%d" % q4, out_store_done[dmt - 4])
        psg_free_v[par] = v_op("scalar_tensor_tensor", outp[q4][:, :],
                               psg[par][:, 0:1024], 1.0, xcol[q4][:, :],
                               ALU.mult, ALU.add, inc=True)
        W("sync", "vs", psg_free_v[par])
        out_store_done[dmt] = dma("sync", "st%d" % q4,
                                  bass.AP(out_d, dmt * 128,
                                          [[4096, 128], [524288, 8], [1, 128]]),
                                  outp[q4][:, :].rearrange("p (a b) -> p a b", b=128))
        if dmt + 4 < 32:
            W("scalar", "vs", psg_free_v[par])
            xcol_d[dmt + 4] = dma("scalar", "dx%d" % q4,
                                  xcol[q4][:, :].rearrange("p (a b) -> p a b", b=128),
                                  bass.AP(x_sl, (dmt + 4) * 128,
                                          [[4096, 128], [524288, 8], [1, 128]]))

    p4_mms(0)
    p4_mms(1)
    for dmt in range(32):
        p4_evac(dmt)
        if dmt % 2 == 1:
            p4_fin(dmt - 1)
            p4_fin(dmt)
        if dmt + 2 < 32:
            p4_mms(dmt + 2)
    for stn in ("st0", "st1", "st2", "st3"):
        W("sync", stn, cnt[stn])

    p4ps.close()
    p4sc.close()
    osc.close()
    gssc.close()
    kvearly.close()
    wqesc_dummy = None
    gsc.close()

    with nc.Block() as block:
        @block.sync
        def _(e):
            for f in prog["sync"]:
                f(e)

        @block.gpsimd
        def _(e):
            for f in prog["gpsimd"]:
                f(e)

        @block.tensor
        def _(e):
            for f in prog["tensor"]:
                f(e)

        @block.vector
        def _(e):
            for f in prog["vector"]:
                f(e)

        @block.scalar
        def _(e):
            for f in prog["scalar"]:
                f(e)

    return nc


_NC_CACHE = []


def kernel(L0_attn_x, L0_attn_wq, L0_attn_wk, L0_attn_wv, L0_attn_wo,
           L0_ffn_W_base, L0_ffn_b_base, L0_ffn_W_router, L0_ffn_lora_A,
           L0_ffn_lora_B):
    x = np.ascontiguousarray(np.asarray(L0_attn_x, dtype=np.float32)[0])
    wq = np.asarray(L0_attn_wq, dtype=np.float32)
    wk = np.asarray(L0_attn_wk, dtype=np.float32)
    wv = np.asarray(L0_attn_wv, dtype=np.float32)
    wo = np.asarray(L0_attn_wo, dtype=np.float32)
    wb = np.asarray(L0_ffn_W_base, dtype=np.float32)
    bb = np.asarray(L0_ffn_b_base, dtype=np.float32)
    wr = np.asarray(L0_ffn_W_router, dtype=np.float32)
    la = np.asarray(L0_ffn_lora_A, dtype=np.float32)
    lb = np.asarray(L0_ffn_lora_B, dtype=np.float32)

    ball = np.ascontiguousarray(lb.transpose(0, 2, 1).reshape(ER, D))
    lat = np.ascontiguousarray(la.reshape(ER, D).T)
    b2 = np.ascontiguousarray(bb.reshape(32, 128).T)
    gmat = np.ascontiguousarray(
        (np.kron(np.eye(8), np.ones((1, 16))) * 2.0).astype(np.float32))
    id_bf = np.eye(128, dtype=ml_dtypes.bfloat16)
    id_f32 = np.eye(128, dtype=np.float32)

    if not _NC_CACHE:
        _NC_CACHE.append(build_program())
    nc = _NC_CACHE[0]

    in_maps = []
    for c in range(NC):
        in_maps.append({
            "x_sl": np.ascontiguousarray(x[c * TOK:(c + 1) * TOK]),
            "wq_sl": np.ascontiguousarray(wq[:, c * 512:(c + 1) * 512]),
            "wk_sl": np.ascontiguousarray(wk[:, c * 128:(c + 1) * 128]),
            "wv_sl": np.ascontiguousarray(wv[:, c * 128:(c + 1) * 128]),
            "wo_sl": np.ascontiguousarray(wo[:, c * 512:(c + 1) * 512]),
            "wbt_sl": np.ascontiguousarray(wb[c * 512:(c + 1) * 512, :].T),
            "ball_sl": np.ascontiguousarray(ball[:, c * 512:(c + 1) * 512]),
            "lat_sl": np.ascontiguousarray(lat[c * 512:(c + 1) * 512, :]),
            "wr_in": wr,
            "b2_in": b2,
            "gmat_in": gmat,
            "id_bf": id_bf,
            "id_f32": id_f32,
        })
    res = run_bass_kernel_spmd(nc, in_maps, core_ids=list(range(NC)))
    out = np.concatenate([np.asarray(res.results[c]["out"]) for c in range(NC)],
                         axis=0)
    return out.reshape(1, S, D).astype(np.float32)


# version 41
# speedup vs baseline: 1.1430x; 1.0020x over previous
import math
import numpy as np
import ml_dtypes
import concourse.bass as bass
import concourse.mybir as mybir
from concourse.bass_utils import run_bass_kernel_spmd

F32 = mybir.dt.float32
BF16 = mybir.dt.bfloat16
AF = mybir.ActivationFunctionType
ALU = mybir.AluOpType
AX = mybir.AxisListType

NC = 8
TOK = 1024
S = 8192
D = 4096
NH = 8
HD = 128
DKV = 1024
ER = 128
EPS = 1e-5
ISQ = 1.0 / math.sqrt(128.0)

AGQ_BLK = 4096 * 128
AGW_WO, AGW_BALL, AGW_LAT = 0, 524288, 589824
AGW_BLK = 655360
HBLK = 524288


ENGS = ("sync", "gpsimd", "tensor", "vector", "scalar")


def build_program():
    nc = bass.Bass("TRN2", target_bir_lowering=False)

    x_sl = nc.dram_tensor("x_sl", [TOK, D], F32, kind="ExternalInput")
    wq_sl = nc.dram_tensor("wq_sl", [D, 512], F32, kind="ExternalInput")
    wk_sl = nc.dram_tensor("wk_sl", [D, 128], F32, kind="ExternalInput")
    wv_sl = nc.dram_tensor("wv_sl", [D, 128], F32, kind="ExternalInput")
    wo_sl = nc.dram_tensor("wo_sl", [DKV, 512], F32, kind="ExternalInput")
    wbt_sl = nc.dram_tensor("wbt_sl", [D, 512], F32, kind="ExternalInput")
    ball_sl = nc.dram_tensor("ball_sl", [ER, 512], F32, kind="ExternalInput")
    lat_sl = nc.dram_tensor("lat_sl", [512, ER], F32, kind="ExternalInput")
    wr_in = nc.dram_tensor("wr_in", [8, D], F32, kind="ExternalInput")
    b2_in = nc.dram_tensor("b2_in", [128, 32], F32, kind="ExternalInput")
    gmat_in = nc.dram_tensor("gmat_in", [8, 128], F32, kind="ExternalInput")
    id_bf = nc.dram_tensor("id_bf", [128, 128], BF16, kind="ExternalInput")
    id_f32 = nc.dram_tensor("id_f32", [128, 128], F32, kind="ExternalInput")
    out_d = nc.dram_tensor("out", [TOK, D], F32, kind="ExternalOutput")

    awk_in = nc.dram_tensor("awk_in", [HBLK], BF16)
    awk_out = nc.dram_tensor("awk_out", [NC * HBLK], BF16)
    awv_in = nc.dram_tensor("awv_in", [HBLK], BF16)
    awv_out = nc.dram_tensor("awv_out", [NC * HBLK], BF16)
    agq_in = nc.dram_tensor("agq_in", [AGQ_BLK], BF16)
    agq_out = nc.dram_tensor("agq_out", [NC * AGQ_BLK], BF16)
    agw_in = nc.dram_tensor("agw_in", [AGW_BLK], BF16)
    agw_out = nc.dram_tensor("agw_out", [NC * AGW_BLK], BF16)
    wbt_in = [nc.dram_tensor("wbt%d_in" % j, [HBLK], BF16) for j in range(4)]
    wbt_out = [nc.dram_tensor("wbt%d_out" % j, [NC * HBLK], BF16) for j in range(4)]
    ka_in = nc.dram_tensor("ka_in", [HBLK], BF16)
    ka_out = nc.dram_tensor("ka_out", [NC * HBLK], BF16)
    kb_in = nc.dram_tensor("kb_in", [HBLK], BF16)
    kb_out = nc.dram_tensor("kb_out", [NC * HBLK], BF16)
    va_in = nc.dram_tensor("va_in", [HBLK], BF16)
    va_out = nc.dram_tensor("va_out", [NC * HBLK], BF16)
    vb_in = nc.dram_tensor("vb_in", [HBLK], BF16)
    vb_out = nc.dram_tensor("vb_out", [NC * HBLK], BF16)

    from contextlib import ExitStack
    gsc = ExitStack()
    wqesc = ExitStack()

    def SB(scope, name, shape, dt, side="left"):
        return scope.enter_context(nc.sbuf_tensor(name, shape, dt, side=side))

    def PS(scope, name, shape, dt):
        return scope.enter_context(nc.psum_tensor(name, shape, dt))

    def SEM(name):
        return gsc.enter_context(nc.semaphore(name))

    sems = {}
    cnt = {}
    for nm in ("dx0", "dx1", "dx2", "dx3", "dw0", "dw1", "dw2", "dw3", "dkv0", "dkv1", "dqe", "dmisc",
               "dsm", "dag", "dkvin", "dki0", "dki1", "dki2", "dwo0", "dwo1", "dwo2", "dwo3", "dsg0", "dsg1", "st0", "st1", "st2", "st3", "ts", "vs",
               "ss", "cc", "gp", "dw4", "dw5", "dw6", "dw7",
               "dh0", "dh1", "dh2", "dh3", "dh4", "dh5", "dh6", "dh7", "dlat",
               "dkh0", "dkh1", "dkh2", "dkh3", "dkh4", "dkh5"):
        sems[nm] = SEM(nm)
        cnt[nm] = 0

    prog = {e: [] for e in ENGS}

    def E(eng, fn):
        prog[eng].append(fn)

    def W(eng, sem_nm, val):
        if val <= 0:
            return
        h = sems[sem_nm]
        E(eng, lambda e, h=h, v=val: e.wait_ge(h, v))

    def bump(sem_nm, n=1):
        cnt[sem_nm] += n
        return cnt[sem_nm]

    def dma(eng, sem_nm, out_ap, in_ap, accum=False):
        h = sems[sem_nm]
        if accum:
            E(eng, lambda e, o=out_ap, i=in_ap, h=h: e.dma_start(
                out=o, in_=i, accum_op=ALU.add).then_inc(h, 16))
        else:
            E(eng, lambda e, o=out_ap, i=in_ap, h=h: e.dma_start(
                out=o, in_=i).then_inc(h, 16))
        return bump(sem_nm, 16)

    def t_mm(out_ap, l_ap, r_ap, start, stop, inc=False):
        if inc:
            h = sems["ts"]
            E("tensor", lambda e, o=out_ap, l=l_ap, r=r_ap, a=start, b=stop, h=h:
              e.matmul(o, l, r, start=a, stop=b).then_inc(h, 1))
            return bump("ts")
        E("tensor", lambda e, o=out_ap, l=l_ap, r=r_ap, a=start, b=stop:
          e.matmul(o, l, r, start=a, stop=b))
        return cnt["ts"]

    def t_tr(out_ap, in_ap, id_ap, inc=False):
        if inc:
            h = sems["ts"]
            E("tensor", lambda e, o=out_ap, i=in_ap, d=id_ap, h=h:
              e.transpose(o, i, d).then_inc(h, 1))
            return bump("ts")
        E("tensor", lambda e, o=out_ap, i=in_ap, d=id_ap: e.transpose(o, i, d))
        return cnt["ts"]

    def v_op(meth, *aps, inc=False, **kw):
        if inc:
            h = sems["vs"]
            E("vector", lambda e, m=meth, a=aps, k=kw, h=h:
              getattr(e, m)(*a, **k).then_inc(h, 1))
            return bump("vs")
        E("vector", lambda e, m=meth, a=aps, k=kw: getattr(e, m)(*a, **k))
        return cnt["vs"]

    def vq(meth, *aps, **kw):
        W("vector", "vs", cnt["vs"])
        return v_op(meth, *aps, inc=True, **kw)

    def s_act(out_ap, in_ap, func, bias=0.0, scale=1.0, accum=None, inc=False):
        kw = dict(bias=bias, scale=scale)
        if accum is not None:
            kw["accum_out"] = accum
        if inc:
            h = sems["ss"]
            E("scalar", lambda e, o=out_ap, i=in_ap, f=func, k=kw, h=h:
              e.activation(o, i, f, **k).then_inc(h, 1))
            return bump("ss")
        E("scalar", lambda e, o=out_ap, i=in_ap, f=func, k=kw: e.activation(o, i, f, **k))
        return cnt["ss"]

    qsc = ExitStack()
    q_sb = SB(qsc, "q_sb", [128, NH * TOK], BF16, side="right")
    ident = SB(gsc, "ident", [128, 128], BF16)
    identf = SB(gsc, "identf", [128, 128], F32)
    ones_c = SB(gsc, "ones_c", [128, 1], BF16)
    onesr = SB(gsc, "onesr", [1, 128], BF16)
    gmat_sb = SB(gsc, "gmat_sb", [8, 128], BF16)
    b2_sb = SB(gsc, "b2_sb", [128, 32], F32)
    wrt_sb = SB(gsc, "wrt_sb", [128, 256], BF16)
    ltm_sb = SB(gsc, "ltm_sb", [128, 64], F32)
    wtm_sb = SB(gsc, "wtm_sb", [128, 64], F32)
    wfm_bf = SB(gsc, "wfm_bf", [8, TOK], BF16)
    rt0 = SB(gsc, "rt0", [128, 8], F32)
    rt1 = SB(gsc, "rt1", [128, 8], F32)
    rt2 = SB(gsc, "rt2", [128, 8], F32)
    rs0 = SB(gsc, "rs0", [128, 1], F32)
    rs1 = SB(gsc, "rs1", [128, 1], F32)
    stats = SB(gsc, "stats", [128, 16], F32)
    mq_sb = SB(gsc, "mq_sb", [1, 16], F32)
    bh_sb = SB(gsc, "bh_sb", [128, 8], F32)
    bhr_sb = SB(gsc, "bhr_sb", [1, 8], BF16)
    dtm_sb = SB(gsc, "dtm_sb", [128, 64], F32)
    dtm2_sb = SB(gsc, "dtm2_sb", [128, 64], F32)
    wgt = [SB(gsc, "wgt0", [128, D], BF16), SB(gsc, "wgt1", [128, D], BF16)]
    h_sb = SB(gsc, "h_sb", [128, TOK], F32)
    hw_sb = SB(gsc, "hw_sb", [128, TOK], BF16)
    kearly = ExitStack()
    ktmp = [SB(kearly, "ktmp%d" % i, [128, 1024], BF16) for i in range(3)]
    latsc = ExitStack()
    latw = SB(latsc, "latw", [128, D], BF16, side="right")
    p0sc = ExitStack()
    xtsc = ExitStack()
    xt = [SB(xtsc, "xt%d" % i, [128, D], F32) for i in range(3)]
    wr_sb = xt[2][0:8, 0:2048].bitcast(BF16)
    lg_sb = h_sb[0:8, :]
    wqe_acc = SB(wqesc, "wqe_acc", [128, D], F32)
    wqe_bf = SB(wqesc, "wqe_bf", [128, D], BF16)

    dma("gpsimd", "dmisc", gmat_sb[:, :], gmat_in.ap())
    rg = [list(range(NC))]

    def trigger_ag(in_t, out_t):
        E("gpsimd", lambda e, a=in_t, b=out_t: e.collective_compute(
            "AllGather", ALU.bypass, replica_groups=rg,
            ins=[a.ap().opt()], outs=[b.ap().opt()]).then_inc(sems["cc"], 1))
        return bump("cc")

    dma("gpsimd", "dag", bass.AP(awk_in, 0, [[128, 4096], [1, 128]]), wk_sl.ap())
    W("gpsimd", "dag", cnt["dag"])
    trigger_ag(awk_in, awk_out)
    dma("gpsimd", "dag", bass.AP(awv_in, 0, [[128, 4096], [1, 128]]), wv_sl.ap())
    W("gpsimd", "dag", cnt["dag"])
    trigger_ag(awv_in, awv_out)
    for j in range(4):
        v = dma("gpsimd", "dqe",
                wqe_acc[:, :].rearrange("p (a b) -> p a b", b=128),
                bass.AP(wq_sl, j * 128, [[512, 128], [65536, 32], [1, 128]]),
                accum=(j > 0))
        W("gpsimd", "dqe", v)
    E("gpsimd", lambda e: e.tensor_copy(wqe_bf[:, :], wqe_acc[:, :]).then_inc(
        sems["gp"], 1))
    bump("gp")
    W("gpsimd", "gp", cnt["gp"])
    dma("gpsimd", "dag", bass.AP(agq_in, 0, [[4096, 128], [128, 32], [1, 128]]),
        wqe_bf[:, :].rearrange("p (a b) -> p a b", b=128))
    W("gpsimd", "dag", cnt["dag"])
    trigger_ag(agq_in, agq_out)
    dma("gpsimd", "dag", bass.AP(agw_in, AGW_WO, [[512, 1024], [1, 512]]), wo_sl.ap())
    dma("gpsimd", "dag", bass.AP(agw_in, AGW_BALL, [[512, 128], [1, 512]]), ball_sl.ap())
    dma("gpsimd", "dag", bass.AP(agw_in, AGW_LAT, [[128, 512], [1, 128]]), lat_sl.ap())
    W("gpsimd", "dag", cnt["dag"])
    trigger_ag(agw_in, agw_out)
    for j in range(4):
        dma("gpsimd", "dag", bass.AP(wbt_in[j], 0, [[512, 1024], [1, 512]]),
            wbt_sl.ap()[j * 1024:(j + 1) * 1024, :])
    W("gpsimd", "dag", cnt["dag"])
    for j in (0, 1):
        trigger_ag(wbt_in[j], wbt_out[j])
    dma("gpsimd", "dmisc", wr_sb[:, :], wr_in.ap())
    W("gpsimd", "cc", 4)
    for r in range(NC):
        dlat = dma("gpsimd", "dlat",
                   latw[:, r * 512:(r + 1) * 512].rearrange("p (a b) -> p a b", b=128),
                   bass.AP(agw_out, r * AGW_BLK + AGW_LAT,
                           [[128, 128], [16384, 4], [1, 128]]))

    xnsc = ExitStack()
    xn_sb = SB(xnsc, "xn_sb", [128, 32 * TOK], BF16, side="right")
    hi_sb = SB(p0sc, "hi_sb", [128, D], BF16)
    qkvps = ExitStack()
    pqkv = [PS(qkvps, "pqkv%d" % i, [128, 512], F32) for i in range(4)]
    p0ps = ExitStack()
    ptr = [PS(p0ps, "ptr0", [128, 1024], BF16), PS(p0ps, "ptr1", [128, 1024], BF16)]
    plog = PS(p0ps, "plog", [128, 1024], F32)

    dma("sync", "dsm", ident[:, :], id_bf.ap())
    dma("sync", "dsm", identf[:, :], id_f32.ap())
    dma("sync", "dsm", b2_sb[:, :], b2_in.ap())
    v_op("memset", ones_c[:, :], 1.0)
    v_op("memset", onesr[:, :], 1.0, inc=True)
    vs_memset = cnt["vs"]

    W("tensor", "dmisc", 32)
    W("tensor", "dsm", 48)
    for fc in range(32):
        t_tr(ptr[0][:, fc * 8:(fc + 1) * 8], wr_sb[:, fc * 128:(fc + 1) * 128],
             ident[0:8, 0:8], inc=(fc == 31))
    ts_wrt = cnt["ts"]
    W("vector", "ts", ts_wrt)
    vs_wrt = v_op("tensor_copy", wrt_sb[:, :], ptr[0][:, 0:256], inc=True)

    pdvK = [0, 0]
    ev3 = {}
    strip_ts = {}
    k_store_done = {-3: 0, -2: 0, -1: 0}
    k_sq_done = {-3: 0, -2: 0, -1: 0}
    vs_k_at = {}
    bank_free_v = [0] * 4

    def emit_kevac(dt):
        for st_, bk in enumerate([(2 * dt) % 4, (2 * dt + 1) % 4]):
            W("vector", "ts", strip_ts[(dt, st_ * 2 + 1)])
            bank_free_v[bk] = v_op("tensor_copy",
                                   ktmp[dt % 3][:, st_ * 512:(st_ + 1) * 512],
                                   pqkv[bk][:, :], inc=True)
        vs_k_at[dt] = cnt["vs"]
        W("sync", "vs", vs_k_at[dt])
        k_store_done[dt] = dma("sync", "dki%d" % (dt % 3),
                               bass.AP(ka_in, (dt % 4) * 131072,
                                       [[1024, 128], [1, 1024]]),
                               ktmp[dt % 3][:, :])

    def emit_kstrips(pr):
        W("tensor", "vs", ev3[2 * pr + 1])
        for ki in range(2):
            if pr == 0:
                W("tensor", "dw%d" % ki, pdvK[ki])
            bank = ki * 2 + pr // 2
            col = (pr % 2) * 256
            for fc in range(32):
                t_mm(pqkv[bank][:, col:col + 256],
                     wgt[ki][:, fc * 128:(fc + 1) * 128],
                     xn_sb[:, fc * TOK + pr * 256: fc * TOK + pr * 256 + 256],
                     start=(fc == 0), stop=(fc == 31), inc=(fc == 31))
            strip_ts[(ki, pr)] = cnt["ts"]

    wgt4 = list(wgt)
    xt_free_s = [0, 0, 0]
    xt_free_v = [0, 0, 0]
    hi_free_t = 0
    tr_bank_free_v = [vs_wrt, vs_wrt]
    xn_done_v = 0
    for tt in range(8):
        sl = tt % 3
        W("sync", "ss", xt_free_s[sl])
        W("sync", "vs", xt_free_v[sl])
        if tt == 2:
            W("sync", "ts", ts_wrt)
        dv = dma("sync", "dx%d" % sl, xt[sl][:, :],
                 x_sl.ap()[tt * 128:(tt + 1) * 128, :])
        if tt == 1:
            for pi in range(2):
                W("scalar", "cc", 1)
                pdvK[pi] = dma("scalar", "dw%d" % pi, wgt[pi][:, :],
                               bass.AP(awk_out, pi * HBLK,
                                       [[4096, 128], [1, 4096]]))
        W("vector", "dx%d" % sl, dv)
        if tt == 0:
            W("vector", "gp", 1)
        v_op("tensor_scalar", wqe_acc[:, :], xt[sl][:, :], 1.0, None, ALU.mult,
             op1=ALU.add, accum_out=stats[:, 0:1], inc=True)
        W("scalar", "dx%d" % sl, dv)
        W("scalar", "ts", hi_free_t)
        s_act(hi_sb[:, :], xt[sl][:, :], AF.Square, accum=stats[:, 1:2], inc=True)
        ss_sq = cnt["ss"]
        W("vector", "ss", ss_sq)
        vq("tensor_scalar", stats[:, 2:3], stats[:, 0:1], 1.0 / D, None, ALU.mult)
        vq("tensor_scalar", stats[:, 3:4], stats[:, 1:2], 1.0 / D, None, ALU.mult)
        vq("tensor_tensor", stats[:, 6:7], stats[:, 2:3], stats[:, 2:3], ALU.mult)
        vq("tensor_sub", stats[:, 3:4], stats[:, 3:4], stats[:, 6:7])
        vq("tensor_scalar", stats[:, 3:4], stats[:, 3:4], EPS, None, ALU.add)
        vq("reciprocal", stats[:, 6:7], stats[:, 3:4])
        vs_rv = cnt["vs"]
        W("scalar", "vs", vs_rv)
        s_act(stats[:, 4:5], stats[:, 6:7], AF.Sqrt, inc=True)
        ss_a = cnt["ss"]
        W("vector", "ss", ss_a)
        vq("tensor_tensor", stats[:, 5:6], stats[:, 2:3], stats[:, 4:5], ALU.mult)
        vq("tensor_scalar", stats[:, 5:6], stats[:, 5:6], -1.0, None, ALU.mult)
        vs_nb = cnt["vs"]
        W("scalar", "vs", vs_nb)
        W("scalar", "ts", hi_free_t)
        s_act(hi_sb[:, :], xt[sl][:, :], AF.Identity, bias=stats[:, 5:6],
              scale=stats[:, 4:5], inc=True)
        ss_xn = cnt["ss"]
        xt_free_s[sl] = ss_xn
        xt_free_v[sl] = cnt["vs"]
        W("tensor", "ss", ss_xn)
        for g in range(4):
            par = g % 2
            W("tensor", "vs", tr_bank_free_v[par])
            for k in range(8):
                fc = g * 8 + k
                t_tr(ptr[par][:, k * 128:(k + 1) * 128],
                     hi_sb[:, fc * 128:(fc + 1) * 128], ident[:, :],
                     inc=(k == 7))
            ts_g = cnt["ts"]
            if g == 3:
                hi_free_t = ts_g
            W("vector", "ts", ts_g)
            dst = xn_sb[:, :].rearrange("p (a b) -> p a b", b=TOK)[
                :, g * 8:(g + 1) * 8, tt * 128:(tt + 1) * 128]
            src = ptr[par][:, :].rearrange("p (a b) -> p a b", b=128)
            tr_bank_free_v[par] = v_op("tensor_copy", dst, src, inc=True)
        ev3[tt] = cnt["vs"]
        if tt >= 3 and tt % 2 == 1:
            emit_kstrips((tt - 3) // 2)
    emit_kstrips(2)
    emit_kstrips(3)
    emit_kevac(0)
    emit_kevac(1)
    xn_done_v = cnt["vs"]

    W("tensor", "vs", xn_done_v)
    W("tensor", "vs", vs_wrt)
    for fc in range(32):
        for st_ in range(2):
            t_mm(plog[0:8, st_ * 512:(st_ + 1) * 512],
                 wrt_sb[:, fc * 8:(fc + 1) * 8],
                 xn_sb[:, fc * TOK + st_ * 512: fc * TOK + st_ * 512 + 512],
                 start=(fc == 0), stop=(fc == 31), inc=(fc == 31 and st_ == 1))
    ts_log = cnt["ts"]
    W("vector", "ts", ts_log)
    vs_lg = v_op("tensor_copy", lg_sb[:, :], plog[0:8, 0:1024], inc=True)
    W("tensor", "vs", vs_lg)
    for tt in range(8):
        t_tr(plog[:, 512 + tt * 8:512 + (tt + 1) * 8],
             lg_sb[:, tt * 128:(tt + 1) * 128],
             identf[0:8, 0:8], inc=(tt == 7))
    ts_ltm = cnt["ts"]
    W("vector", "ts", ts_ltm)
    v_op("tensor_copy", ltm_sb[:, :], plog[:, 512:576], inc=True)
    p0_psum_free_v = cnt["vs"]
    for tt in range(8):
        lt = ltm_sb[:, tt * 8:(tt + 1) * 8]
        vq("reduce_max", rs0[:, :], lt, axis=AX.X)
        vq("tensor_scalar", rt1[:, :], lt, rs0[:, :], None, ALU.is_ge)
        vq("tensor_scalar", rs0[:, :], rs0[:, :], -1.0, None, ALU.mult)
        vs_m1 = cnt["vs"]
        W("scalar", "vs", vs_m1)
        s_act(rt0[:, :], lt, AF.Exp, bias=rs0[:, :], inc=True)
        ss_pr = cnt["ss"]
        W("vector", "ss", ss_pr)
        vq("tensor_tensor", rt2[:, :], rt0[:, :], rt1[:, :], ALU.mult)
        vq("tensor_sub", rt2[:, :], rt0[:, :], rt2[:, :])
        vq("reduce_max", rs1[:, :], rt2[:, :], axis=AX.X)
        vq("tensor_scalar", rt1[:, :], rt0[:, :], rs1[:, :], None, ALU.is_ge)
        vq("reduce_max", rs0[:, :], rt0[:, :], axis=AX.X)
        vq("tensor_add", rs1[:, :], rs1[:, :], rs0[:, :])
        vq("reciprocal", rs1[:, :], rs1[:, :])
        vq("tensor_tensor", rt2[:, :], rt0[:, :], rt1[:, :], ALU.mult)
        vq("tensor_scalar", wtm_sb[:, tt * 8:(tt + 1) * 8], rt2[:, :],
           rs1[:, :], None, ALU.mult)
    vs_w = cnt["vs"]

    p0ps.close()
    p0sc.close()
    wqesc.close()
    xtsc.close()
    p1wsc = ExitStack()
    wgt2 = [SB(p1wsc, "wgt2_%d" % i, [128, D], BF16, side="right") for i in range(6)]
    wgt4.extend(wgt2)
    p1sc = ExitStack()
    p1ps = ExitStack()
    pnorm = PS(p1ps, "pnorm", [128, 1024], F32)
    pbb = PS(p1ps, "pbb", [128, 512], F32)
    pvt = PS(p1ps, "pvt", [128, 1024], BF16)
    q2_sb = SB(p1sc, "q2_sb", [128, TOK], BF16)
    wgt4 = wgt + wgt2
    vfm = [SB(p1sc, "vfm0", [128, 1024], BF16), SB(p1sc, "vfm1", [128, 1024], BF16)]
    vtm = SB(p1sc, "vtm", [128, 1024], BF16)

    wblk_free_t = [0] * 8
    p0_v_done = p0_psum_free_v

    def qkv_block(kind, dt):
        nonlocal widx
        sl = widx % 8
        if kind == "q":
            ten, off, cc_need = agq_out, dt * AGQ_BLK, 3
        elif kind == "k":
            ten, off, cc_need = awk_out, dt * HBLK, 1
        else:
            ten, off, cc_need = awv_out, dt * HBLK, 2
        qe, s0, s1 = "scalar", "dw%d" % sl, "dh%d" % sl
        W(qe, "cc", cc_need)
        W(qe, "ts", wblk_free_t[sl])
        dvw_h = []
        for j, snm in enumerate((s0, s1)):
            dvw_h.append(dma(qe, snm,
                             wgt4[sl][:, j * 2048:(j + 1) * 2048],
                             bass.AP(ten, off + j * 2048,
                                     [[4096, 128], [1, 2048]])))
        widx += 1
        bks = [(2 * dt) % 4, (2 * dt + 1) % 4]
        W("tensor", "vs", max(bank_free_v[bks[0]], bank_free_v[bks[1]]))
        for fc in range(32):
            if fc % 16 == 0:
                W("tensor", (s0 if fc == 0 else s1), dvw_h[fc // 16])
            for st_ in range(2):
                t_mm(pqkv[bks[st_]][:, :], wgt4[sl][:, fc * 128:(fc + 1) * 128],
                     xn_sb[:, fc * TOK + st_ * 512: fc * TOK + st_ * 512 + 512],
                     start=(fc == 0), stop=(fc == 31),
                     inc=(fc == 31))
        wblk_free_t[sl] = cnt["ts"]
        return bks

    nrm_mm_done = 0
    nrm_red_done = p0_v_done

    def norms(src_ap, col):
        nonlocal nrm_mm_done, nrm_red_done
        W("scalar", "ts", nrm_mm_done)
        s_act(q2_sb[:, :], src_ap, AF.Square, inc=True)
        W("tensor", "ss", cnt["ss"])
        W("tensor", "vs", vs_memset)
        W("tensor", "vs", nrm_red_done)
        for st_ in range(2):
            t_mm(pnorm[0:1, st_ * 512:(st_ + 1) * 512], ones_c[:, :],
                 q2_sb[:, st_ * 512:(st_ + 1) * 512], start=True, stop=True,
                 inc=(st_ == 1))
        nrm_mm_done = cnt["ts"]
        W("vector", "ts", nrm_mm_done)
        nrm_red_done = v_op("reduce_max", mq_sb[0:1, col:col + 1], pnorm[0:1, 0:1024],
                            axis=AX.X, inc=True)

    widx = 2
    wblk_free_t[0] = strip_ts[(0, 3)]
    wblk_free_t[1] = strip_ts[(1, 3)]
    for dt in range(2, NH):
        banks = qkv_block("k", dt)
        for st_, bk in enumerate(banks):
            W("vector", "ts", cnt["ts"] - (1 - st_))
            if st_ == 0:
                W("vector", "dki%d" % (dt % 3), k_store_done[dt - 3])
                W("vector", "ss", k_sq_done[dt - 3])
            bank_free_v[bk] = v_op("tensor_copy",
                                   ktmp[dt % 3][:, st_ * 512:(st_ + 1) * 512],
                                   pqkv[bk][:, :], inc=True)
        vs_k_at[dt] = cnt["vs"]
        W("sync", "vs", vs_k_at[dt])
        k_store_done[dt] = dma("sync", "dki%d" % (dt % 3),
                               bass.AP(ka_in if dt < 4 else kb_in,
                                       (dt % 4) * 131072, [[1024, 128], [1, 1024]]),
                               ktmp[dt % 3][:, :])
        if dt == 3:
            for dkn in ("dki0", "dki1", "dki2"):
                W("gpsimd", dkn, cnt[dkn])
            cc_ka = trigger_ag(ka_in, ka_out)
            trigger_ag(wbt_in[2], wbt_out[2])
        if dt >= 2:
            W("scalar", "vs", vs_k_at[dt - 2])
            norms(ktmp[(dt - 2) % 3][:, :], 8 + dt - 2)
            k_sq_done[dt - 2] = cnt["ss"]
    for dtl in (NH - 2, NH - 1):
        W("scalar", "vs", vs_k_at[dtl])
        norms(ktmp[dtl % 3][:, :], 8 + dtl)
        k_sq_done[dtl] = cnt["ss"]
    for dkn in ("dki0", "dki1", "dki2"):
        W("gpsimd", dkn, cnt[dkn])
    trigger_ag(kb_in, kb_out)
    trigger_ag(wbt_in[3], wbt_out[3])
    v_store_done = cnt["dkvin"]
    vtail_ts = {}
    vfm_vs = {}
    vtm_copy_vs = 0

    def v_tail(dt):
        nonlocal v_store_done, vtm_copy_vs
        W("tensor", "vs", vfm_vs[dt])
        W("tensor", "vs", vtm_copy_vs)
        for tt in range(8):
            t_tr(pvt[:, tt * 128:(tt + 1) * 128],
                 vfm[dt % 2][:, tt * 128:(tt + 1) * 128],
                 ident[:, :], inc=(tt == 7))
        vtail_ts[dt] = cnt["ts"]
        W("vector", "ts", vtail_ts[dt])
        W("vector", "dkvin", v_store_done)
        vs_vt = v_op("tensor_copy", vtm[:, :], pvt[:, 0:1024], inc=True)
        vtm_copy_vs = vs_vt
        W("sync", "vs", vs_vt)
        dma("sync", "dkvin",
            bass.AP(va_in, dt * 128, [[1024, 128], [131072, 4], [1, 128]]),
            vtm[:, 0:512].rearrange("p (a b) -> p a b", b=128))
        v_store_done = dma("sync", "dkvin",
                           bass.AP(vb_in, dt * 128,
                                   [[1024, 128], [131072, 4], [1, 128]]),
                           vtm[:, 512:1024].rearrange("p (a b) -> p a b", b=128))

    for dt in range(NH):
        banks = qkv_block("v", dt)
        for st_, bk in enumerate(banks):
            W("vector", "ts", cnt["ts"] - (1 - st_))
            if st_ == 0 and dt >= 2:
                W("vector", "ts", vtail_ts[dt - 2])
            bank_free_v[bk] = v_op("tensor_copy",
                                   vfm[dt % 2][:, st_ * 512:(st_ + 1) * 512],
                                   pqkv[bk][:, :], inc=True)
        vfm_vs[dt] = cnt["vs"]
        if dt >= 1:
            v_tail(dt - 1)
    v_tail(NH - 1)
    q_ev_at = {}
    for dt in range(NH):
        banks = qkv_block("q", dt)
        for st_, bk in enumerate(banks):
            W("vector", "ts", cnt["ts"] - (1 - st_))
            bank_free_v[bk] = v_op(
                "tensor_copy", q_sb[:, dt * TOK + st_ * 512: dt * TOK + st_ * 512 + 512],
                pqkv[bk][:, :], inc=True)
        q_ev_at[dt] = cnt["vs"]
        if dt >= 2:
            W("scalar", "vs", q_ev_at[dt - 2])
            norms(q_sb[:, (dt - 2) * TOK: (dt - 1) * TOK], dt - 2)
    for dtl in (NH - 2, NH - 1):
        W("scalar", "vs", q_ev_at[dtl])
        norms(q_sb[:, dtl * TOK: (dtl + 1) * TOK], dtl)
    vs_q_done = cnt["vs"]
    vq("tensor_tensor", mq_sb[0:1, 0:8], mq_sb[0:1, 0:8], mq_sb[0:1, 8:16], ALU.mult)
    vs_bh1 = cnt["vs"]
    W("scalar", "vs", vs_bh1)
    s_act(mq_sb[0:1, 0:8], mq_sb[0:1, 0:8], AF.Sqrt, inc=True)
    ss_bh = cnt["ss"]
    W("vector", "ss", ss_bh)
    v_op("tensor_scalar", bhr_sb[0:1, 0:8], mq_sb[0:1, 0:8], ISQ, None, ALU.mult,
         inc=True)
    vs_bh2 = cnt["vs"]
    W("tensor", "vs", vs_bh2)
    t_mm(pbb[0:128, 0:8], onesr[:, :], bhr_sb[0:1, 0:8], start=True, stop=True,
         inc=True)
    ts_bb = cnt["ts"]
    W("vector", "ts", ts_bb)
    vs_bh = v_op("tensor_scalar", bh_sb[:, :], pbb[0:128, 0:8], -1.0, None, ALU.mult,
                 inc=True)
    W("gpsimd", "dkvin", cnt["dkvin"])
    trigger_ag(va_in, va_out)
    trigger_ag(vb_in, vb_out)

    p1ps.close()
    qkvps.close()
    p2ps = ExitStack()
    pbase = [PS(p2ps, "pbase%d" % i, [128, 512], F32) for i in range(4)]
    ph = PS(p2ps, "ph", [128, 1024], F32)
    pwx = PS(p2ps, "pwx", [128, 1024], F32)
    p1sc.close()
    p1wsc.close()
    kearly.close()
    kvearly = ExitStack()
    kbuf = [SB(kvearly, "kbuf0", [128, D], BF16)]
    vbuf = [SB(kvearly, "vbuf0", [128, D], BF16)]
    gssc = ExitStack()
    g_sb = SB(gssc, "g_sb", [128, 32 * TOK], BF16)
    p2sc = ExitStack()
    ball_sb = SB(p2sc, "ball_sb", [128, D], BF16, side="right")
    W("gpsimd", "ts", wblk_free_t[2])
    dball = dma("gpsimd", "dmisc", ball_sb[:, :].rearrange("p (a b) -> p a b", b=512),
                bass.AP(agw_out, AGW_BALL, [[512, 128], [AGW_BLK, 8], [1, 512]]))
    W("tensor", "vs", cnt["vs"])
    W("tensor", "vs", vs_w)
    for tt in range(8):
        t_tr(pwx[0:8, tt * 128:(tt + 1) * 128], wtm_sb[:, tt * 8:(tt + 1) * 8],
             identf[:, :], inc=(tt == 7))
    ts_wf = cnt["ts"]
    W("vector", "ts", ts_wf)
    vs_wfm = v_op("tensor_copy", wfm_bf[:, :], pwx[0:8, 0:1024], inc=True)
    W("tensor", "dlat", dlat)
    W("tensor", "dmisc", dball)
    for fc in range(32):
        for st_ in range(2):
            t_mm(ph[:, st_ * 512:(st_ + 1) * 512], latw[:, fc * 128:(fc + 1) * 128],
                 xn_sb[:, fc * TOK + st_ * 512: fc * TOK + st_ * 512 + 512],
                 start=(fc == 0), stop=(fc == 31), inc=(fc == 31 and st_ == 1))
    ts_h = cnt["ts"]
    W("tensor", "vs", vs_wfm)
    W("tensor", "dmisc", 32)
    for st_ in range(2):
        t_mm(pwx[:, st_ * 512:(st_ + 1) * 512], gmat_sb[:, :],
             wfm_bf[0:8, st_ * 512:(st_ + 1) * 512], start=True, stop=True,
             inc=(st_ == 1))
    ts_wx = cnt["ts"]
    W("vector", "ts", max(ts_h, ts_wx))
    v_op("tensor_copy", h_sb[:, :], ph[:, 0:1024], inc=True)
    vs_hw = vq("tensor_tensor", hw_sb[:, :], h_sb[:, :], pwx[:, 0:1024], ALU.mult)
    base_bank_free_s = [0, 0, 0, 0]
    wblk_free_t = wblk_free_t[:2]
    wblk_free_t[0] = max(wblk_free_t[0], cnt["ts"])
    wblk_free_t[1] = max(wblk_free_t[1], cnt["ts"])
    for ot in range(32):
        sl = widx % 2
        W("sync", "cc", 10)
        W("sync", "ts", wblk_free_t[sl])
        for j in range(4):
            dvw = dma("sync", "dw%d" % sl,
                      wgt[sl][:, j * 1024:(j + 1) * 1024],
                      bass.AP(wbt_out[j], (ot // 4) * HBLK + (ot % 4) * 131072,
                              [[1024, 128], [1, 1024]]))
        widx += 1
        W("tensor", "dw%d" % sl, dvw)
        if ot == 0:
            W("tensor", "vs", vs_hw)
            W("tensor", "dmisc", dball)
        bks = [2 * (ot % 2), 2 * (ot % 2) + 1]
        W("tensor", "ss", max(base_bank_free_s[bks[0]], base_bank_free_s[bks[1]]))
        for fc in range(32):
            for st_ in range(2):
                t_mm(pbase[bks[st_]][:, :], wgt[sl][:, fc * 128:(fc + 1) * 128],
                     xn_sb[:, fc * TOK + st_ * 512: fc * TOK + st_ * 512 + 512],
                     start=(fc == 0), stop=False)
        for st_ in range(2):
            t_mm(pbase[bks[st_]][:, :], ball_sb[:, ot * 128:(ot + 1) * 128],
                 hw_sb[:, st_ * 512:(st_ + 1) * 512], start=False, stop=True,
                 inc=(st_ == 1))
        wblk_free_t[sl] = cnt["ts"]
        if ot == 0:
            W("scalar", "dsm", 48)
            W("scalar", "dkvin", cnt["dkvin"])
            W("scalar", "dki0", cnt["dki0"])
            W("scalar", "dki1", cnt["dki1"])
            W("scalar", "dki2", cnt["dki2"])
        W("scalar", "ts", cnt["ts"])
        for st_ in range(2):
            bk = 2 * (ot % 2) + st_
            base_bank_free_s[bk] = s_act(
                g_sb[:, ot * TOK + st_ * 512: ot * TOK + st_ * 512 + 512],
                pbase[bk][:, :], AF.Silu, bias=b2_sb[:, ot:ot + 1], inc=True)
    g_done_s = cnt["ss"]

    p2ps.close()
    p2sc.close()
    xnsc.close()
    p3ps = ExitStack()
    pO = [PS(p3ps, "pO%d" % i, [128, 512], F32) for i in range(2)]
    pST = [PS(p3ps, "pST%d" % i, [128, 1024], F32) for i in range(2)]
    pdr = PS(p3ps, "pdr", [128, 512], F32)
    pot1 = PS(p3ps, "pot1", [128, 1024], BF16)
    osc = ExitStack()
    o_sb = SB(osc, "o_sb", [128, NH * TOK], BF16)
    p3sc = ExitStack()
    kbuf.append(SB(p3sc, "kbuf1", [128, D], BF16))
    vbuf.append(SB(p3sc, "vbuf1", [128, D], BF16))
    pt_sb = [SB(p3sc, "pt0", [128, 1024], BF16), SB(p3sc, "pt1", [128, 1024], BF16)]
    dacc = SB(p3sc, "dacc", [128, 1024], F32)
    otm = SB(p3sc, "otm", [128, 1024], BF16)

    def emit_kv_load(idx, slot_free_ts):
        h, half = divmod(idx, 2)
        sl = idx % 2
        W("sync", "cc", 7 if h < 4 else 9)
        W("sync", "ts", slot_free_ts)
        kt_src = ka_out if h < 4 else kb_out
        dma("sync", "dkv%d" % sl,
            kbuf[sl][:, :].rearrange("p (a b) -> p a b", b=1024),
            bass.AP(kt_src, 4 * half * HBLK + (h % 4) * 131072,
                    [[1024, 128], [HBLK, 4], [1, 1024]]))
        W("sync", "cc", 12)
        last = 0
        for r4 in range(4):
            dma("sync", "dkv%d" % sl,
                vbuf[sl][:, r4 * 1024: r4 * 1024 + 512].rearrange(
                    "p (a b) -> p a b", b=128),
                bass.AP(va_out, (4 * half + r4) * HBLK + h * 128,
                        [[1024, 128], [131072, 4], [1, 128]]))
            last = dma("sync", "dkv%d" % sl,
                       vbuf[sl][:, r4 * 1024 + 512: (r4 + 1) * 1024].rearrange(
                           "p (a b) -> p a b", b=128),
                       bass.AP(vb_out, (4 * half + r4) * HBLK + h * 128,
                               [[1024, 128], [131072, 4], [1, 128]]))
        return last

    ts_p2_done = cnt["ts"]
    kv_dma = {0: emit_kv_load(0, 0), 1: emit_kv_load(1, ts_p2_done)}
    scores_ts = {}
    exp_ss = {}
    pv_ts = {-2: 0, -1: 0}
    dadd_vs = {-2: 0, -1: 0}
    o_evac_v = 0
    dred_vs = {-1: p0_v_done}
    ts_transp = {-1: 0}
    pot1_free_v = [0]
    otm_free_t = [0]
    pending_dance = []

    def emit_dance(h):
        W("tensor", "vs", o_evac_v)
        W("tensor", "vs", pot1_free_v[0])
        for tt in range(8):
            t_tr(pot1[:, tt * 128:(tt + 1) * 128],
                 o_sb[:, h * TOK + tt * 128: h * TOK + tt * 128 + 128],
                 ident[:, :], inc=(tt == 7))
        ts_o1 = cnt["ts"]
        W("vector", "ts", ts_o1)
        W("vector", "vs", dred_vs[h])
        W("vector", "ts", otm_free_t[0])
        for tt in range(8):
            v_op("tensor_scalar", otm[:, tt * 128:(tt + 1) * 128],
                 pot1[:, tt * 128:(tt + 1) * 128],
                 dtm2_sb[:, h * 8 + tt: h * 8 + tt + 1], None, ALU.mult,
                 inc=(tt == 7))
        vs_otm = cnt["vs"]
        W("tensor", "vs", vs_otm)
        for tt in range(8):
            t_tr(pot1[:, tt * 128:(tt + 1) * 128],
                 otm[:, tt * 128:(tt + 1) * 128], ident[:, :], inc=(tt == 7))
        otm_free_t[0] = cnt["ts"]
        W("vector", "ts", otm_free_t[0])
        pot1_free_v[0] = v_op("tensor_copy", o_sb[:, h * TOK:(h + 1) * TOK],
                              pot1[:, 0:1024], inc=True)

    def emit_scores(g):
        h, kt = divmod(g, 64)
        idx = 2 * h + (kt // 32)
        sl = idx % 2
        kt32 = kt % 32
        stb = g % 2
        W("tensor", "dkv%d" % sl, kv_dma[idx])
        if g == 0:
            W("tensor", "vs", vs_q_done)
            W("tensor", "vs", vs_bh)
            W("tensor", "vs", vs_hw)
            W("tensor", "ss", g_done_s)
        W("tensor", "ss", exp_ss.get(g - 2, 0))
        if kt == 0:
            W("tensor", "vs", o_evac_v)
        for st_ in range(2):
            t_mm(pST[stb][:, st_ * 512:(st_ + 1) * 512],
                 kbuf[sl][:, kt32 * 128:(kt32 + 1) * 128],
                 q_sb[:, h * TOK + st_ * 512: h * TOK + st_ * 512 + 512],
                 start=True, stop=True, inc=(st_ == 1))
        scores_ts[g] = cnt["ts"]

    def emit_exp(g):
        h = g // 64
        stb = g % 2
        W("scalar", "ts", scores_ts[g])
        W("scalar", "ts", pv_ts.get(g - 2, 0))
        W("scalar", "vs", dadd_vs.get(g - 2, 0))
        s_act(pt_sb[stb][:, :], pST[stb][:, 0:1024], AF.Exp,
              bias=bh_sb[:, h:h + 1], scale=ISQ, inc=True)
        exp_ss[g] = cnt["ss"]

    def emit_dadd(g):
        kt = g % 64
        stb = g % 2
        W("vector", "ss", exp_ss[g])
        if kt == 0:
            W("vector", "ts", ts_transp.get(g // 64 - 1, 0))
            v_op("tensor_copy", dacc[:, :], pt_sb[stb][:, :], inc=True)
        else:
            W("vector", "vs", cnt["vs"])
            v_op("tensor_add", dacc[:, :], dacc[:, :], pt_sb[stb][:, :], inc=True)
        dadd_vs[g] = cnt["vs"]

    def emit_pv(g):
        h, kt = divmod(g, 64)
        idx = 2 * h + (kt // 32)
        sl = idx % 2
        kt32 = kt % 32
        stb = g % 2
        W("tensor", "ss", exp_ss[g])
        for st_ in range(2):
            t_mm(pO[st_][:, :], vbuf[sl][:, kt32 * 128:(kt32 + 1) * 128],
                 pt_sb[stb][:, st_ * 512:(st_ + 1) * 512],
                 start=(kt == 0), stop=(kt == 63), inc=(st_ == 1))
        pv_ts[g] = cnt["ts"]

    emit_scores(0)
    for g in range(NH * 64):
        h, kt = divmod(g, 64)
        if g + 1 < NH * 64 and (g + 1) % 64 != 0:
            emit_scores(g + 1)
        emit_exp(g)
        emit_dadd(g)
        emit_pv(g)
        if kt == 4 and pending_dance:
            emit_dance(pending_dance.pop(0))
        if kt == 31:
            idx = 2 * h + 0
            if idx + 2 < 16:
                kv_dma[idx + 2] = emit_kv_load(idx + 2, cnt["ts"])
        if kt == 63:
            idx = 2 * h + 1
            if idx + 2 < 16:
                kv_dma[idx + 2] = emit_kv_load(idx + 2, cnt["ts"])
            W("vector", "ts", pv_ts[g])
            for st_ in range(2):
                v_op("tensor_copy",
                     o_sb[:, h * TOK + st_ * 512: h * TOK + st_ * 512 + 512],
                     pO[st_][:, :], inc=(st_ == 1))
            o_evac_v = cnt["vs"]
            for half4 in range(2):
                W("tensor", "vs", dadd_vs[g] if half4 == 0 else cnt["vs"])
                W("tensor", "vs", dred_vs.get(h - 1, 0) if half4 == 0 else cnt["vs"])
                for tt4 in range(4):
                    tt = half4 * 4 + tt4
                    t_tr(pdr[:, tt4 * 128:(tt4 + 1) * 128],
                         dacc[:, tt * 128:(tt + 1) * 128], identf[:, :],
                         inc=(tt4 == 3))
                W("vector", "ts", cnt["ts"])
                for tt4 in range(4):
                    tt = half4 * 4 + tt4
                    v_op("reduce_sum", dtm_sb[:, h * 8 + tt: h * 8 + tt + 1],
                         pdr[:, tt4 * 128:(tt4 + 1) * 128], axis=AX.X,
                         inc=(tt4 == 3))
            ts_transp[h] = cnt["ts"]
            vq("reciprocal", dtm2_sb[:, h * 8:(h + 1) * 8],
               dtm_sb[:, h * 8:(h + 1) * 8])
            dred_vs[h] = cnt["vs"]
            if g + 1 < NH * 64:
                emit_scores(g + 1)
            pending_dance.append(h)

    while pending_dance:
        emit_dance(pending_dance.pop(0))
    p3ps.close()
    p3tps = ExitStack()
    o_norm_v = cnt["vs"]

    p3tps.close()
    p3sc.close()
    p4ps = ExitStack()
    ps4 = [PS(p4ps, "ps4_%d" % i, [128, 512], F32) for i in range(4)]
    psg = [PS(p4ps, "psg0", [128, 1024], BF16), PS(p4ps, "psg1", [128, 1024], BF16)]
    latsc.close()
    qsc.close()
    p4sc = ExitStack()
    xcol = [SB(p4sc, "xcol%d" % i, [128, 1024], F32) for i in range(4)]
    outp = [SB(p4sc, "outp%d" % i, [128, 1024], F32) for i in range(4)]


    s4_bank_free_v = [0, 0, 0, 0]
    psg_free_v = [0, 0]
    sgv_free_t = [0, 0]
    vs_sg = {}
    out_store_done = {-4: 0, -3: 0, -2: 0, -1: 0}
    xcol_d = {}
    mm_ts = {}
    for par0 in range(4):
        W("scalar", "vs", o_norm_v)
        xcol_d[par0] = dma("scalar", "dx%d" % par0,
                           xcol[par0][:, :].rearrange("p (a b) -> p a b", b=128),
                           bass.AP(x_sl, par0 * 128,
                                   [[4096, 128], [524288, 8], [1, 128]]))

    ts_p4_wgt_free = max(wblk_free_t[0], wblk_free_t[1])
    dwo_done = {}

    def wo_slot(q4):
        return wgt[q4 % 2][:, (q4 // 2) * 1024:(q4 // 2) * 1024 + 1024]

    def wo_load(dmt, gate_ts):
        q4 = dmt % 4
        W("gpsimd", "ts", gate_ts)
        dwo_done[dmt] = dma("gpsimd", "dwo%d" % q4,
                            wo_slot(q4),
                            bass.AP(agw_out,
                                    (dmt // 4) * AGW_BLK + AGW_WO + (dmt % 4) * 131072,
                                    [[1024, 128], [1, 1024]]))

    for dmt0 in range(4):
        wo_load(dmt0, ts_p4_wgt_free)

    def p4_mms(dmt):
        q4 = dmt % 4
        W("tensor", "dwo%d" % q4, dwo_done[dmt])
        if dmt == 0:
            W("tensor", "vs", o_norm_v)
        bks4 = [2 * (dmt % 2), 2 * (dmt % 2) + 1]
        W("tensor", "vs", max(s4_bank_free_v[bks4[0]], s4_bank_free_v[bks4[1]]))
        wsl = wo_slot(q4)
        for dkc in range(8):
            for st_ in range(2):
                t_mm(ps4[bks4[st_]][:, :], wsl[:, dkc * 128:(dkc + 1) * 128],
                     o_sb[:, dkc * TOK + st_ * 512: dkc * TOK + st_ * 512 + 512],
                     start=(dkc == 0), stop=(dkc == 7),
                     inc=(dkc == 7 and st_ == 1))
        mm_ts[dmt] = cnt["ts"]
        if dmt + 4 < 32:
            wo_load(dmt + 4, mm_ts[dmt])

    def p4_evac(dmt):
        par = dmt % 2
        if dmt == 0:
            W("vector", "ss", g_done_s)
        W("vector", "ts", sgv_free_t[par])
        W("vector", "ts", mm_ts[dmt])
        for st_ in range(2):
            bk = 2 * (dmt % 2) + st_
            s4_bank_free_v[bk] = v_op(
                "tensor_tensor",
                wgt[par][:, 2048 + st_ * 512: 2048 + st_ * 512 + 512],
                ps4[bk][:, :],
                g_sb[:, dmt * TOK + st_ * 512: dmt * TOK + st_ * 512 + 512],
                ALU.mult, inc=True)
        vs_sg[dmt] = cnt["vs"]

    def p4_fin(dmt):
        par = dmt % 2
        q4 = dmt % 4
        W("tensor", "vs", vs_sg[dmt])
        W("tensor", "vs", psg_free_v[par])
        for tt in range(8):
            t_tr(psg[par][:, tt * 128:(tt + 1) * 128],
                 wgt[par][:, 2048 + tt * 128: 2048 + tt * 128 + 128], ident[:, :],
                 inc=(tt == 7))
        sgv_free_t[par] = cnt["ts"]
        W("vector", "ts", sgv_free_t[par])
        W("vector", "dx%d" % q4, xcol_d[dmt])
        W("vector", "st%d" % q4, out_store_done[dmt - 4])
        psg_free_v[par] = v_op("scalar_tensor_tensor", outp[q4][:, :],
                               psg[par][:, 0:1024], 1.0, xcol[q4][:, :],
                               ALU.mult, ALU.add, inc=True)
        W("sync", "vs", psg_free_v[par])
        out_store_done[dmt] = dma("sync", "st%d" % q4,
                                  bass.AP(out_d, dmt * 128,
                                          [[4096, 128], [524288, 8], [1, 128]]),
                                  outp[q4][:, :].rearrange("p (a b) -> p a b", b=128))
        if dmt + 4 < 32:
            W("scalar", "vs", psg_free_v[par])
            xcol_d[dmt + 4] = dma("scalar", "dx%d" % q4,
                                  xcol[q4][:, :].rearrange("p (a b) -> p a b", b=128),
                                  bass.AP(x_sl, (dmt + 4) * 128,
                                          [[4096, 128], [524288, 8], [1, 128]]))

    p4_mms(0)
    p4_mms(1)
    for dmt in range(32):
        p4_evac(dmt)
        if dmt % 2 == 1:
            p4_fin(dmt - 1)
            p4_fin(dmt)
        if dmt + 2 < 32:
            p4_mms(dmt + 2)
    for stn in ("st0", "st1", "st2", "st3"):
        W("sync", stn, cnt[stn])

    p4ps.close()
    p4sc.close()
    osc.close()
    gssc.close()
    kvearly.close()
    wqesc_dummy = None
    gsc.close()

    with nc.Block() as block:
        @block.sync
        def _(e):
            for f in prog["sync"]:
                f(e)

        @block.gpsimd
        def _(e):
            for f in prog["gpsimd"]:
                f(e)

        @block.tensor
        def _(e):
            for f in prog["tensor"]:
                f(e)

        @block.vector
        def _(e):
            for f in prog["vector"]:
                f(e)

        @block.scalar
        def _(e):
            for f in prog["scalar"]:
                f(e)

    return nc


_NC_CACHE = []


def kernel(L0_attn_x, L0_attn_wq, L0_attn_wk, L0_attn_wv, L0_attn_wo,
           L0_ffn_W_base, L0_ffn_b_base, L0_ffn_W_router, L0_ffn_lora_A,
           L0_ffn_lora_B):
    x = np.ascontiguousarray(np.asarray(L0_attn_x, dtype=np.float32)[0])
    wq = np.asarray(L0_attn_wq, dtype=np.float32)
    wk = np.asarray(L0_attn_wk, dtype=np.float32)
    wv = np.asarray(L0_attn_wv, dtype=np.float32)
    wo = np.asarray(L0_attn_wo, dtype=np.float32)
    wb = np.asarray(L0_ffn_W_base, dtype=np.float32)
    bb = np.asarray(L0_ffn_b_base, dtype=np.float32)
    wr = np.asarray(L0_ffn_W_router, dtype=np.float32)
    la = np.asarray(L0_ffn_lora_A, dtype=np.float32)
    lb = np.asarray(L0_ffn_lora_B, dtype=np.float32)

    ball = np.ascontiguousarray(lb.transpose(0, 2, 1).reshape(ER, D))
    lat = np.ascontiguousarray(la.reshape(ER, D).T)
    b2 = np.ascontiguousarray(bb.reshape(32, 128).T)
    gmat = np.ascontiguousarray(
        (np.kron(np.eye(8), np.ones((1, 16))) * 2.0).astype(np.float32))
    id_bf = np.eye(128, dtype=ml_dtypes.bfloat16)
    id_f32 = np.eye(128, dtype=np.float32)

    if not _NC_CACHE:
        _NC_CACHE.append(build_program())
    nc = _NC_CACHE[0]

    in_maps = []
    for c in range(NC):
        in_maps.append({
            "x_sl": np.ascontiguousarray(x[c * TOK:(c + 1) * TOK]),
            "wq_sl": np.ascontiguousarray(wq[:, c * 512:(c + 1) * 512]),
            "wk_sl": np.ascontiguousarray(
                wk[:, c * 128:(c + 1) * 128].reshape(32, 128, 128)
                .transpose(1, 0, 2).reshape(4096, 128)),
            "wv_sl": np.ascontiguousarray(
                wv[:, c * 128:(c + 1) * 128].reshape(32, 128, 128)
                .transpose(1, 0, 2).reshape(4096, 128)),
            "wo_sl": np.ascontiguousarray(
                wo[:, c * 512:(c + 1) * 512].reshape(8, 128, 4, 128)
                .transpose(2, 1, 0, 3).reshape(1024, 512)),
            "wbt_sl": np.ascontiguousarray(
                wb[c * 512:(c + 1) * 512, :].T.reshape(4, 8, 128, 4, 128)
                .transpose(0, 3, 2, 1, 4).reshape(4096, 512)),
            "ball_sl": np.ascontiguousarray(ball[:, c * 512:(c + 1) * 512]),
            "lat_sl": np.ascontiguousarray(lat[c * 512:(c + 1) * 512, :]),
            "wr_in": wr,
            "b2_in": b2,
            "gmat_in": gmat,
            "id_bf": id_bf,
            "id_f32": id_f32,
        })
    res = run_bass_kernel_spmd(nc, in_maps, core_ids=list(range(NC)))
    out = np.concatenate([np.asarray(res.results[c]["out"]) for c in range(NC)],
                         axis=0)
    return out.reshape(1, S, D).astype(np.float32)
